# version 23
# baseline (speedup 1.0000x reference)
import math
from contextlib import ExitStack

import numpy as np
import ml_dtypes
import concourse.bass as bass
import concourse.mybir as mybir
from concourse.bass_utils import run_bass_kernel_spmd

F32 = mybir.dt.float32
BF16 = mybir.dt.bfloat16
I32 = mybir.dt.int32
AX = mybir.AxisListType.X
ALU = mybir.AluOpType
ACT = mybir.ActivationFunctionType

NCORES = 8


def _hilo(x):
    hi = np.float32(x)
    return float(hi), float(x - float(hi))


class Cfg:
    def __init__(self, D=2048, S=2048, DFF=8192):
        self.D, self.S, self.DFF = D, S, DFF
        self.DC = D // 128
        self.H = self.DC
        self.SB = S // 512
        self.ST = S // 128
        self.FC = DFF // 128
        self.ECW = max(1, min(4, self.DC // 2))
        assert D % 256 == 0 and S % 512 == 0 and DFF % 128 == 0


class Sem:
    __slots__ = ("h", "v")
    def __init__(self, h):
        self.h = h
        self.v = 0


class Buf:
    __slots__ = ("cons",)
    def __init__(self):
        self.cons = []
    def used(self, sem, val):
        self.cons.append((sem, val))
    def take(self):
        c = self.cons
        self.cons = []
        return c


class Prog:
    ENGINES = ("sync", "gpsimd", "vector", "scalar", "tensor")

    def __init__(self, nc, es):
        self.nc = nc
        self.es = es
        self.q = {e: [] for e in self.ENGINES}
        self.waited = {e: {} for e in self.ENGINES}
        self.st = self.sem("c_tensor")
        self.sv = self.sem("c_vector")
        self.sc = self.sem("c_scalar")
        self.sg = self.sem("c_gpsimd")

    def sem(self, name):
        return Sem(self.es.enter_context(self.nc.semaphore(name)))

    def op(self, eng, fn, deps=(), inc=None, n=1):
        w = self.waited[eng]
        own = {"tensor": self.st, "vector": self.sv,
               "scalar": self.sc, "gpsimd": self.sg}.get(eng)
        dl = []
        for s, v in deps:
            if own is not None and s is own and v > own.v:
                raise AssertionError(
                    "emission inversion on %s: dep %d > current %d" % (eng, v, own.v))
            if v and v > w.get(s, 0):
                dl.append((s.h, v))
                w[s] = v
        if inc is not None:
            inc.v += n
        def run(e, fn=fn, dl=tuple(dl), ih=(inc.h if inc is not None else None), n=n):
            for h, v in dl:
                e.wait_ge(h, v)
            ins = fn(e)
            if ih is not None:
                ins.then_inc(ih, n)
        self.q[eng].append(run)
        return inc.v if inc is not None else None

    def wait(self, eng, deps):
        w = self.waited[eng]
        dl = []
        for s, v in deps:
            if v and v > w.get(s, 0):
                dl.append((s.h, v))
                w[s] = v
        if dl:
            def run(e, dl=tuple(dl)):
                for h, v in dl:
                    e.wait_ge(h, v)
            self.q[eng].append(run)

    def barrier(self, extra=()):
        deps = [(c, c.v) for c in (self.st, self.sv, self.sc, self.sg)] + list(extra)
        for e in self.ENGINES:
            self.wait(e, deps)


def build(cfg, debug=False):
    D, S, DFF = cfg.D, cfg.S, cfg.DFF
    DC, H, SB, ST, FC, ECW = cfg.DC, cfg.H, cfg.SB, cfg.ST, cfg.FC, cfg.ECW

    nc = bass.Bass("TRN2", target_bir_lowering=False)
    es = ExitStack()
    P = Prog(nc, es)
    st, sv, sc, sg = P.st, P.sv, P.sc, P.sg

    def din(name, shape, dt=F32):
        return nc.dram_tensor(name, shape, dt, kind="ExternalInput").ap()

    xT = din("xT", [D, S])
    wqT = din("wqT", [D, D]); wkT = din("wkT", [D, D])
    wvT = din("wvT", [D, D]); woT = din("woT", [D, D])
    bq_i = din("bq", [128, DC]); bk_i = din("bk", [128, DC])
    bvr_i = din("bvr", [1, D]); bo_i = din("bo", [128, DC])
    w1T = din("w1T", [D, DFF]); b1_i = din("b1", [128, FC])
    w2T = din("w2T", [DFF, D]); b2_i = din("b2", [128, DC])
    srow_i = din("srow", [1, S])
    icol_i = din("icol", [64, 1])
    ones1_i = din("ones1", [1, 128])
    onesk_i = din("onesk", [128, 1])
    oneskb_i = nc.dram_tensor("oneskb", [128, 1], BF16, kind="ExternalInput").ap()
    ones1b_i = nc.dram_tensor("ones1b", [1, 128], BF16, kind="ExternalInput").ap()
    ones1sc_i = nc.dram_tensor("ones1sc", [1, 128], BF16, kind="ExternalInput").ap()
    masks_i = din("masks", [128, 4, 512], BF16)

    outT = nc.dram_tensor("outT", [D, S], F32, kind="ExternalOutput").ap()
    dbg = {}
    if debug:
        def dout(name, shape, dt=F32):
            dbg[name] = nc.dram_tensor(name, shape, dt, kind="ExternalOutput").ap()
        dout("dbg_vtm", [ST, 128, D], BF16)
        dout("dbg_q0", [128, S], BF16); dout("dbg_k0", [128, S], BF16)
        dout("dbg_aa1", [128, 1]); dout("dbg_aa2", [128, 1])
        dout("dbg_att0", [128, S], BF16)
        dout("dbg_x1", [DC, 128, S])
        dout("dbg_cc", [128, S]); dout("dbg_ss", [128, S])
        dout("dbg_t1", [1, 1]); dout("dbg_t2", [1, 1])

    v_d = nc.dram_tensor("v_d", [ST, 128, D], BF16).ap()
    attb_d = nc.dram_tensor("attb_d", [H, 128, S], BF16).ap()
    x1_d = nc.dram_tensor("x1_d", [DC, 128, S], F32).ap()
    h_d = nc.dram_tensor("h_d", [FC, 128, S], BF16).ap()
    w2q_d = nc.dram_tensor("w2q_d", [FC, 128, D], BF16).ap()
    z2_d = nc.dram_tensor("z2_d", [DC, 128, S], F32).ap()

    d_const = P.sem("d_const")
    d_x = P.sem("d_x")
    d_wv = P.sem("d_wv")
    d_vw = [P.sem("d_vw0"), P.sem("d_vw1")]
    d_vh = [P.sem("d_vh0"), P.sem("d_vh1")]
    d_wq = P.sem("d_wq"); d_wk = P.sem("d_wk")
    d_wo = [P.sem("d_wo0"), P.sem("d_wo1")]
    d_swq = P.sem("d_swq"); d_swk = P.sem("d_swk")
    d_trig = P.sem("d_trig")
    d_trigc = P.sem("d_trigc")
    d_cbf = P.sem("d_cbf")
    d_ab = [P.sem("d_ab%d" % i) for i in range(4)]
    d_atw = [P.sem("d_atw0"), P.sem("d_atw1")]
    d_ats = [P.sem("d_ats0"), P.sem("d_ats1")]
    d_xr = [P.sem("d_xr0"), P.sem("d_xr1")]
    d_x1w = [P.sem("d_x1w0"), P.sem("d_x1w1")]
    d_w1 = [P.sem("d_w10"), P.sem("d_w11")]
    d_w2 = [P.sem("d_w20"), P.sem("d_w21")]
    d_hw = [P.sem("d_hw0"), P.sem("d_hw1")]
    d_w2qw = [P.sem("d_w2qw0"), P.sem("d_w2qw1")]
    d_w2r = P.sem("d_w2r")
    d_hs = [P.sem("d_hs%d" % i) for i in range(4)]
    d_x1r = [P.sem("d_x1r0"), P.sem("d_x1r1")]
    d_z2w = [P.sem("d_z2w0"), P.sem("d_z2w1")]
    d_z2r = [P.sem("d_z2r0"), P.sem("d_z2r1")]
    d_out = [P.sem("d_out0"), P.sem("d_out1")]
    d_dbg = P.sem("d_dbg")

    C2PI_HI, C2PI_LO = _hilo(2.0 * math.pi)
    CPI2_HI, CPI2_LO = _hilo(math.pi / 2.0)
    INV2PI = float(np.float32(1.0 / (2.0 * math.pi)))
    SC_HI, SC_LO = _hilo(1.0 / math.sqrt(128.0))
    LN4_HI, LN4_LO = _hilo(math.log(10000.0) / 64.0)
    PI_F = float(np.float32(math.pi))
    TQ_HI, TQ_LO = _hilo(0.7 / (D * DFF))
    EPS = 1e-5

    def DMA(eng, out_ap, in_ap, deps, inc):
        return P.op(eng, (lambda e, o=out_ap, i=in_ap: e.dma_start(out=o, in_=i)),
                    deps=deps, inc=inc, n=16)

    def MM(out_ap, lhsT_ap, rhs_ap, deps, start, stop):
        return P.op("tensor", (lambda e, o=out_ap, l=lhsT_ap, r=rhs_ap, a=start, b=stop:
                               e.matmul(o, l, r, start=a, stop=b, skip_group_check=True)),
                    deps=deps, inc=st)

    sb = nc.alloc_sbuf_tensor
    c_ones1 = sb("c_ones1", [1, 128], F32)
    c_onesk = sb("c_onesk", [128, 1], F32)
    c_oneskb = sb("c_oneskb", [128, 1], BF16)
    c_ones1b = sb("c_ones1b", [1, 128], BF16)
    c_ones1sc = sb("c_ones1sc", [1, 128], BF16)
    c_bq = sb("c_bq", [128, DC], F32); c_bk = sb("c_bk", [128, DC], F32)
    c_bo = sb("c_bo", [128, DC], F32); c_b2 = sb("c_b2", [128, DC], F32)
    c_b1 = sb("c_b1", [128, FC], F32)
    t_t1 = sb("t_t1", [128, 1], F32); t_t1n = sb("t_t1n", [128, 1], F32)
    t_t2 = sb("t_t2", [128, 1], F32); t_t2n = sb("t_t2n", [128, 1], F32)
    absac = [sb("absac1", [128, 1], F32), sb("absac2", [128, 1], F32)]
    absred = sb("absred", [128, 1], F32)
    tm11 = [sb("tm11a", [1, 1], F32), sb("tm11b", [1, 1], F32)]

    B = {}
    def buf(name):
        if name not in B:
            B[name] = Buf()
        return B[name]

    nconst = 0

    esA = ExitStack()
    sbA = lambda n, s_, d: esA.enter_context(nc.sbuf_tensor(n, s_, d))
    psA = lambda n, s_: esA.enter_context(nc.psum_tensor(n, s_, F32))
    x_bf = sbA("x_bf", [128, DC, S], BF16)
    c_masks = sbA("c_masks", [128, 4, 512], BF16)
    c_cc2 = sbA("c_cc2", [128, S], F32)
    c_ss2 = sbA("c_ss2", [128, S], F32)
    c_cc2k = sbA("c_cc2k", [128, S], F32)
    c_ss2k = sbA("c_ss2k", [128, S], F32)
    c_bvr = sbA("c_bvr", [1, D], BF16)
    vh_s = [sbA("vh_s%d" % i, [128, ST, 128], BF16) for i in range(2)]
    abuf = [sbA("abuf%d" % i, [128, 512], F32) for i in range(4)]
    absscr = sbA("absscr", [128, 512], F32)
    absred2 = sbA("absred2", [128, 1], F32)
    wq_s = sbA("wq_s", [128, DC, 128], BF16)
    wk_s = sbA("wk_s", [128, DC, 128], BF16)
    esT = ExitStack()
    sbT = lambda n, s_, d: esT.enter_context(nc.sbuf_tensor(n, s_, d))
    c_srow = sbT("c_srow", [1, 512], F32)
    c_icol = sbT("c_icol", [64, 1], F32)
    g_th = sbT("g_th", [64, 512], F32)
    g_a = sbT("g_a", [64, 512], F32)
    g_b = sbT("g_b", [64, 512], F32)
    g_ki = sbT("g_ki", [64, 512], I32)
    g_f = sbT("g_f", [64, 1], F32)
    g_sin = sbT("g_sin", [64, 512], F32)
    esV = ExitStack()
    sbV = lambda n, s_, d: esV.enter_context(nc.sbuf_tensor(n, s_, d))
    vstage = [sbV("vstage%d" % i, [128, 512], BF16) for i in range(2)]
    EW = min(512, D)
    EH = D // EW
    wv_s = sbV("wv_s", [128, DC, EW], BF16)

    pA = [psA("pA%d" % i, [128, 512]) for i in range(2)]
    pS = [psA("pS%d" % i, [128, 512]) for i in range(2)]
    pV = [psA("pV%d" % i, [128, 512]) for i in range(2)]
    pB = psA("pB", [128, 512])
    pT = psA("pT", [1, 512])
    pG = pV[1]

    for o, i in ((c_ones1[:], ones1_i), (c_onesk[:], onesk_i), (c_oneskb[:], oneskb_i),
                 (c_ones1b[:], ones1b_i), (c_ones1sc[:], ones1sc_i),
                 (c_masks[:], masks_i),
                 (c_bq[:], bq_i), (c_bk[:], bk_i), (c_bo[:], bo_i), (c_b2[:], b2_i),
                 (c_b1[:], b1_i), (c_icol[:], icol_i)):
        DMA("sync", o, i, (), d_const)
        nconst += 16
    CONST = (d_const, nconst)
    DMA("gpsimd", c_bvr[:], bvr_i, (), d_cbf)
    CBVR = (d_cbf, 16)
    d_srow = P.sem("d_srow")

    P.op("vector", lambda e: e.tensor_scalar(g_f[:], c_icol[:], -LN4_HI, None, ALU.mult),
         deps=[CONST] + buf("g_f").take(), inc=sv)
    P.op("vector", lambda e: e.scalar_tensor_tensor(g_f[:], c_icol[:], -LN4_LO, g_f[:],
                                                    ALU.mult, ALU.add),
         deps=[(sv, sv.v), CONST], inc=sv)
    P.op("scalar", lambda e: e.activation(g_f[:], g_f[:], ACT.Exp),
         deps=[(sv, sv.v)], inc=sc)
    FREQ = sc.v

    n_trig = 0
    for cki in range(S // 512):
        sl = slice(cki * 512, cki * 512 + 512)
        DMA("sync", c_srow[:], srow_i[:, sl], buf("c_srow").take(), d_srow)
        MM(pG[0:64, :], c_ones1[:, 0:64], c_srow[:],
           [CONST, (d_srow, d_srow.v)] + buf("pV1").take(), True, True)
        buf("c_srow").used(st, st.v)
        P.op("vector", lambda e: e.tensor_scalar(g_th[:], pG[0:64, :], g_f[:], None, ALU.mult),
             deps=[(st, st.v), (sc, FREQ)] + buf("g_th").take(), inc=sv)
        buf("pV1").used(sv, sv.v)
        TH = sv.v
        for half, target in ((0, "sin"), (1, "cos")):
            if half == 0:
                P.op("vector", lambda e: e.tensor_scalar(g_a[:], g_th[:], INV2PI,
                                                         None, ALU.mult),
                     deps=[(sv, TH)] + buf("g_a").take(), inc=sv)
            else:
                P.op("vector", lambda e: e.tensor_scalar(g_b[:], g_th[:], CPI2_HI, CPI2_LO,
                                                         ALU.add, ALU.add),
                     deps=[(sv, TH)] + buf("g_b").take(), inc=sv)
                P.op("vector", lambda e: e.tensor_scalar(g_a[:], g_b[:], INV2PI,
                                                         None, ALU.mult),
                     deps=[(sv, sv.v)] + buf("g_a").take(), inc=sv)
            P.op("vector", lambda e: e.tensor_copy(g_ki[:], g_a[:]), deps=[(sv, sv.v)], inc=sv)
            P.op("vector", lambda e: e.tensor_copy(g_a[:], g_ki[:]), deps=[(sv, sv.v)], inc=sv)
            src = g_th if half == 0 else g_b
            P.op("vector", (lambda e, src=src: e.scalar_tensor_tensor(
                    g_b[:], g_a[:], -C2PI_HI, src[:], ALU.mult, ALU.add)),
                 deps=[(sv, sv.v)] + (buf("g_b").take() if half == 0 else []), inc=sv)
            P.op("vector", lambda e: e.scalar_tensor_tensor(g_b[:], g_a[:], -C2PI_LO, g_b[:],
                                                            ALU.mult, ALU.add),
                 deps=[(sv, sv.v)], inc=sv)
            P.op("vector", lambda e: e.tensor_scalar(g_a[:], g_b[:], PI_F, None, ALU.is_gt),
                 deps=[(sv, sv.v)], inc=sv)
            P.op("vector", lambda e: e.scalar_tensor_tensor(g_b[:], g_a[:], -C2PI_HI, g_b[:],
                                                            ALU.mult, ALU.add),
                 deps=[(sv, sv.v)], inc=sv)
            P.op("vector", lambda e: e.scalar_tensor_tensor(g_b[:], g_a[:], -C2PI_LO, g_b[:],
                                                            ALU.mult, ALU.add),
                 deps=[(sv, sv.v)], inc=sv)
            P.op("vector", lambda e: e.tensor_scalar(g_b[:], g_b[:], PI_F, -PI_F,
                                                     ALU.min, ALU.max),
                 deps=[(sv, sv.v)], inc=sv)
            buf("g_a").used(sv, sv.v)
            RED = sv.v
            if half == 0:
                P.op("scalar", lambda e: e.activation(g_sin[:], g_b[:], ACT.Sin),
                     deps=[(sv, RED)] + buf("g_sin").take(), inc=sc)
                buf("g_b").used(sc, sc.v)
                P.op("vector", (lambda e, sl=sl: e.tensor_scalar(
                        c_ss2[0:64, sl], g_sin[:], -1.0, None, ALU.mult)),
                     deps=[(sc, sc.v)], inc=sv)
                DMA("gpsimd", c_ss2[64:128, sl], g_sin[:], [(sc, sc.v)], d_trig)
                n_trig += 16
                buf("g_sin").used(d_trig, n_trig)
                buf("g_sin").used(sv, sv.v)
            else:
                P.op("scalar", (lambda e, sl=sl: e.activation(
                        c_cc2[0:64, sl], g_b[:], ACT.Sin)),
                     deps=[(sv, RED)], inc=sc)
                buf("g_b").used(sc, sc.v)
                DMA("gpsimd", c_cc2[64:128, sl], c_cc2[0:64, sl], [(sc, sc.v)], d_trigc)
        buf("g_th").used(sv, sv.v)
        buf("g_th").used(sc, sc.v)
    TRIG0 = (d_trig, n_trig)
    TRIG0C = (d_trigc, d_trigc.v)
    for (srct, dstt, dstn) in ((c_cc2, c_cc2k, "c_cc2k"), (c_ss2, c_ss2k, "c_ss2k")):
        P.op("vector", (lambda e, srct=srct, dstt=dstt: e.tensor_scalar(
                dstt[:], srct[:], SC_HI, None, ALU.mult)),
             deps=[TRIG0, TRIG0C, (sc, sc.v), (sv, sv.v)] + buf(dstn).take(), inc=sv)
        P.op("vector", (lambda e, srct=srct, dstt=dstt: e.scalar_tensor_tensor(
                dstt[:], srct[:], SC_LO, dstt[:], ALU.mult, ALU.add)),
             deps=[(sv, sv.v)], inc=sv)
    TRIG = (sv, sv.v)

    xTr = xT.rearrange("(c p) s -> p c s", p=128)
    for dc in range(DC):
        DMA("gpsimd", x_bf[:, dc, :], xTr[:, dc, :], (), d_x)
    XL = (d_x, 16 * DC)

    ab_tiles = []
    t1c = min(512, DFF)
    w1Tr = w1T.rearrange("(c p) f -> p c f", p=128)
    for dc in range(DC):
        for j in range(DFF // t1c):
            ab_tiles.append((0, w1Tr[:, dc, j * t1c:(j + 1) * t1c], t1c))
    t2c = min(512, D)
    w2Tr = w2T.rearrange("(c p) e -> p c e", p=128)
    for fc in range(FC):
        for j in range(D // t2c):
            ab_tiles.append((1, w2Tr[:, fc, j * t2c:(j + 1) * t2c], t2c))
    n_ab = len(ab_tiles)
    ab_i = [0]

    P.op("vector", lambda e: e.memset(absac[0][:], 0.0), deps=(), inc=sv)
    P.op("vector", lambda e: e.memset(absac[1][:], 0.0), deps=(), inc=sv)
    ABZ = sv.v
    absac_last = [ABZ, ABZ]

    def emit_absmean(k):
        for _ in range(k):
            i = ab_i[0]
            if i >= n_ab:
                return
            ab_i[0] = i + 1
            which, iap, tc = ab_tiles[i]
            par = i % 4
            DMA("sync", abuf[par][:, 0:tc], iap, buf("abuf%d" % par).take(), d_ab[par])
            nl = d_ab[par].v
            if par % 2 == 0:
                P.op("vector", (lambda e, par=par, tc=tc: e.reduce_sum(
                        absred[:], abuf[par][:, 0:tc], AX, apply_absolute_value=True)),
                     deps=[(d_ab[par], nl)] + buf("absred").take(), inc=sv)
                buf("abuf%d" % par).used(sv, sv.v)
                P.op("vector", (lambda e, which=which: e.tensor_add(
                        absac[which][:], absac[which][:], absred[:])),
                     deps=[(sv, sv.v), (sv, absac_last[which])], inc=sv)
                absac_last[which] = sv.v
                buf("absred").used(sv, sv.v)
            else:
                P.op("scalar", (lambda e, par=par, tc=tc: e.activation(
                        absscr[:, 0:tc], abuf[par][:, 0:tc], ACT.Abs,
                        accum_out=absred2[:])),
                     deps=[(d_ab[par], nl)] + buf("absred2").take() + buf("absscr").take(),
                     inc=sc)
                buf("abuf%d" % par).used(sc, sc.v)
                buf("absscr").used(sc, sc.v)
                P.op("vector", (lambda e, which=which: e.tensor_add(
                        absac[which][:], absac[which][:], absred2[:])),
                     deps=[(sc, sc.v), (sv, absac_last[which])], inc=sv)
                absac_last[which] = sv.v
                buf("absred2").used(sv, sv.v)

    wvTr = wvT.rearrange("(c p) e -> p c e", p=128)
    EBW = min(512, EW)
    for eh in range(EH):
        DMA("gpsimd", wv_s[:], wvTr[:, :, eh * EW:(eh + 1) * EW],
            buf("wv_s").take(), d_wv)
        WVL = d_wv.v
        for sti in range(ST):
            for ebi in range(EW // EBW):
                pvp = pS[ebi % 2]
                pb_ = buf("pS%d" % (ebi % 2))
                e0 = eh * EW + ebi * EBW
                MM(pvp[:, 0:EBW], c_ones1b[:, 0:128], c_bvr[:, e0:e0 + EBW],
                   [CONST, CBVR] + pb_.take(), True, False)
                for dc in range(DC):
                    MM(pvp[:, 0:EBW], x_bf[:, dc, sti * 128:(sti + 1) * 128],
                       wv_s[:, dc, ebi * EBW:(ebi + 1) * EBW],
                       [XL, (d_wv, WVL)], False, dc == DC - 1)
                buf("wv_s").used(st, st.v)
                buf("x_bf").used(st, st.v)
                pstg = (sti * (EW // EBW) + ebi) % 2
                P.op("scalar", (lambda e, p=pvp, pstg=pstg: e.activation(
                        vstage[pstg][:, 0:EBW], p[:, 0:EBW], ACT.Identity)),
                     deps=[(st, st.v)] + buf("vstage%d" % pstg).take(), inc=sc)
                pb_.used(sc, sc.v)
                DMA("scalar", v_d[sti, :, e0:e0 + EBW], vstage[pstg][:, 0:EBW],
                    [(sc, sc.v)], d_vw[pstg])
                buf("vstage%d" % pstg).used(d_vw[pstg], d_vw[pstg].v)
            emit_absmean(1)
    VDONE = [(d_vw[0], d_vw[0].v), (d_vw[1], d_vw[1].v)]
    esV.close()
    esT.close()
    qf = sbA("qf", [128, S], F32)
    kf = sbA("kf", [128, S], F32)
    qsw = sbA("qsw", [128, S], F32)
    rt1 = sbA("rt1", [128, S], F32)
    q_bf = [sbA("q_bf%d" % i, [128, S], BF16) for i in range(2)]
    k_bf = [sbA("k_bf%d" % i, [128, S], BF16) for i in range(2)]
    sqbs = [sbA("sqbs%d" % i, [128, 512], BF16) for i in range(2 * SB)]
    dsum = [sbA("dsum%d" % i, [128, 512], F32) for i in range(2)]
    dsumb = [sbA("dsumb%d" % i, [128, 512], BF16) for i in range(2)]
    negb2 = [sbA("negb2_%d" % i, [1, 1], F32) for i in range(2)]
    expb = [sbA("expb%d" % i, [128, 512], BF16) for i in range(4)]
    recb = sbA("recb", [128, 512], F32)
    attb_t = [sbA("attbt%d" % i, [128, 512], BF16) for i in range(2)]
    qmx = sbA("qmx", [1, 1], F32)
    kmx = sbA("kmx", [1, 1], F32)
    tmx = sbA("tmx", [1, 1], F32)
    nB128 = [sbA("nB128_%d" % i, [128, 1], F32) for i in range(2)]
    rec1b = [sbA("rec1b%d" % i, [1, 512], BF16) for i in range(2)]
    v_dr = v_d.rearrange("t p e -> p t e")
    if debug:
        DMA("scalar", dbg["dbg_vtm"], v_d, VDONE + [(d_dbg, d_dbg.v)], d_dbg)

    wqTr = wqT.rearrange("(c p) e -> p c e", p=128)
    wkTr = wkT.rearrange("(c p) e -> p c e", p=128)
    DMA("gpsimd", wq_s[:], wqTr[:, :, 0:128], (), d_wq)
    DMA("gpsimd", wk_s[:], wkTr[:, :, 0:128], (), d_wk)
    DMA("sync", vh_s[0][:], v_dr[:, :, 0:128], list(VDONE), d_vh[0])

    head_state = {}

    def emit_proj(h):
        for (wsem, wbuf_s, wbufn, bias_s, outf, outn) in (
                (d_wq, wq_s, "wq_s", c_bq, qf, "qf"),
                (d_wk, wk_s, "wk_s", c_bk, kf, "kf")):
            WL = 16 * (h + 1)
            for sbi in range(SB):
                par = sbi % 2
                pb_ = buf("pA%d" % par)
                war = pb_.take()
                for dc in range(DC):
                    MM(pA[par][:], wbuf_s[:, dc, :], x_bf[:, dc, sbi * 512:(sbi + 1) * 512],
                       [XL, (wsem, WL)] + (war if dc == 0 else []), dc == 0, dc == DC - 1)
                buf(wbufn).used(st, st.v)
                buf("x_bf").used(st, st.v)
                P.op("scalar", (lambda e, par=par, sbi=sbi, outf=outf, bias_s=bias_s, h=h:
                        e.activation(outf[:, sbi * 512:(sbi + 1) * 512], pA[par][:],
                                     ACT.Identity, bias=bias_s[:, h:h + 1])),
                     deps=[(st, st.v), CONST] + (buf(outn).take() if sbi == 0 else []), inc=sc)
                pb_.used(sc, sc.v)
                emit_absmean(4)
        QKEV = sc.v
        sqv = {}
        for si, (outf, outn) in enumerate(((qf, "qf"), (kf, "kf"))):
            for sbi in range(SB):
                j = si * SB + sbi
                sq_ = sqbs[j]
                sqn = "sqbs%d" % j
                P.op("vector", (lambda e, outf=outf, sbi=sbi, sq_=sq_: e.tensor_mul(
                        sq_[:], outf[:, sbi * 512:(sbi + 1) * 512],
                        outf[:, sbi * 512:(sbi + 1) * 512])),
                     deps=[(sc, QKEV)] + buf(sqn).take(), inc=sv)
                buf(outn).used(sv, sv.v)
                sqv[(si, sbi)] = (sqn, sq_, sv.v)
        head_state[h] = {"QKEV": QKEV, "sqv": sqv}

    def emit_stats(h):
        hp = h % 2
        sqv = head_state[h]["sqv"]
        for si, (mxb, mxn) in enumerate(((qmx, "qmx"), (kmx, "kmx"))):
            for sbi in range(SB):
                sp = (si * SB + sbi) % 2
                sqn, sq_, sqdone = sqv[(si, sbi)]
                pst = pT if sp == 0 else pB
                pstn = "pT" if sp == 0 else "pB"
                MM(pst[0:1, :], c_oneskb[:], sq_[:],
                   [(sv, sqdone), CONST] + buf(pstn).take(), True, True)
                buf(sqn).used(st, st.v)
                P.op("vector", (lambda e, pst=pst: e.reduce_max(tmx[:], pst[0:1, :], AX)),
                     deps=[(st, st.v)] + buf("tmx").take(), inc=sv)
                buf(pstn).used(sv, sv.v)
                if sbi == 0:
                    P.op("vector", (lambda e, mxb=mxb: e.tensor_copy(mxb[:], tmx[:])),
                         deps=[(sv, sv.v)] + buf(mxn).take(), inc=sv)
                else:
                    P.op("vector", (lambda e, mxb=mxb: e.tensor_max(mxb[:], mxb[:], tmx[:])),
                         deps=[(sv, sv.v)], inc=sv)
                buf("tmx").used(sv, sv.v)
        NRM = sv.v
        P.op("vector", lambda e: e.tensor_mul(tm11[0][:], qmx[:], kmx[:]),
             deps=[(sv, NRM)] + buf("tm0").take(), inc=sv)
        buf("qmx").used(sv, sv.v)
        buf("kmx").used(sv, sv.v)
        P.op("scalar", lambda e: e.activation(tm11[1][:], tm11[0][:], ACT.Sqrt,
                                              scale=float(1.02 * 1.02 / 128.0)),
             deps=[(sv, sv.v)] + buf("tm1").take(), inc=sc)
        buf("tm0").used(sc, sc.v)
        P.op("vector", (lambda e, hp=hp: e.tensor_scalar(negb2[hp][:], tm11[1][:],
                                                         -1.0, None, ALU.mult)),
             deps=[(sc, sc.v)] + buf("negb2_%d" % hp).take(), inc=sv)
        buf("tm1").used(sv, sv.v)
        NEGB = sv.v
        MM(pB[:, 0:1], c_ones1[:], negb2[hp][:], [(sv, NEGB), CONST] + buf("pB").take(),
           True, True)
        buf("negb2_%d" % hp).used(st, st.v)
        P.op("vector", (lambda e, hp=hp: e.tensor_copy(nB128[hp][:], pB[:, 0:1])),
             deps=[(st, st.v)] + buf("nB128_%d" % hp).take(), inc=sv)
        buf("pB").used(sv, sv.v)
        head_state[h]["NBB"] = sv.v

    def emit_rope(h):
        hp = h % 2
        QKEV = head_state[h]["QKEV"]
        war = buf("qsw").take()
        DMA("gpsimd", qsw[0:64, :], qf[64:128, :], [(sc, QKEV)] + war, d_swq)
        DMA("gpsimd", qsw[64:128, :], qf[0:64, :], [(sc, QKEV)] + war, d_swq)
        SWQ = d_swq.v
        buf("qf").used(d_swq, SWQ)
        P.op("vector", lambda e: e.tensor_mul(rt1[:], qf[:], c_cc2[:]),
             deps=[(sc, QKEV), TRIG, CONST] + buf("rt1").take(), inc=sv)
        buf("qf").used(sv, sv.v)
        P.op("vector", lambda e: e.tensor_mul(qsw[:], qsw[:], c_ss2[:]),
             deps=[(d_swq, SWQ), TRIG], inc=sv)
        P.op("vector", (lambda e, hp=hp: e.tensor_add(q_bf[hp][:], rt1[:], qsw[:])),
             deps=[(sv, sv.v)] + buf("q_bf%d" % hp).take(), inc=sv)
        buf("rt1").used(sv, sv.v)
        buf("qsw").used(sv, sv.v)
        QROPE = sv.v
        war = buf("qsw").take()
        DMA("gpsimd", qsw[0:64, :], kf[64:128, :], [(sc, QKEV), (sv, QROPE)] + war, d_swk)
        DMA("gpsimd", qsw[64:128, :], kf[0:64, :], [(sc, QKEV), (sv, QROPE)] + war, d_swk)
        SWK = d_swk.v
        buf("kf").used(d_swk, SWK)
        P.op("vector", lambda e: e.tensor_mul(rt1[:], kf[:], c_cc2k[:]),
             deps=[(sc, QKEV)] + buf("rt1").take(), inc=sv)
        buf("kf").used(sv, sv.v)
        P.op("vector", lambda e: e.tensor_mul(qsw[:], qsw[:], c_ss2k[:]),
             deps=[(d_swk, SWK)], inc=sv)
        P.op("vector", (lambda e, hp=hp: e.tensor_add(k_bf[hp][:], rt1[:], qsw[:])),
             deps=[(sv, sv.v)] + buf("k_bf%d" % hp).take(), inc=sv)
        buf("rt1").used(sv, sv.v)
        buf("qsw").used(sv, sv.v)
        head_state[h]["KROPE"] = sv.v
        if debug and h == 0:
            DMA("scalar", dbg["dbg_q0"], q_bf[0][:],
                [(sv, sv.v), (d_dbg, d_dbg.v)], d_dbg)
            DMA("scalar", dbg["dbg_k0"], k_bf[0][:],
                [(sv, sv.v), (d_dbg, d_dbg.v)], d_dbg)
            buf("q_bf0").used(d_dbg, d_dbg.v)
            buf("k_bf0").used(d_dbg, d_dbg.v)

    def emit_attn(h):
        hp = h % 2
        KROPE = head_state[h]["KROPE"]
        NBB = head_state[h]["NBB"]

        pend = None

        def emit_norm(p):
            qb0, PVD0, RCP0 = p
            MM(pB[:], c_ones1b[:], rec1b[qb0 % 2][:],
               [(sv, RCP0), CONST] + buf("pB").take(), True, True)
            buf("rec1b%d" % (qb0 % 2)).used(st, st.v)
            P.op("scalar", lambda e: e.activation(recb[:], pB[:], ACT.Identity),
                 deps=[(st, st.v)] + buf("recb").take(), inc=sc)
            buf("pB").used(sc, sc.v)
            RECB = sc.v
            pa = qb0 % 2
            ab_ = buf("attbt%d" % pa)
            P.op("vector", (lambda e, pa=pa, pv_=pV[qb0 % 2]: e.tensor_mul(
                    attb_t[pa][:], pv_[:], recb[:])),
                 deps=[(sc, RECB), (st, PVD0)] + ab_.take(), inc=sv)
            buf("pV%d" % (qb0 % 2)).used(sv, sv.v)
            buf("recb").used(sv, sv.v)
            DMA("scalar", attb_d[h, :, qb0 * 512:(qb0 + 1) * 512], attb_t[pa][:],
                [(sv, sv.v)], d_atw[pa])
            ab_.used(d_atw[pa], d_atw[pa].v)

        for qb in range(SB):
            kc_max = ((qb + 1) * 512) // 128
            pvp = qb % 2
            pv_war = buf("pV%d" % pvp).take()
            pT_war = buf("pT").take()
            pv_pend = None
            ds_last = [None, None]
            ds_cast = [None, None]
            sc_banks = (pS[0], pS[1], pA[0], pA[1])
            sc_names = ("pS0", "pS1", "pA0", "pA1")
            for kc in range(kc_max + 1):
                if kc < kc_max:
                    par = kc % 4
                    pb_ = buf(sc_names[par])
                    eb_ = buf("expb%d" % par)
                    MM(sc_banks[par][:], k_bf[hp][:, kc * 128:(kc + 1) * 128],
                       q_bf[hp][:, qb * 512:(qb + 1) * 512],
                       [(sv, KROPE)] + pb_.take(), True, True)
                    SCO = st.v
                    buf("k_bf%d" % hp).used(st, SCO)
                    buf("q_bf%d" % hp).used(st, SCO)
                    P.op("scalar", (lambda e, par=par, hp=hp: e.activation(
                            expb[par][:], sc_banks[par][:], ACT.Exp,
                            bias=nB128[hp][:], scale=1.0)),
                         deps=[(st, SCO), (sv, NBB)] + eb_.take(), inc=sc)
                    pb_.used(sc, sc.v)
                    buf("nB128_%d" % hp).used(sc, sc.v)
                    dep_exp = (sc, sc.v)
                    off = kc * 128 - qb * 512
                    if off >= 0:
                        mi = off // 128
                        P.op("vector", (lambda e, par=par, mi=mi: e.tensor_mul(
                                expb[par][:], expb[par][:], c_masks[:, mi, :])),
                             deps=[dep_exp, CONST], inc=sv)
                        dep_exp = (sv, sv.v)
                if pv_pend is not None:
                    kp, dep_p = pv_pend
                    parp = kp % 4
                    cp = kp % 2
                    MM(pV[pvp][:], vh_s[hp][:, kp, :], expb[parp][:],
                       [dep_p, (d_vh[hp], 16 * (h // 2 + 1))] +
                       (pv_war if kp == 0 else []),
                       kp == 0, kp == kc_max - 1)
                    buf("expb%d" % parp).used(st, st.v)
                    buf("vh_s%d" % hp).used(st, st.v)
                    if ds_last[cp] is None:
                        P.op("vector", (lambda e, parp=parp, cp=cp: e.tensor_copy(
                                dsum[cp][:], expb[parp][:])),
                             deps=[dep_p] + buf("dsum%d" % cp).take(), inc=sv)
                    else:
                        P.op("vector", (lambda e, parp=parp, cp=cp: e.tensor_add(
                                dsum[cp][:], dsum[cp][:], expb[parp][:])),
                             deps=[dep_p, (sv, ds_last[cp])], inc=sv)
                    ds_last[cp] = sv.v
                    buf("expb%d" % parp).used(sv, sv.v)
                    if kp == (kc_max - 1 if (kc_max - 1) % 2 == cp else kc_max - 2):
                        P.op("vector", (lambda e, cp=cp: e.tensor_copy(
                                dsumb[cp][:], dsum[cp][:])),
                             deps=[(sv, ds_last[cp])] + buf("dsumb%d" % cp).take(), inc=sv)
                        buf("dsum%d" % cp).used(sv, sv.v)
                        ds_cast[cp] = sv.v
                if kc < kc_max:
                    pv_pend = (kc, dep_exp)
            PVD = st.v
            MM(pT[0:1, :], c_oneskb[:], dsumb[0][:],
               [(sv, ds_cast[0]), CONST] + pT_war, True, False)
            buf("dsumb0").used(st, st.v)
            MM(pT[0:1, :], c_oneskb[:], dsumb[1][:],
               [(sv, ds_cast[1]), CONST], False, True)
            buf("dsumb1").used(st, st.v)
            DEN = st.v
            def _recip(e, rb=rec1b[qb % 2]):
                with nc.allow_low_precision(reason="bf16 softmax scale, within tolerance"):
                    return e.reciprocal(rb[:], pT[0:1, :])
            P.op("vector", _recip,
                 deps=[(st, DEN)] + buf("rec1b%d" % (qb % 2)).take(), inc=sv)
            buf("pT").used(sv, sv.v)
            RCP = sv.v
            if pend is not None:
                emit_norm(pend)
            pend = (qb, PVD, RCP)
        emit_norm(pend)
        if debug and h == 0:
            DMA("scalar", dbg["dbg_att0"], attb_d[0],
                [(d_atw[0], d_atw[0].v), (d_atw[1], d_atw[1].v), (d_dbg, d_dbg.v)], d_dbg)

    def emit_prefetch(hn):
        if hn >= H:
            return
        DMA("gpsimd", wq_s[:], wqTr[:, :, hn * 128:(hn + 1) * 128],
            buf("wq_s").take(), d_wq)
        DMA("gpsimd", wk_s[:], wkTr[:, :, hn * 128:(hn + 1) * 128],
            buf("wk_s").take(), d_wk)
        hpn = hn % 2
        DMA("sync", vh_s[hpn][:], v_dr[:, :, hn * 128:(hn + 1) * 128],
            list(VDONE) + buf("vh_s%d" % hpn).take(), d_vh[hpn])

    for h in range(H):
        emit_proj(h)
        emit_rope(h)
        if h >= 1:
            emit_attn(h - 1)
        emit_stats(h)
        emit_prefetch(h + 1)
    emit_attn(H - 1)

    emit_absmean(n_ab)
    if debug:
        DMA("scalar", dbg["dbg_aa1"], absac[0][:], [(sv, sv.v), (d_dbg, d_dbg.v)], d_dbg)
        DMA("scalar", dbg["dbg_aa2"], absac[1][:], [(sv, sv.v), (d_dbg, d_dbg.v)], d_dbg)
    for w_i, (thi, tlo, tpos, tneg) in enumerate(((TQ_HI, TQ_LO, t_t1, t_t1n),
                                                  (TQ_HI, TQ_LO, t_t2, t_t2n))):
        MM(pT[:, 0:1], c_onesk[:], absac[w_i][:],
           [(sv, sv.v), CONST] + buf("pT").take(), True, True)
        buf("absac%d" % w_i).used(st, st.v)
        P.op("vector", (lambda e, thi=thi: e.tensor_scalar(
                tm11[0][:], pT[0:1, 0:1], thi, None, ALU.mult)),
             deps=[(st, st.v)] + buf("tm0").take(), inc=sv)
        P.op("vector", (lambda e, tlo=tlo: e.scalar_tensor_tensor(
                tm11[0][:], pT[0:1, 0:1], tlo, tm11[0][:], ALU.mult, ALU.add)),
             deps=[(sv, sv.v)], inc=sv)
        buf("pT").used(sv, sv.v)
        MM(pB[:, 0:1], c_ones1[:], tm11[0][:], [(sv, sv.v), CONST] + buf("pB").take(),
           True, True)
        buf("tm0").used(st, st.v)
        P.op("vector", (lambda e, tpos=tpos: e.tensor_copy(tpos[:], pB[:, 0:1])),
             deps=[(st, st.v)], inc=sv)
        buf("pB").used(sv, sv.v)
        P.op("vector", (lambda e, tpos=tpos, tneg=tneg: e.tensor_scalar(
                tneg[:], tpos[:], -1.0, None, ALU.mult)),
             deps=[(sv, sv.v)], inc=sv)
        if debug:
            DMA("scalar", dbg["dbg_t1" if w_i == 0 else "dbg_t2"], tm11[0][:],
                [(sv, sv.v), (d_dbg, d_dbg.v)], d_dbg)
            buf("tm0").used(d_dbg, d_dbg.v)
    THRD = sv.v
    if debug:
        DMA("scalar", dbg["dbg_cc"], c_cc2[:], [TRIG, (d_dbg, d_dbg.v)], d_dbg)
        DMA("scalar", dbg["dbg_ss"], c_ss2[:], [TRIG, (d_dbg, d_dbg.v)], d_dbg)

    P.barrier(extra=[(d_ab[2], d_ab[2].v), (d_ab[3], d_ab[3].v),
                     (d_atw[0], d_atw[0].v), (d_atw[1], d_atw[1].v), (d_x, d_x.v), (d_wv, d_wv.v),
                     (d_wq, d_wq.v), (d_wk, d_wk.v), (d_swq, d_swq.v),
                     (d_swk, d_swk.v), (d_trig, d_trig.v), (d_trigc, d_trigc.v),
                     (d_srow, d_srow.v),
                     (d_cbf, d_cbf.v),
                     (d_vw[0], d_vw[0].v), (d_vw[1], d_vw[1].v),
                     (d_vh[0], d_vh[0].v), (d_vh[1], d_vh[1].v),
                     (d_ab[0], d_ab[0].v), (d_ab[1], d_ab[1].v), (d_dbg, d_dbg.v)])
    esA.close()
    for n in list(B.keys()):
        if n not in ("x1bf",):
            B.pop(n, None)

    esBC = ExitStack()
    x1bf = esBC.enter_context(nc.sbuf_tensor("x1bf", [128, DC, S], BF16))
    esB = ExitStack()
    sbB = lambda n, s_, d: esB.enter_context(nc.sbuf_tensor(n, s_, d))
    psB = lambda n, s_: esB.enter_context(nc.psum_tensor(n, s_, F32))
    wo_s = [sbB("wo_s%d" % i, [128, DC, 128], BF16) for i in range(2)]
    att_s = [sbB("att_s%d" % i, [128, H, 512], BF16) for i in range(2)]
    xf_s = [sbB("xf_s%d" % i, [128, 512], F32) for i in range(2)]
    z_sb = sbB("z_sb", [128, DC, 512], F32)
    zb_b = [sbB("zb_b%d" % i, [128, 512], BF16) for i in range(2)]
    zz_b = [sbB("zz_b%d" % i, [128, 512], BF16) for i in range(2)]
    arow = sbB("arow", [1, 512], F32)
    brow = sbB("brow", [1, 512], F32)
    r0r = sbB("r0r", [1, 512], F32)
    r2r = sbB("r2r", [1, 512], F32)
    aB = sbB("aB", [128, 512], F32)
    bB = sbB("bB", [128, 512], F32)
    x1f = [sbB("x1f%d" % i, [128, 512], F32) for i in range(2)]
    pA2 = [psB("pA2%d" % i, [128, 512]) for i in range(2)]
    pT1 = psB("pT1", [1, 512])
    pT2 = psB("pT2", [1, 512])
    pBa = psB("pBa", [128, 512])
    pBb = psB("pBb", [128, 512])

    woTr = woT.rearrange("(c p) e -> p c e", p=128)
    attb_r = attb_d.rearrange("h p s -> p h s")

    def emit_ln1_tail(sbi, STATS):
        P.op("scalar", lambda e: e.activation(arow[:], pT1[0:1, :], ACT.Identity,
                                              scale=1.0 / D),
             deps=[(st, STATS)] + buf("arow").take(), inc=sc)
        P.op("scalar", lambda e: e.activation(brow[:], pT2[0:1, :], ACT.Identity,
                                              scale=1.0 / D),
             deps=[(st, STATS)] + buf("brow").take(), inc=sc)
        buf("pT1").used(sc, sc.v)
        buf("pT2").used(sc, sc.v)
        MS = sc.v
        P.op("vector", lambda e: e.tensor_mul(r2r[:], arow[:], arow[:]),
             deps=[(sc, MS)] + buf("r2r").take(), inc=sv)
        P.op("vector", lambda e: e.tensor_sub(r0r[:], brow[:], r2r[:]),
             deps=[(sv, sv.v)] + buf("r0r").take(), inc=sv)
        buf("brow").used(sv, sv.v)
        P.op("vector", lambda e: e.tensor_scalar(r0r[:], r0r[:], 1.0, EPS, ALU.mult, ALU.add),
             deps=[(sv, sv.v)], inc=sv)
        VPE = sv.v
        P.op("scalar", lambda e: e.activation(r2r[:], r0r[:], ACT.Sqrt),
             deps=[(sv, VPE)], inc=sc)
        P.op("vector", lambda e: e.reciprocal(r2r[:], r2r[:]),
             deps=[(sc, sc.v)], inc=sv)
        P.op("vector", lambda e: e.tensor_mul(brow[:], r2r[:], r2r[:]),
             deps=[(sv, sv.v)], inc=sv)
        P.op("vector", lambda e: e.tensor_mul(brow[:], brow[:], r0r[:]),
             deps=[(sv, sv.v)], inc=sv)
        buf("r0r").used(sv, sv.v)
        P.op("vector", lambda e: e.tensor_scalar(brow[:], brow[:], -0.5, 1.5,
                                                 ALU.mult, ALU.add),
             deps=[(sv, sv.v)], inc=sv)
        P.op("vector", lambda e: e.tensor_mul(r2r[:], r2r[:], brow[:]),
             deps=[(sv, sv.v)], inc=sv)
        P.op("vector", lambda e: e.scalar_tensor_tensor(brow[:], arow[:], -1.0, r2r[:],
                                                        ALU.mult, ALU.mult),
             deps=[(sv, sv.v)], inc=sv)
        buf("arow").used(sv, sv.v)
        ROWS = sv.v
        MM(pBa[:], c_ones1[:], r2r[:], [(sv, ROWS), CONST] + buf("pBa").take(), True, True)
        buf("r2r").used(st, st.v)
        MM(pBb[:], c_ones1[:], brow[:], [(sv, ROWS), CONST] + buf("pBb").take(), True, True)
        buf("brow").used(st, st.v)
        BC = st.v
        P.op("scalar", lambda e: e.activation(aB[:], pBa[:], ACT.Identity),
             deps=[(st, BC)] + buf("aB").take(), inc=sc)
        buf("pBa").used(sc, sc.v)
        P.op("scalar", lambda e: e.activation(bB[:], pBb[:], ACT.Identity),
             deps=[(st, BC)] + buf("bB").take(), inc=sc)
        buf("pBb").used(sc, sc.v)
        ABB = sc.v
        for ec in range(DC):
            parf = ec % 2
            fb_ = buf("x1f%d" % parf)
            P.op("vector", (lambda e, ec=ec, parf=parf: e.tensor_mul(
                    x1f[parf][:], z_sb[:, ec, :], aB[:])),
                 deps=[(sc, ABB)] + fb_.take(), inc=sv)
            buf("aB").used(sv, sv.v)
            buf("z_sb%d" % ec).used(sv, sv.v)
            P.op("vector", (lambda e, parf=parf: e.tensor_add(
                    x1f[parf][:], x1f[parf][:], bB[:])),
                 deps=[(sv, sv.v)], inc=sv)
            buf("bB").used(sv, sv.v)
            X1D = sv.v
            P.op("scalar", (lambda e, ec=ec, sbi=sbi, parf=parf: e.activation(
                    x1bf[:, ec, sbi * 512:(sbi + 1) * 512], x1f[parf][:], ACT.Identity)),
                 deps=[(sv, X1D)], inc=sc)
            fb_.used(sc, sc.v)
            DMA("scalar", x1_d[ec, :, sbi * 512:(sbi + 1) * 512], x1f[parf][:],
                [(sv, X1D)], d_x1w[parf])
            fb_.used(d_x1w[parf], d_x1w[parf].v)

    pend_tail = [None]
    for sbi in range(SB):
        atp = sbi % 2
        if sbi == 0:
            DMA("sync", att_s[0][:], attb_r[:, :, 0:512],
                [(d_atw[0], d_atw[0].v), (d_atw[1], d_atw[1].v)] + buf("att_s0").take(),
                d_ats[0])
        ATTL = 16 * (sbi // 2 + 1)
        pend_stat = [None]
        for ec in range(DC):
            parw = ec % 2
            DMA("gpsimd", wo_s[parw][:], woTr[:, :, ec * 128:(ec + 1) * 128],
                buf("wo_s%d" % parw).take(), d_wo[parw])
            WOL = d_wo[parw].v
            DMA("sync", xf_s[parw][:], xTr[:, ec, sbi * 512:(sbi + 1) * 512],
                buf("xf_s%d" % parw).take(), d_xr[parw])
            XRL = d_xr[parw].v
            pb_ = buf("pA2%d" % parw)
            war = pb_.take()
            for hc in range(H):
                MM(pA2[parw][:], wo_s[parw][:, hc, :], att_s[atp][:, hc, :],
                   [(d_wo[parw], WOL), (d_ats[atp], ATTL)] + (war if hc == 0 else []),
                   hc == 0, hc == H - 1)
            buf("wo_s%d" % parw).used(st, st.v)
            buf("att_s%d" % atp).used(st, st.v)
            if ec == 0 and pend_tail[0] is not None:
                pend_tail[0]()
                pend_tail[0] = None
            if pend_stat[0] is not None:
                pend_stat[0]()
                pend_stat[0] = None
            if ec == min(2, DC - 1) and sbi + 1 < SB:
                DMA("sync", att_s[(sbi + 1) % 2][:],
                    attb_r[:, :, (sbi + 1) * 512:(sbi + 2) * 512],
                    [(d_atw[0], d_atw[0].v), (d_atw[1], d_atw[1].v)] +
                    buf("att_s%d" % ((sbi + 1) % 2)).take(), d_ats[(sbi + 1) % 2])
            P.op("scalar", (lambda e, parw=parw, ec=ec: e.activation(
                    z_sb[:, ec, :], pA2[parw][:], ACT.Identity, bias=c_bo[:, ec:ec + 1])),
                 deps=[(st, st.v), CONST] + buf("z_sb%d" % ec).take(), inc=sc)
            pb_.used(sc, sc.v)
            P.op("vector", (lambda e, parw=parw, ec=ec: e.tensor_add(
                    z_sb[:, ec, :], z_sb[:, ec, :], xf_s[parw][:])),
                 deps=[(sc, sc.v), (d_xr[parw], XRL)], inc=sv)
            ZD = sv.v
            buf("xf_s%d" % parw).used(sv, ZD)
            P.op("scalar", (lambda e, parw=parw, ec=ec: e.activation(
                    zb_b[parw][:], z_sb[:, ec, :], ACT.Identity)),
                 deps=[(sv, ZD)] + buf("zb_b%d" % parw).take(), inc=sc)
            ZB = sc.v
            buf("z_sb%d" % ec).used(sc, ZB)
            P.op("vector", (lambda e, parw=parw, ec=ec: e.tensor_mul(
                    zz_b[parw][:], z_sb[:, ec, :], z_sb[:, ec, :])),
                 deps=[(sv, ZD)] + buf("zz_b%d" % parw).take(), inc=sv)
            ZZ = sv.v
            buf("z_sb%d" % ec).used(sv, ZZ)

            def _stat(ec=ec, parw=parw, ZB=ZB, ZZ=ZZ):
                MM(pT1[:], c_oneskb[:], zb_b[parw][:],
                   [(sc, ZB), CONST] + (buf("pT1").take() if ec == 0 else []),
                   ec == 0, ec == DC - 1)
                buf("zb_b%d" % parw).used(st, st.v)
                MM(pT2[:], c_oneskb[:], zz_b[parw][:],
                   [(sv, ZZ), CONST] + (buf("pT2").take() if ec == 0 else []),
                   ec == 0, ec == DC - 1)
                buf("zz_b%d" % parw).used(st, st.v)
            pend_stat[0] = _stat
        pend_stat[0]()
        pend_stat[0] = None
        STATS = st.v
        pend_tail[0] = (lambda sbi=sbi, STATS=STATS: emit_ln1_tail(sbi, STATS))
    pend_tail[0]()
    pend_tail[0] = None
    if debug:
        DMA("scalar", dbg["dbg_x1"], x1_d, [(d_x1w[0], d_x1w[0].v), (d_x1w[1], d_x1w[1].v), (d_dbg, d_dbg.v)], d_dbg)

    P.barrier(extra=[(d_x1w[0], d_x1w[0].v), (d_x1w[1], d_x1w[1].v),
                     (d_ats[0], d_ats[0].v), (d_ats[1], d_ats[1].v),
                     (d_wo[0], d_wo[0].v), (d_wo[1], d_wo[1].v),
                     (d_xr[0], d_xr[0].v), (d_xr[1], d_xr[1].v), (d_dbg, d_dbg.v)])
    esB.close()
    for n in list(B.keys()):
        if n not in ("x1bf",):
            B.pop(n, None)

    esC = ExitStack()
    sbC = lambda n, s_, d: esC.enter_context(nc.sbuf_tensor(n, s_, d))
    psC = lambda n, s_: esC.enter_context(nc.psum_tensor(n, s_, F32))
    w1b = [sbC("w1b%d" % i, [128, DC, 128], F32) for i in range(2)]
    a1b = [sbC("a1b%d" % i, [128, DC * 128], F32) for i in range(2)]
    w1q = [sbC("w1q%d" % i, [128, DC, 128], BF16) for i in range(2)]
    hb = [sbC("hb%d" % i, [128, S], BF16) for i in range(2)]
    w2b = [sbC("w2b%d" % i, [128, D], F32) for i in range(2)]
    a2b = [sbC("a2b%d" % i, [128, D], F32) for i in range(2)]
    w2qs = [sbC("w2qs%d" % i, [128, D], BF16) for i in range(2)]
    NPH = 2 * SB
    pH = [psC("pH%d" % i, [128, 512]) for i in range(NPH)]

    for fc in range(FC):
        par = fc % 2
        DMA("sync", w1b[par][:], w1Tr[:, :, fc * 128:(fc + 1) * 128],
            buf("w1b%d" % par).take(), d_w1[par])
        W1L = d_w1[par].v
        P.op("vector", (lambda e, par=par: e.tensor_scalar(
                a1b[par][:], w1b[par][:].rearrange("p c f -> p (c f)"), t_t1n[:],
                None, ALU.is_le)),
             deps=[(d_w1[par], W1L), (sv, THRD)] + buf("a1b%d" % par).take(), inc=sv)
        P.op("vector", (lambda e, par=par: e.scalar_tensor_tensor(
                w1q[par][:].rearrange("p c f -> p (c f)"),
                w1b[par][:].rearrange("p c f -> p (c f)"), t_t1[:],
                a1b[par][:], ALU.is_ge, ALU.subtract)),
             deps=[(sv, sv.v)] + buf("w1q%d" % par).take(), inc=sv)
        buf("w1b%d" % par).used(sv, sv.v)
        buf("a1b%d" % par).used(sv, sv.v)
        QD = sv.v
        for dc in range(DC):
            for sbi in range(SB):
                bank = par * SB + sbi
                MM(pH[bank][:], w1q[par][:, dc, :], x1bf[:, dc, sbi * 512:(sbi + 1) * 512],
                   [(sv, QD)] + (buf("pH%d" % bank).take() if dc == 0 else []),
                   dc == 0, dc == DC - 1)
        buf("w1q%d" % par).used(st, st.v)
        buf("x1bf").used(st, st.v)
        HMM = st.v
        for sbi in range(SB):
            bank = par * SB + sbi
            P.op("scalar", (lambda e, bank=bank, sbi=sbi, par=par, fc=fc: e.activation(
                    hb[par][:, sbi * 512:(sbi + 1) * 512], pH[bank][:], ACT.Gelu,
                    bias=c_b1[:, fc:fc + 1])),
                 deps=[(st, HMM), CONST] + (buf("hb%d" % par).take() if sbi == 0 else []),
                 inc=sc)
            buf("pH%d" % bank).used(sc, sc.v)
        DMA("scalar", h_d[fc], hb[par][:], [(sc, sc.v)], d_hw[par])
        buf("hb%d" % par).used(d_hw[par], d_hw[par].v)
        DMA("sync", w2b[par][:], w2Tr[:, fc, :], buf("w2b%d" % par).take(), d_w2[par])
        W2L = d_w2[par].v
        P.op("vector", (lambda e, par=par: e.tensor_scalar(
                a2b[par][:], w2b[par][:], t_t2n[:], None, ALU.is_le)),
             deps=[(d_w2[par], W2L), (sv, THRD)] + buf("a2b%d" % par).take(), inc=sv)
        P.op("vector", (lambda e, par=par: e.scalar_tensor_tensor(
                w2qs[par][:], w2b[par][:], t_t2[:], a2b[par][:],
                ALU.is_ge, ALU.subtract)),
             deps=[(sv, sv.v)] + buf("w2qs%d" % par).take(), inc=sv)
        buf("w2b%d" % par).used(sv, sv.v)
        buf("a2b%d" % par).used(sv, sv.v)
        DMA("scalar", w2q_d[fc], w2qs[par][:], [(sv, sv.v)], d_w2qw[par])
        buf("w2qs%d" % par).used(d_w2qw[par], d_w2qw[par].v)

    P.barrier(extra=[(d_hw[0], d_hw[0].v), (d_hw[1], d_hw[1].v), (d_w2qw[0], d_w2qw[0].v), (d_w2qw[1], d_w2qw[1].v),
                     (d_w1[0], d_w1[0].v), (d_w1[1], d_w1[1].v),
                     (d_w2[0], d_w2[0].v), (d_w2[1], d_w2[1].v), (d_dbg, d_dbg.v)])
    esC.close()
    esBC.close()
    B.clear()

    esD = ExitStack()
    sbD = lambda n, s_, d: esD.enter_context(nc.sbuf_tensor(n, s_, d))
    psD = lambda n, s_: esD.enter_context(nc.psum_tensor(n, s_, F32))
    EW2 = ECW * 128
    NEH = DC // ECW
    w2r = [sbD("w2r%d" % i, [128, FC, EW2], BF16) for i in range(2)]
    hs = [sbD("hs%d" % i, [128, 512], BF16) for i in range(4)]
    x1r = [sbD("x1r%d" % i, [128, 512], F32) for i in range(2)]
    z2t = [sbD("z2t%d" % i, [128, 512], F32) for i in range(2)]
    z2b = [sbD("z2b%d" % i, [128, 512], BF16) for i in range(ECW)]
    zzd = [sbD("zzd%d" % i, [128, 512], BF16) for i in range(ECW)]
    ssum = sbD("ssum", [1, S], F32)
    ssq = sbD("ssq", [1, S], F32)
    ar2 = sbD("ar2", [1, 512], F32)
    br2 = sbD("br2", [1, 512], F32)
    r02 = sbD("r02", [1, 512], F32)
    r22 = sbD("r22", [1, 512], F32)
    aB2 = sbD("aB2", [128, 512], F32)
    bB2 = sbD("bB2", [128, 512], F32)
    z2rd = [sbD("z2rd%d" % i, [128, 512], F32) for i in range(2)]
    outt = [sbD("outt%d" % i, [128, 512], F32) for i in range(2)]
    pY = [psD("pY%d" % i, [128, 512]) for i in range(ECW)]
    pSt_h = [psD("pSt0", [1, 512]), psD("pSt1", [1, 512])]
    pBc_h = [psD("pBc0", [128, 512]), psD("pBc1", [128, 512])]
    pSt = [pSt_h[0][:], pSt_h[1][:]]
    pBc = [pBc_h[0][:], pBc_h[1][:]]
    pSt_names = ["pSt0", "pSt1"]
    pBc_names = ["pBc0", "pBc1"]

    P.op("vector", lambda e: e.memset(ssum[:], 0.0), deps=(), inc=sv)
    P.op("vector", lambda e: e.memset(ssq[:], 0.0), deps=(), inc=sv)

    w2qr = w2q_d.rearrange("f p e -> p f e")
    x1_dr = x1_d
    def _ln2_block(ssl, sbi, SACC):
        P.op("vector", (lambda e, ssl=ssl: e.tensor_scalar(
                ar2[:], ssum[:, ssl], 1.0 / D, None, ALU.mult)),
             deps=[(sv, SACC)] + buf("ar2").take(), inc=sv)
        P.op("vector", (lambda e, ssl=ssl: e.tensor_scalar(
                br2[:], ssq[:, ssl], 1.0 / D, EPS, ALU.mult, ALU.add)),
             deps=[(sv, SACC)] + buf("br2").take(), inc=sv)
        P.op("vector", lambda e: e.tensor_mul(r22[:], ar2[:], ar2[:]),
             deps=[(sv, sv.v)] + buf("r22").take(), inc=sv)
        P.op("vector", lambda e: e.tensor_sub(r02[:], br2[:], r22[:]),
             deps=[(sv, sv.v)] + buf("r02").take(), inc=sv)
        buf("br2").used(sv, sv.v)
        VPE2 = sv.v
        P.op("scalar", lambda e: e.activation(r22[:], r02[:], ACT.Sqrt),
             deps=[(sv, VPE2)], inc=sc)
        P.op("vector", lambda e: e.reciprocal(r22[:], r22[:]),
             deps=[(sc, sc.v)], inc=sv)
        P.op("vector", lambda e: e.tensor_mul(br2[:], r22[:], r22[:]),
             deps=[(sv, sv.v)], inc=sv)
        P.op("vector", lambda e: e.tensor_mul(br2[:], br2[:], r02[:]),
             deps=[(sv, sv.v)], inc=sv)
        buf("r02").used(sv, sv.v)
        P.op("vector", lambda e: e.tensor_scalar(br2[:], br2[:], -0.5, 1.5,
                                                 ALU.mult, ALU.add),
             deps=[(sv, sv.v)], inc=sv)
        P.op("vector", lambda e: e.tensor_mul(r22[:], r22[:], br2[:]),
             deps=[(sv, sv.v)], inc=sv)
        P.op("vector", lambda e: e.scalar_tensor_tensor(br2[:], ar2[:], -1.0, r22[:],
                                                        ALU.mult, ALU.mult),
             deps=[(sv, sv.v)], inc=sv)
        buf("ar2").used(sv, sv.v)
        ROWS2 = sv.v
        MM(pBc[0], c_ones1[:], r22[:],
           [(sv, ROWS2), CONST] + buf(pBc_names[0]).take(), True, True)
        buf("r22").used(st, st.v)
        MM(pBc[1], c_ones1[:], br2[:],
           [(sv, ROWS2), CONST] + buf(pBc_names[1]).take(), True, True)
        buf("br2").used(st, st.v)
        BC2 = st.v
        P.op("scalar", lambda e: e.activation(aB2[:], pBc[0], ACT.Identity),
             deps=[(st, BC2)] + buf("aB2").take(), inc=sc)
        buf(pBc_names[0]).used(sc, sc.v)
        P.op("scalar", lambda e: e.activation(bB2[:], pBc[1], ACT.Identity),
             deps=[(st, BC2)] + buf("bB2").take(), inc=sc)
        buf(pBc_names[1]).used(sc, sc.v)
        ABB2 = sc.v
        for ec in range(DC):
            parr = ec % 2
            DMA("sync", z2rd[parr][:], z2_d[ec, :, sbi * 512:(sbi + 1) * 512],
                [(d_z2w[0], d_z2w[0].v), (d_z2w[1], d_z2w[1].v)] + buf("z2rd%d" % parr).take(), d_z2r[parr])
            Z2RL = d_z2r[parr].v
            P.op("vector", (lambda e, parr=parr: e.tensor_mul(
                    outt[parr][:], z2rd[parr][:], aB2[:])),
                 deps=[(sc, ABB2), (d_z2r[parr], Z2RL)] +
                      buf("outt%d" % parr).take(), inc=sv)
            buf("z2rd%d" % parr).used(sv, sv.v)
            buf("aB2").used(sv, sv.v)
            P.op("vector", (lambda e, parr=parr: e.tensor_add(
                    outt[parr][:], outt[parr][:], bB2[:])),
                 deps=[(sv, sv.v)], inc=sv)
            buf("bB2").used(sv, sv.v)
            DMA("scalar",
                outT.rearrange("(c p) s -> p c s", p=128)[:, ec, sbi * 512:(sbi + 1) * 512],
                outt[parr][:], [(sv, sv.v)], d_out[parr])
            buf("outt%d" % parr).used(d_out[parr], d_out[parr].v)
    defer_ln2 = [None]
    pend_dstat = [None]
    for eh in range(NEH):
        ep = eh % 2
        if eh == 0:
            for ee in range(min(2, NEH)):
                DMA("sync", w2r[ee % 2][:], w2qr[:, :, ee * EW2:(ee + 1) * EW2],
                    [(d_w2qw[0], d_w2qw[0].v), (d_w2qw[1], d_w2qw[1].v)] +
                    buf("w2r%d" % (ee % 2)).take(), d_w2r)
        elif eh + 1 < NEH:
            DMA("sync", w2r[(eh + 1) % 2][:], w2qr[:, :, (eh + 1) * EW2:(eh + 2) * EW2],
                buf("w2r%d" % ((eh + 1) % 2)).take(), d_w2r)
        W2RL = 16 * min(eh + 2, NEH)
        for sbi in range(SB):
            for fc in range(FC):
                hi_ = fc % 4
                DMA("sync", hs[hi_][:], h_d[fc, :, sbi * 512:(sbi + 1) * 512],
                    [(d_hw[0], d_hw[0].v), (d_hw[1], d_hw[1].v)] + buf("hs%d" % hi_).take(), d_hs[hi_])
                HL = d_hs[hi_].v
                for j in range(ECW):
                    MM(pY[j][:], w2r[ep][:, fc, j * 128:(j + 1) * 128], hs[hi_][:],
                       [(d_hs[hi_], HL), (d_w2r, W2RL)] +
                       (buf("pY%d" % j).take() if fc == 0 else []),
                       fc == 0, fc == FC - 1)
                buf("hs%d" % hi_).used(st, st.v)
                if fc == 4 and pend_dstat[0] is not None:
                    pend_dstat[0]()
                    pend_dstat[0] = None
            SWEEP = st.v
            buf("w2r%d" % ep).used(st, SWEEP)
            if pend_dstat[0] is not None:
                pend_dstat[0]()
                pend_dstat[0] = None
            if defer_ln2[0] is not None:
                defer_ln2[0]()
                defer_ln2[0] = None
            stat_deps = []
            for j in range(ECW):
                ec = eh * ECW + j
                parz = j % 2
                DMA("sync", x1r[parz][:], x1_dr[ec, :, sbi * 512:(sbi + 1) * 512],
                    [(d_x1w[0], d_x1w[0].v), (d_x1w[1], d_x1w[1].v)] + buf("x1r%d" % parz).take(), d_x1r[parz])
                X1RL = d_x1r[parz].v
                P.op("scalar", (lambda e, j=j, ec=ec, parz=parz: e.activation(
                        z2t[parz][:], pY[j][:], ACT.Identity, bias=c_b2[:, ec:ec + 1])),
                     deps=[(st, SWEEP), CONST] + buf("z2t%d" % parz).take(), inc=sc)
                buf("pY%d" % j).used(sc, sc.v)
                P.op("vector", (lambda e, parz=parz: e.tensor_add(
                        z2t[parz][:], z2t[parz][:], x1r[parz][:])),
                     deps=[(sc, sc.v), (d_x1r[parz], X1RL)], inc=sv)
                Z2D = sv.v
                buf("x1r%d" % parz).used(sv, Z2D)
                P.op("scalar", (lambda e, j=j, parz=parz: e.activation(
                        z2b[j][:], z2t[parz][:], ACT.Identity)),
                     deps=[(sv, Z2D)] + buf("z2b%d" % j).take(), inc=sc)
                ZB2 = sc.v
                buf("z2t%d" % parz).used(sc, ZB2)
                P.op("vector", (lambda e, j=j, parz=parz: e.tensor_mul(
                        zzd[j][:], z2t[parz][:], z2t[parz][:])),
                     deps=[(sv, Z2D)] + buf("zzd%d" % j).take(), inc=sv)
                ZZ2 = sv.v
                buf("z2t%d" % parz).used(sv, ZZ2)
                DMA("scalar", z2_d[ec, :, sbi * 512:(sbi + 1) * 512], z2t[parz][:],
                    [(sv, Z2D)], d_z2w[parz])
                buf("z2t%d" % parz).used(d_z2w[parz], d_z2w[parz].v)
                stat_deps.append((j, ZB2, ZZ2))

            def _dstat(stat_deps=stat_deps, sbi=sbi, eh=eh):
                for (j, ZB2, ZZ2) in stat_deps:
                    MM(pSt[0], c_oneskb[:], z2b[j][:],
                       [(sc, ZB2), CONST] + (buf(pSt_names[0]).take() if j == 0 else []),
                       j == 0, j == ECW - 1)
                    buf("z2b%d" % j).used(st, st.v)
                    MM(pSt[1], c_oneskb[:], zzd[j][:],
                       [(sv, ZZ2), CONST] + (buf(pSt_names[1]).take() if j == 0 else []),
                       j == 0, j == ECW - 1)
                    buf("zzd%d" % j).used(st, st.v)
                STT = st.v
                ssl = slice(sbi * 512, (sbi + 1) * 512)
                P.op("vector", (lambda e, ssl=ssl: e.tensor_add(ssum[:, ssl], ssum[:, ssl],
                                                                pSt[0])),
                     deps=[(st, STT)], inc=sv)
                buf(pSt_names[0]).used(sv, sv.v)
                P.op("vector", (lambda e, ssl=ssl: e.tensor_add(ssq[:, ssl], ssq[:, ssl],
                                                                pSt[1])),
                     deps=[(st, STT)], inc=sv)
                buf(pSt_names[1]).used(sv, sv.v)
                SACC = sv.v
                if eh == NEH - 1:
                    defer_ln2[0] = (lambda: _ln2_block(ssl, sbi, SACC))
            pend_dstat[0] = _dstat

    if pend_dstat[0] is not None:
        pend_dstat[0]()
        pend_dstat[0] = None


    if defer_ln2[0] is not None:
        defer_ln2[0]()
        defer_ln2[0] = None

    P.wait("sync", [(d_out[0], d_out[0].v), (d_out[1], d_out[1].v),
                    (d_z2w[0], d_z2w[0].v), (d_z2w[1], d_z2w[1].v), (d_z2r[0], d_z2r[0].v),
                    (d_z2r[1], d_z2r[1].v), (d_x1r[0], d_x1r[0].v),
                    (d_x1r[1], d_x1r[1].v), (d_w2r, d_w2r.v), (d_dbg, d_dbg.v)]
            + [(d_hs[i], d_hs[i].v) for i in range(4)])

    with nc.Block() as block:
        block.sync(lambda e: [f(e) for f in P.q["sync"]])
        block.gpsimd(lambda e: [f(e) for f in P.q["gpsimd"]])
        block.vector(lambda e: [f(e) for f in P.q["vector"]])
        block.scalar(lambda e: [f(e) for f in P.q["scalar"]])
        block.tensor(lambda e: [f(e) for f in P.q["tensor"]])

    esD.close()
    return nc



def make_inputs_for_core(cfg, xb, wq, bq_, wk, bk_, wv, bv_, wo, bo_, w1, b1_, w2, b2_):
    D, S = cfg.D, cfg.S
    DC, FC = cfg.DC, cfg.FC
    perm = np.concatenate([h * 128 + np.concatenate([np.arange(0, 128, 2),
                                                     np.arange(1, 128, 2)])
                           for h in range(cfg.H)])
    m = {}
    m["xT"] = np.ascontiguousarray(xb.T)
    m["wqT"] = np.ascontiguousarray(wq.T[:, perm])
    m["wkT"] = np.ascontiguousarray(wk.T[:, perm])
    m["wvT"] = np.ascontiguousarray(wv.T)
    m["woT"] = np.ascontiguousarray(wo.T)
    m["bq"] = np.ascontiguousarray(bq_[perm].reshape(DC, 128).T)
    m["bk"] = np.ascontiguousarray(bk_[perm].reshape(DC, 128).T)
    m["bvr"] = np.ascontiguousarray(bv_.reshape(1, D))
    m["bo"] = np.ascontiguousarray(bo_.reshape(DC, 128).T)
    m["w1T"] = np.ascontiguousarray(w1.T)
    m["b1"] = np.ascontiguousarray(b1_.reshape(FC, 128).T)
    m["w2T"] = np.ascontiguousarray(w2.T)
    m["b2"] = np.ascontiguousarray(b2_.reshape(DC, 128).T)
    m["srow"] = np.arange(S, dtype=np.float32).reshape(1, S)
    m["icol"] = np.arange(64, dtype=np.float32).reshape(64, 1)
    m["ones1"] = np.ones((1, 128), np.float32)
    m["onesk"] = np.ones((128, 1), np.float32)
    m["oneskb"] = np.ones((128, 1), ml_dtypes.bfloat16)
    m["ones1b"] = np.ones((1, 128), ml_dtypes.bfloat16)
    m["ones1sc"] = np.full((1, 128), 2.0 ** 20, ml_dtypes.bfloat16)
    i_ = np.arange(128)[:, None]
    j_ = np.arange(512)[None, :]
    masks = np.stack([(i_ + mi * 128 <= j_) for mi in range(4)], axis=1)
    m["masks"] = masks.astype(ml_dtypes.bfloat16)
    return m


def kernel(L0_attn_x, L0_attn_Wq, L0_attn_bq, L0_attn_Wk, L0_attn_bk,
           L0_attn_Wv, L0_attn_bv, L0_attn_Wo, L0_attn_bo,
           L0_ffn_w1, L0_ffn_b1, L0_ffn_w2, L0_ffn_b2):
    cfg = Cfg(D=2048, S=2048, DFF=8192)
    nc = build(cfg)
    args = [np.asarray(a) for a in
            (L0_attn_Wq, L0_attn_bq, L0_attn_Wk, L0_attn_bk, L0_attn_Wv,
             L0_attn_bv, L0_attn_Wo, L0_attn_bo, L0_ffn_w1, L0_ffn_b1,
             L0_ffn_w2, L0_ffn_b2)]
    x = np.asarray(L0_attn_x)
    in_maps = [make_inputs_for_core(cfg, x[b], *args) for b in range(NCORES)]
    res = run_bass_kernel_spmd(nc, in_maps, core_ids=list(range(NCORES)))
    out = np.empty((NCORES, cfg.S, cfg.D), np.float32)
    for b in range(NCORES):
        out[b] = res.results[b]["outT"].T
    return out


# version 24
# speedup vs baseline: 1.0067x; 1.0067x over previous
import math
from contextlib import ExitStack

import numpy as np
import ml_dtypes
import concourse.bass as bass
import concourse.mybir as mybir
from concourse.bass_utils import run_bass_kernel_spmd

F32 = mybir.dt.float32
BF16 = mybir.dt.bfloat16
I32 = mybir.dt.int32
AX = mybir.AxisListType.X
ALU = mybir.AluOpType
ACT = mybir.ActivationFunctionType

NCORES = 8


def _hilo(x):
    hi = np.float32(x)
    return float(hi), float(x - float(hi))


class Cfg:
    def __init__(self, D=2048, S=2048, DFF=8192):
        self.D, self.S, self.DFF = D, S, DFF
        self.DC = D // 128
        self.H = self.DC
        self.SB = S // 512
        self.ST = S // 128
        self.FC = DFF // 128
        self.ECW = max(1, min(4, self.DC // 2))
        assert D % 256 == 0 and S % 512 == 0 and DFF % 128 == 0


class Sem:
    __slots__ = ("h", "v")
    def __init__(self, h):
        self.h = h
        self.v = 0


class Buf:
    __slots__ = ("cons",)
    def __init__(self):
        self.cons = []
    def used(self, sem, val):
        self.cons.append((sem, val))
    def take(self):
        c = self.cons
        self.cons = []
        return c


class Prog:
    ENGINES = ("sync", "gpsimd", "vector", "scalar", "tensor")

    def __init__(self, nc, es):
        self.nc = nc
        self.es = es
        self.q = {e: [] for e in self.ENGINES}
        self.waited = {e: {} for e in self.ENGINES}
        self.st = self.sem("c_tensor")
        self.sv = self.sem("c_vector")
        self.sc = self.sem("c_scalar")
        self.sg = self.sem("c_gpsimd")

    def sem(self, name):
        return Sem(self.es.enter_context(self.nc.semaphore(name)))

    def op(self, eng, fn, deps=(), inc=None, n=1):
        w = self.waited[eng]
        own = {"tensor": self.st, "vector": self.sv,
               "scalar": self.sc, "gpsimd": self.sg}.get(eng)
        dl = []
        for s, v in deps:
            if own is not None and s is own and v > own.v:
                raise AssertionError(
                    "emission inversion on %s: dep %d > current %d" % (eng, v, own.v))
            if v and v > w.get(s, 0):
                dl.append((s.h, v))
                w[s] = v
        if inc is not None:
            inc.v += n
        def run(e, fn=fn, dl=tuple(dl), ih=(inc.h if inc is not None else None), n=n):
            for h, v in dl:
                e.wait_ge(h, v)
            ins = fn(e)
            if ih is not None:
                ins.then_inc(ih, n)
        self.q[eng].append(run)
        return inc.v if inc is not None else None

    def wait(self, eng, deps):
        w = self.waited[eng]
        dl = []
        for s, v in deps:
            if v and v > w.get(s, 0):
                dl.append((s.h, v))
                w[s] = v
        if dl:
            def run(e, dl=tuple(dl)):
                for h, v in dl:
                    e.wait_ge(h, v)
            self.q[eng].append(run)

    def barrier(self, extra=()):
        deps = [(c, c.v) for c in (self.st, self.sv, self.sc, self.sg)] + list(extra)
        for e in self.ENGINES:
            self.wait(e, deps)


def build(cfg, debug=False):
    D, S, DFF = cfg.D, cfg.S, cfg.DFF
    DC, H, SB, ST, FC, ECW = cfg.DC, cfg.H, cfg.SB, cfg.ST, cfg.FC, cfg.ECW

    nc = bass.Bass("TRN2", target_bir_lowering=False)
    es = ExitStack()
    P = Prog(nc, es)
    st, sv, sc, sg = P.st, P.sv, P.sc, P.sg

    def din(name, shape, dt=F32):
        return nc.dram_tensor(name, shape, dt, kind="ExternalInput").ap()

    xT = din("xT", [D, S])
    wqT = din("wqT", [D, D]); wkT = din("wkT", [D, D])
    wvT = din("wvT", [D, D]); woT = din("woT", [D, D])
    bq_i = din("bq", [128, DC]); bk_i = din("bk", [128, DC])
    bvr_i = din("bvr", [1, D]); bo_i = din("bo", [128, DC])
    w1T = din("w1T", [D, DFF]); b1_i = din("b1", [128, FC])
    w2T = din("w2T", [DFF, D]); b2_i = din("b2", [128, DC])
    srow_i = din("srow", [1, S])
    icol_i = din("icol", [64, 1])
    ones1_i = din("ones1", [1, 128])
    onesk_i = din("onesk", [128, 1])
    oneskb_i = nc.dram_tensor("oneskb", [128, 1], BF16, kind="ExternalInput").ap()
    ones1b_i = nc.dram_tensor("ones1b", [1, 128], BF16, kind="ExternalInput").ap()
    ones1sc_i = nc.dram_tensor("ones1sc", [1, 128], BF16, kind="ExternalInput").ap()
    masks_i = din("masks", [128, 4, 512], BF16)

    outT = nc.dram_tensor("outT", [D, S], F32, kind="ExternalOutput").ap()
    dbg = {}
    if debug:
        def dout(name, shape, dt=F32):
            dbg[name] = nc.dram_tensor(name, shape, dt, kind="ExternalOutput").ap()
        dout("dbg_vtm", [ST, 128, D], BF16)
        dout("dbg_q0", [128, S], BF16); dout("dbg_k0", [128, S], BF16)
        dout("dbg_aa1", [128, 1]); dout("dbg_aa2", [128, 1])
        dout("dbg_att0", [128, S], BF16)
        dout("dbg_x1", [DC, 128, S])
        dout("dbg_cc", [128, S]); dout("dbg_ss", [128, S])
        dout("dbg_t1", [1, 1]); dout("dbg_t2", [1, 1])

    v_d = nc.dram_tensor("v_d", [ST, 128, D], BF16).ap()
    attb_d = nc.dram_tensor("attb_d", [H, 128, S], BF16).ap()
    x1_d = nc.dram_tensor("x1_d", [DC, 128, S], F32).ap()
    h_d = nc.dram_tensor("h_d", [FC, 128, S], BF16).ap()
    w2q_d = nc.dram_tensor("w2q_d", [FC, 128, D], BF16).ap()
    z2_d = nc.dram_tensor("z2_d", [DC, 128, S], F32).ap()

    d_const = P.sem("d_const")
    d_x = P.sem("d_x")
    d_wv = P.sem("d_wv")
    d_vw = [P.sem("d_vw0"), P.sem("d_vw1")]
    d_vh = [P.sem("d_vh0"), P.sem("d_vh1")]
    d_wq = P.sem("d_wq"); d_wk = P.sem("d_wk")
    d_wo = [P.sem("d_wo0"), P.sem("d_wo1")]
    d_swq = P.sem("d_swq"); d_swk = P.sem("d_swk")
    d_trig = P.sem("d_trig")
    d_trigc = P.sem("d_trigc")
    d_cbf = P.sem("d_cbf")
    d_ab = [P.sem("d_ab%d" % i) for i in range(4)]
    d_atw = [P.sem("d_atw0"), P.sem("d_atw1")]
    d_ats = [P.sem("d_ats0"), P.sem("d_ats1")]
    d_xr = [P.sem("d_xr0"), P.sem("d_xr1")]
    d_x1w = [P.sem("d_x1w0"), P.sem("d_x1w1")]
    d_w1 = [P.sem("d_w10"), P.sem("d_w11")]
    d_w2 = [P.sem("d_w20"), P.sem("d_w21")]
    d_hw = [P.sem("d_hw0"), P.sem("d_hw1")]
    d_w2qw = [P.sem("d_w2qw0"), P.sem("d_w2qw1")]
    d_w2r = P.sem("d_w2r")
    d_hs = [P.sem("d_hs%d" % i) for i in range(4)]
    d_x1r = [P.sem("d_x1r%d" % i) for i in range(4)]
    d_z2w = [P.sem("d_z2w0"), P.sem("d_z2w1")]
    d_z2r = [P.sem("d_z2r0"), P.sem("d_z2r1")]
    d_out = [P.sem("d_out0"), P.sem("d_out1")]
    d_dbg = P.sem("d_dbg")

    C2PI_HI, C2PI_LO = _hilo(2.0 * math.pi)
    CPI2_HI, CPI2_LO = _hilo(math.pi / 2.0)
    INV2PI = float(np.float32(1.0 / (2.0 * math.pi)))
    SC_HI, SC_LO = _hilo(1.0 / math.sqrt(128.0))
    LN4_HI, LN4_LO = _hilo(math.log(10000.0) / 64.0)
    PI_F = float(np.float32(math.pi))
    TQ_HI, TQ_LO = _hilo(0.7 / (D * DFF))
    EPS = 1e-5

    def DMA(eng, out_ap, in_ap, deps, inc):
        return P.op(eng, (lambda e, o=out_ap, i=in_ap: e.dma_start(out=o, in_=i)),
                    deps=deps, inc=inc, n=16)

    def MM(out_ap, lhsT_ap, rhs_ap, deps, start, stop):
        return P.op("tensor", (lambda e, o=out_ap, l=lhsT_ap, r=rhs_ap, a=start, b=stop:
                               e.matmul(o, l, r, start=a, stop=b, skip_group_check=True)),
                    deps=deps, inc=st)

    sb = nc.alloc_sbuf_tensor
    c_ones1 = sb("c_ones1", [1, 128], F32)
    c_onesk = sb("c_onesk", [128, 1], F32)
    c_oneskb = sb("c_oneskb", [128, 1], BF16)
    c_ones1b = sb("c_ones1b", [1, 128], BF16)
    c_ones1sc = sb("c_ones1sc", [1, 128], BF16)
    c_bq = sb("c_bq", [128, DC], F32); c_bk = sb("c_bk", [128, DC], F32)
    c_bo = sb("c_bo", [128, DC], F32); c_b2 = sb("c_b2", [128, DC], F32)
    c_b1 = sb("c_b1", [128, FC], F32)
    t_t1 = sb("t_t1", [128, 1], F32); t_t1n = sb("t_t1n", [128, 1], F32)
    t_t2 = sb("t_t2", [128, 1], F32); t_t2n = sb("t_t2n", [128, 1], F32)
    absac = [sb("absac1", [128, 1], F32), sb("absac2", [128, 1], F32)]
    absred = sb("absred", [128, 1], F32)
    tm11 = [sb("tm11a", [1, 1], F32), sb("tm11b", [1, 1], F32)]

    B = {}
    def buf(name):
        if name not in B:
            B[name] = Buf()
        return B[name]

    nconst = 0

    esA = ExitStack()
    sbA = lambda n, s_, d: esA.enter_context(nc.sbuf_tensor(n, s_, d))
    psA = lambda n, s_: esA.enter_context(nc.psum_tensor(n, s_, F32))
    x_bf = sbA("x_bf", [128, DC, S], BF16)
    c_masks = sbA("c_masks", [128, 4, 512], BF16)
    c_cc2 = sbA("c_cc2", [128, S], F32)
    c_ss2 = sbA("c_ss2", [128, S], F32)
    c_cc2k = sbA("c_cc2k", [128, S], F32)
    c_ss2k = sbA("c_ss2k", [128, S], F32)
    c_bvr = sbA("c_bvr", [1, D], BF16)
    vh_s = [sbA("vh_s%d" % i, [128, ST, 128], BF16) for i in range(2)]
    abuf = [sbA("abuf%d" % i, [128, 512], F32) for i in range(4)]
    absscr = sbA("absscr", [128, 512], F32)
    absred2 = sbA("absred2", [128, 1], F32)
    wq_s = sbA("wq_s", [128, DC, 128], BF16)
    wk_s = sbA("wk_s", [128, DC, 128], BF16)
    esT = ExitStack()
    sbT = lambda n, s_, d: esT.enter_context(nc.sbuf_tensor(n, s_, d))
    c_srow = sbT("c_srow", [1, 512], F32)
    c_icol = sbT("c_icol", [64, 1], F32)
    g_th = sbT("g_th", [64, 512], F32)
    g_a = sbT("g_a", [64, 512], F32)
    g_b = sbT("g_b", [64, 512], F32)
    g_ki = sbT("g_ki", [64, 512], I32)
    g_f = sbT("g_f", [64, 1], F32)
    g_sin = sbT("g_sin", [64, 512], F32)
    esV = ExitStack()
    sbV = lambda n, s_, d: esV.enter_context(nc.sbuf_tensor(n, s_, d))
    vstage = [sbV("vstage%d" % i, [128, 512], BF16) for i in range(2)]
    EW = min(512, D)
    EH = D // EW
    wv_s = sbV("wv_s", [128, DC, EW], BF16)

    pA = [psA("pA%d" % i, [128, 512]) for i in range(2)]
    pS = [psA("pS%d" % i, [128, 512]) for i in range(2)]
    pV = [psA("pV%d" % i, [128, 512]) for i in range(2)]
    pB = psA("pB", [128, 512])
    pT = psA("pT", [1, 512])
    pG = pV[1]

    for o, i in ((c_ones1[:], ones1_i), (c_onesk[:], onesk_i), (c_oneskb[:], oneskb_i),
                 (c_ones1b[:], ones1b_i), (c_ones1sc[:], ones1sc_i),
                 (c_masks[:], masks_i),
                 (c_bq[:], bq_i), (c_bk[:], bk_i), (c_bo[:], bo_i), (c_b2[:], b2_i),
                 (c_b1[:], b1_i), (c_icol[:], icol_i)):
        DMA("sync", o, i, (), d_const)
        nconst += 16
    CONST = (d_const, nconst)
    DMA("gpsimd", c_bvr[:], bvr_i, (), d_cbf)
    CBVR = (d_cbf, 16)
    d_srow = P.sem("d_srow")

    P.op("vector", lambda e: e.tensor_scalar(g_f[:], c_icol[:], -LN4_HI, None, ALU.mult),
         deps=[CONST] + buf("g_f").take(), inc=sv)
    P.op("vector", lambda e: e.scalar_tensor_tensor(g_f[:], c_icol[:], -LN4_LO, g_f[:],
                                                    ALU.mult, ALU.add),
         deps=[(sv, sv.v), CONST], inc=sv)
    P.op("scalar", lambda e: e.activation(g_f[:], g_f[:], ACT.Exp),
         deps=[(sv, sv.v)], inc=sc)
    FREQ = sc.v

    n_trig = 0
    for cki in range(S // 512):
        sl = slice(cki * 512, cki * 512 + 512)
        DMA("sync", c_srow[:], srow_i[:, sl], buf("c_srow").take(), d_srow)
        MM(pG[0:64, :], c_ones1[:, 0:64], c_srow[:],
           [CONST, (d_srow, d_srow.v)] + buf("pV1").take(), True, True)
        buf("c_srow").used(st, st.v)
        P.op("vector", lambda e: e.tensor_scalar(g_th[:], pG[0:64, :], g_f[:], None, ALU.mult),
             deps=[(st, st.v), (sc, FREQ)] + buf("g_th").take(), inc=sv)
        buf("pV1").used(sv, sv.v)
        TH = sv.v
        for half, target in ((0, "sin"), (1, "cos")):
            if half == 0:
                P.op("vector", lambda e: e.tensor_scalar(g_a[:], g_th[:], INV2PI,
                                                         None, ALU.mult),
                     deps=[(sv, TH)] + buf("g_a").take(), inc=sv)
            else:
                P.op("vector", lambda e: e.tensor_scalar(g_b[:], g_th[:], CPI2_HI, CPI2_LO,
                                                         ALU.add, ALU.add),
                     deps=[(sv, TH)] + buf("g_b").take(), inc=sv)
                P.op("vector", lambda e: e.tensor_scalar(g_a[:], g_b[:], INV2PI,
                                                         None, ALU.mult),
                     deps=[(sv, sv.v)] + buf("g_a").take(), inc=sv)
            P.op("vector", lambda e: e.tensor_copy(g_ki[:], g_a[:]), deps=[(sv, sv.v)], inc=sv)
            P.op("vector", lambda e: e.tensor_copy(g_a[:], g_ki[:]), deps=[(sv, sv.v)], inc=sv)
            src = g_th if half == 0 else g_b
            P.op("vector", (lambda e, src=src: e.scalar_tensor_tensor(
                    g_b[:], g_a[:], -C2PI_HI, src[:], ALU.mult, ALU.add)),
                 deps=[(sv, sv.v)] + (buf("g_b").take() if half == 0 else []), inc=sv)
            P.op("vector", lambda e: e.scalar_tensor_tensor(g_b[:], g_a[:], -C2PI_LO, g_b[:],
                                                            ALU.mult, ALU.add),
                 deps=[(sv, sv.v)], inc=sv)
            P.op("vector", lambda e: e.tensor_scalar(g_a[:], g_b[:], PI_F, None, ALU.is_gt),
                 deps=[(sv, sv.v)], inc=sv)
            P.op("vector", lambda e: e.scalar_tensor_tensor(g_b[:], g_a[:], -C2PI_HI, g_b[:],
                                                            ALU.mult, ALU.add),
                 deps=[(sv, sv.v)], inc=sv)
            P.op("vector", lambda e: e.scalar_tensor_tensor(g_b[:], g_a[:], -C2PI_LO, g_b[:],
                                                            ALU.mult, ALU.add),
                 deps=[(sv, sv.v)], inc=sv)
            P.op("vector", lambda e: e.tensor_scalar(g_b[:], g_b[:], PI_F, -PI_F,
                                                     ALU.min, ALU.max),
                 deps=[(sv, sv.v)], inc=sv)
            buf("g_a").used(sv, sv.v)
            RED = sv.v
            if half == 0:
                P.op("scalar", lambda e: e.activation(g_sin[:], g_b[:], ACT.Sin),
                     deps=[(sv, RED)] + buf("g_sin").take(), inc=sc)
                buf("g_b").used(sc, sc.v)
                P.op("vector", (lambda e, sl=sl: e.tensor_scalar(
                        c_ss2[0:64, sl], g_sin[:], -1.0, None, ALU.mult)),
                     deps=[(sc, sc.v)], inc=sv)
                DMA("gpsimd", c_ss2[64:128, sl], g_sin[:], [(sc, sc.v)], d_trig)
                n_trig += 16
                buf("g_sin").used(d_trig, n_trig)
                buf("g_sin").used(sv, sv.v)
            else:
                P.op("scalar", (lambda e, sl=sl: e.activation(
                        c_cc2[0:64, sl], g_b[:], ACT.Sin)),
                     deps=[(sv, RED)], inc=sc)
                buf("g_b").used(sc, sc.v)
                DMA("gpsimd", c_cc2[64:128, sl], c_cc2[0:64, sl], [(sc, sc.v)], d_trigc)
        buf("g_th").used(sv, sv.v)
        buf("g_th").used(sc, sc.v)
    TRIG0 = (d_trig, n_trig)
    TRIG0C = (d_trigc, d_trigc.v)
    for (srct, dstt, dstn) in ((c_cc2, c_cc2k, "c_cc2k"), (c_ss2, c_ss2k, "c_ss2k")):
        P.op("vector", (lambda e, srct=srct, dstt=dstt: e.tensor_scalar(
                dstt[:], srct[:], SC_HI, None, ALU.mult)),
             deps=[TRIG0, TRIG0C, (sc, sc.v), (sv, sv.v)] + buf(dstn).take(), inc=sv)
        P.op("vector", (lambda e, srct=srct, dstt=dstt: e.scalar_tensor_tensor(
                dstt[:], srct[:], SC_LO, dstt[:], ALU.mult, ALU.add)),
             deps=[(sv, sv.v)], inc=sv)
    TRIG = (sv, sv.v)

    xTr = xT.rearrange("(c p) s -> p c s", p=128)
    for dc in range(DC):
        DMA("gpsimd", x_bf[:, dc, :], xTr[:, dc, :], (), d_x)
    XL = (d_x, 16 * DC)

    ab_tiles = []
    t1c = min(512, DFF)
    w1Tr = w1T.rearrange("(c p) f -> p c f", p=128)
    for dc in range(DC):
        for j in range(DFF // t1c):
            ab_tiles.append((0, w1Tr[:, dc, j * t1c:(j + 1) * t1c], t1c))
    t2c = min(512, D)
    w2Tr = w2T.rearrange("(c p) e -> p c e", p=128)
    for fc in range(FC):
        for j in range(D // t2c):
            ab_tiles.append((1, w2Tr[:, fc, j * t2c:(j + 1) * t2c], t2c))
    n_ab = len(ab_tiles)
    ab_i = [0]

    P.op("vector", lambda e: e.memset(absac[0][:], 0.0), deps=(), inc=sv)
    P.op("vector", lambda e: e.memset(absac[1][:], 0.0), deps=(), inc=sv)
    ABZ = sv.v
    absac_last = [ABZ, ABZ]

    def emit_absmean(k):
        for _ in range(k):
            i = ab_i[0]
            if i >= n_ab:
                return
            ab_i[0] = i + 1
            which, iap, tc = ab_tiles[i]
            par = i % 4
            DMA("sync", abuf[par][:, 0:tc], iap, buf("abuf%d" % par).take(), d_ab[par])
            nl = d_ab[par].v
            if par % 2 == 0:
                P.op("vector", (lambda e, par=par, tc=tc: e.reduce_sum(
                        absred[:], abuf[par][:, 0:tc], AX, apply_absolute_value=True)),
                     deps=[(d_ab[par], nl)] + buf("absred").take(), inc=sv)
                buf("abuf%d" % par).used(sv, sv.v)
                P.op("vector", (lambda e, which=which: e.tensor_add(
                        absac[which][:], absac[which][:], absred[:])),
                     deps=[(sv, sv.v), (sv, absac_last[which])], inc=sv)
                absac_last[which] = sv.v
                buf("absred").used(sv, sv.v)
            else:
                P.op("scalar", (lambda e, par=par, tc=tc: e.activation(
                        absscr[:, 0:tc], abuf[par][:, 0:tc], ACT.Abs,
                        accum_out=absred2[:])),
                     deps=[(d_ab[par], nl)] + buf("absred2").take() + buf("absscr").take(),
                     inc=sc)
                buf("abuf%d" % par).used(sc, sc.v)
                buf("absscr").used(sc, sc.v)
                P.op("vector", (lambda e, which=which: e.tensor_add(
                        absac[which][:], absac[which][:], absred2[:])),
                     deps=[(sc, sc.v), (sv, absac_last[which])], inc=sv)
                absac_last[which] = sv.v
                buf("absred2").used(sv, sv.v)

    wvTr = wvT.rearrange("(c p) e -> p c e", p=128)
    EBW = min(512, EW)
    for eh in range(EH):
        DMA("gpsimd", wv_s[:], wvTr[:, :, eh * EW:(eh + 1) * EW],
            buf("wv_s").take(), d_wv)
        WVL = d_wv.v
        for sti in range(ST):
            for ebi in range(EW // EBW):
                pvp = pS[ebi % 2]
                pb_ = buf("pS%d" % (ebi % 2))
                e0 = eh * EW + ebi * EBW
                MM(pvp[:, 0:EBW], c_ones1b[:, 0:128], c_bvr[:, e0:e0 + EBW],
                   [CONST, CBVR] + pb_.take(), True, False)
                for dc in range(DC):
                    MM(pvp[:, 0:EBW], x_bf[:, dc, sti * 128:(sti + 1) * 128],
                       wv_s[:, dc, ebi * EBW:(ebi + 1) * EBW],
                       [XL, (d_wv, WVL)], False, dc == DC - 1)
                buf("wv_s").used(st, st.v)
                buf("x_bf").used(st, st.v)
                pstg = (sti * (EW // EBW) + ebi) % 2
                P.op("scalar", (lambda e, p=pvp, pstg=pstg: e.activation(
                        vstage[pstg][:, 0:EBW], p[:, 0:EBW], ACT.Identity)),
                     deps=[(st, st.v)] + buf("vstage%d" % pstg).take(), inc=sc)
                pb_.used(sc, sc.v)
                DMA("scalar", v_d[sti, :, e0:e0 + EBW], vstage[pstg][:, 0:EBW],
                    [(sc, sc.v)], d_vw[pstg])
                buf("vstage%d" % pstg).used(d_vw[pstg], d_vw[pstg].v)
            emit_absmean(1)
    VDONE = [(d_vw[0], d_vw[0].v), (d_vw[1], d_vw[1].v)]
    esV.close()
    esT.close()
    qf = sbA("qf", [128, S], F32)
    kf = sbA("kf", [128, S], F32)
    qsw = sbA("qsw", [128, S], F32)
    rt1 = sbA("rt1", [128, S], F32)
    q_bf = [sbA("q_bf%d" % i, [128, S], BF16) for i in range(2)]
    k_bf = [sbA("k_bf%d" % i, [128, S], BF16) for i in range(2)]
    sqbs = [sbA("sqbs%d" % i, [128, 512], BF16) for i in range(2 * SB)]
    dsum = [sbA("dsum%d" % i, [128, 512], F32) for i in range(2)]
    dsumb = [sbA("dsumb%d" % i, [128, 512], BF16) for i in range(2)]
    negb2 = [sbA("negb2_%d" % i, [1, 1], F32) for i in range(2)]
    expb = [sbA("expb%d" % i, [128, 512], BF16) for i in range(4)]
    recb = sbA("recb", [128, 512], F32)
    attb_t = [sbA("attbt%d" % i, [128, 512], BF16) for i in range(2)]
    qmx = sbA("qmx", [1, 1], F32)
    kmx = sbA("kmx", [1, 1], F32)
    tmx = sbA("tmx", [1, 1], F32)
    nB128 = [sbA("nB128_%d" % i, [128, 1], F32) for i in range(2)]
    rec1b = [sbA("rec1b%d" % i, [1, 512], BF16) for i in range(2)]
    v_dr = v_d.rearrange("t p e -> p t e")
    if debug:
        DMA("scalar", dbg["dbg_vtm"], v_d, VDONE + [(d_dbg, d_dbg.v)], d_dbg)

    wqTr = wqT.rearrange("(c p) e -> p c e", p=128)
    wkTr = wkT.rearrange("(c p) e -> p c e", p=128)
    DMA("gpsimd", wq_s[:], wqTr[:, :, 0:128], (), d_wq)
    DMA("gpsimd", wk_s[:], wkTr[:, :, 0:128], (), d_wk)
    DMA("sync", vh_s[0][:], v_dr[:, :, 0:128], list(VDONE), d_vh[0])

    head_state = {}

    def emit_proj(h):
        for (wsem, wbuf_s, wbufn, bias_s, outf, outn) in (
                (d_wq, wq_s, "wq_s", c_bq, qf, "qf"),
                (d_wk, wk_s, "wk_s", c_bk, kf, "kf")):
            WL = 16 * (h + 1)
            for sbi in range(SB):
                par = sbi % 2
                pb_ = buf("pA%d" % par)
                war = pb_.take()
                for dc in range(DC):
                    MM(pA[par][:], wbuf_s[:, dc, :], x_bf[:, dc, sbi * 512:(sbi + 1) * 512],
                       [XL, (wsem, WL)] + (war if dc == 0 else []), dc == 0, dc == DC - 1)
                buf(wbufn).used(st, st.v)
                buf("x_bf").used(st, st.v)
                P.op("scalar", (lambda e, par=par, sbi=sbi, outf=outf, bias_s=bias_s, h=h:
                        e.activation(outf[:, sbi * 512:(sbi + 1) * 512], pA[par][:],
                                     ACT.Identity, bias=bias_s[:, h:h + 1])),
                     deps=[(st, st.v), CONST] + (buf(outn).take() if sbi == 0 else []), inc=sc)
                pb_.used(sc, sc.v)
                emit_absmean(4)
        QKEV = sc.v
        sqv = {}
        for si, (outf, outn) in enumerate(((qf, "qf"), (kf, "kf"))):
            for sbi in range(SB):
                j = si * SB + sbi
                sq_ = sqbs[j]
                sqn = "sqbs%d" % j
                P.op("vector", (lambda e, outf=outf, sbi=sbi, sq_=sq_: e.tensor_mul(
                        sq_[:], outf[:, sbi * 512:(sbi + 1) * 512],
                        outf[:, sbi * 512:(sbi + 1) * 512])),
                     deps=[(sc, QKEV)] + buf(sqn).take(), inc=sv)
                buf(outn).used(sv, sv.v)
                sqv[(si, sbi)] = (sqn, sq_, sv.v)
        head_state[h] = {"QKEV": QKEV, "sqv": sqv}

    def emit_stats(h):
        hp = h % 2
        sqv = head_state[h]["sqv"]
        for si, (mxb, mxn) in enumerate(((qmx, "qmx"), (kmx, "kmx"))):
            for sbi in range(SB):
                sp = (si * SB + sbi) % 2
                sqn, sq_, sqdone = sqv[(si, sbi)]
                pst = pT if sp == 0 else pB
                pstn = "pT" if sp == 0 else "pB"
                MM(pst[0:1, :], c_oneskb[:], sq_[:],
                   [(sv, sqdone), CONST] + buf(pstn).take(), True, True)
                buf(sqn).used(st, st.v)
                P.op("vector", (lambda e, pst=pst: e.reduce_max(tmx[:], pst[0:1, :], AX)),
                     deps=[(st, st.v)] + buf("tmx").take(), inc=sv)
                buf(pstn).used(sv, sv.v)
                if sbi == 0:
                    P.op("vector", (lambda e, mxb=mxb: e.tensor_copy(mxb[:], tmx[:])),
                         deps=[(sv, sv.v)] + buf(mxn).take(), inc=sv)
                else:
                    P.op("vector", (lambda e, mxb=mxb: e.tensor_max(mxb[:], mxb[:], tmx[:])),
                         deps=[(sv, sv.v)], inc=sv)
                buf("tmx").used(sv, sv.v)
        NRM = sv.v
        P.op("vector", lambda e: e.tensor_mul(tm11[0][:], qmx[:], kmx[:]),
             deps=[(sv, NRM)] + buf("tm0").take(), inc=sv)
        buf("qmx").used(sv, sv.v)
        buf("kmx").used(sv, sv.v)
        P.op("scalar", lambda e: e.activation(tm11[1][:], tm11[0][:], ACT.Sqrt,
                                              scale=float(1.02 * 1.02 / 128.0)),
             deps=[(sv, sv.v)] + buf("tm1").take(), inc=sc)
        buf("tm0").used(sc, sc.v)
        P.op("vector", (lambda e, hp=hp: e.tensor_scalar(negb2[hp][:], tm11[1][:],
                                                         -1.0, None, ALU.mult)),
             deps=[(sc, sc.v)] + buf("negb2_%d" % hp).take(), inc=sv)
        buf("tm1").used(sv, sv.v)
        NEGB = sv.v
        MM(pB[:, 0:1], c_ones1[:], negb2[hp][:], [(sv, NEGB), CONST] + buf("pB").take(),
           True, True)
        buf("negb2_%d" % hp).used(st, st.v)
        P.op("vector", (lambda e, hp=hp: e.tensor_copy(nB128[hp][:], pB[:, 0:1])),
             deps=[(st, st.v)] + buf("nB128_%d" % hp).take(), inc=sv)
        buf("pB").used(sv, sv.v)
        head_state[h]["NBB"] = sv.v

    def emit_rope(h):
        hp = h % 2
        QKEV = head_state[h]["QKEV"]
        war = buf("qsw").take()
        DMA("gpsimd", qsw[0:64, :], qf[64:128, :], [(sc, QKEV)] + war, d_swq)
        DMA("gpsimd", qsw[64:128, :], qf[0:64, :], [(sc, QKEV)] + war, d_swq)
        SWQ = d_swq.v
        buf("qf").used(d_swq, SWQ)
        P.op("vector", lambda e: e.tensor_mul(rt1[:], qf[:], c_cc2[:]),
             deps=[(sc, QKEV), TRIG, CONST] + buf("rt1").take(), inc=sv)
        buf("qf").used(sv, sv.v)
        P.op("vector", lambda e: e.tensor_mul(qsw[:], qsw[:], c_ss2[:]),
             deps=[(d_swq, SWQ), TRIG], inc=sv)
        P.op("vector", (lambda e, hp=hp: e.tensor_add(q_bf[hp][:], rt1[:], qsw[:])),
             deps=[(sv, sv.v)] + buf("q_bf%d" % hp).take(), inc=sv)
        buf("rt1").used(sv, sv.v)
        buf("qsw").used(sv, sv.v)
        QROPE = sv.v
        war = buf("qsw").take()
        DMA("gpsimd", qsw[0:64, :], kf[64:128, :], [(sc, QKEV), (sv, QROPE)] + war, d_swk)
        DMA("gpsimd", qsw[64:128, :], kf[0:64, :], [(sc, QKEV), (sv, QROPE)] + war, d_swk)
        SWK = d_swk.v
        buf("kf").used(d_swk, SWK)
        P.op("vector", lambda e: e.tensor_mul(rt1[:], kf[:], c_cc2k[:]),
             deps=[(sc, QKEV)] + buf("rt1").take(), inc=sv)
        buf("kf").used(sv, sv.v)
        P.op("vector", lambda e: e.tensor_mul(qsw[:], qsw[:], c_ss2k[:]),
             deps=[(d_swk, SWK)], inc=sv)
        P.op("vector", (lambda e, hp=hp: e.tensor_add(k_bf[hp][:], rt1[:], qsw[:])),
             deps=[(sv, sv.v)] + buf("k_bf%d" % hp).take(), inc=sv)
        buf("rt1").used(sv, sv.v)
        buf("qsw").used(sv, sv.v)
        head_state[h]["KROPE"] = sv.v
        if debug and h == 0:
            DMA("scalar", dbg["dbg_q0"], q_bf[0][:],
                [(sv, sv.v), (d_dbg, d_dbg.v)], d_dbg)
            DMA("scalar", dbg["dbg_k0"], k_bf[0][:],
                [(sv, sv.v), (d_dbg, d_dbg.v)], d_dbg)
            buf("q_bf0").used(d_dbg, d_dbg.v)
            buf("k_bf0").used(d_dbg, d_dbg.v)

    def emit_attn(h):
        hp = h % 2
        KROPE = head_state[h]["KROPE"]
        NBB = head_state[h]["NBB"]

        pend = None

        def emit_norm(p):
            qb0, PVD0, RCP0 = p
            MM(pB[:], c_ones1b[:], rec1b[qb0 % 2][:],
               [(sv, RCP0), CONST] + buf("pB").take(), True, True)
            buf("rec1b%d" % (qb0 % 2)).used(st, st.v)
            P.op("scalar", lambda e: e.activation(recb[:], pB[:], ACT.Identity),
                 deps=[(st, st.v)] + buf("recb").take(), inc=sc)
            buf("pB").used(sc, sc.v)
            RECB = sc.v
            pa = qb0 % 2
            ab_ = buf("attbt%d" % pa)
            P.op("vector", (lambda e, pa=pa, pv_=pV[qb0 % 2]: e.tensor_mul(
                    attb_t[pa][:], pv_[:], recb[:])),
                 deps=[(sc, RECB), (st, PVD0)] + ab_.take(), inc=sv)
            buf("pV%d" % (qb0 % 2)).used(sv, sv.v)
            buf("recb").used(sv, sv.v)
            DMA("scalar", attb_d[h, :, qb0 * 512:(qb0 + 1) * 512], attb_t[pa][:],
                [(sv, sv.v)], d_atw[pa])
            ab_.used(d_atw[pa], d_atw[pa].v)

        for qb in range(SB):
            kc_max = ((qb + 1) * 512) // 128
            pvp = qb % 2
            pv_war = buf("pV%d" % pvp).take()
            pT_war = buf("pT").take()
            pv_pend = None
            ds_last = [None, None]
            ds_cast = [None, None]
            sc_banks = (pS[0], pS[1], pA[0], pA[1])
            sc_names = ("pS0", "pS1", "pA0", "pA1")
            for kc in range(kc_max + 1):
                if kc < kc_max:
                    par = kc % 4
                    pb_ = buf(sc_names[par])
                    eb_ = buf("expb%d" % par)
                    MM(sc_banks[par][:], k_bf[hp][:, kc * 128:(kc + 1) * 128],
                       q_bf[hp][:, qb * 512:(qb + 1) * 512],
                       [(sv, KROPE)] + pb_.take(), True, True)
                    SCO = st.v
                    buf("k_bf%d" % hp).used(st, SCO)
                    buf("q_bf%d" % hp).used(st, SCO)
                    P.op("scalar", (lambda e, par=par, hp=hp: e.activation(
                            expb[par][:], sc_banks[par][:], ACT.Exp,
                            bias=nB128[hp][:], scale=1.0)),
                         deps=[(st, SCO), (sv, NBB)] + eb_.take(), inc=sc)
                    pb_.used(sc, sc.v)
                    buf("nB128_%d" % hp).used(sc, sc.v)
                    dep_exp = (sc, sc.v)
                    off = kc * 128 - qb * 512
                    if off >= 0:
                        mi = off // 128
                        P.op("vector", (lambda e, par=par, mi=mi: e.tensor_mul(
                                expb[par][:], expb[par][:], c_masks[:, mi, :])),
                             deps=[dep_exp, CONST], inc=sv)
                        dep_exp = (sv, sv.v)
                if pv_pend is not None:
                    kp, dep_p = pv_pend
                    parp = kp % 4
                    cp = kp % 2
                    MM(pV[pvp][:], vh_s[hp][:, kp, :], expb[parp][:],
                       [dep_p, (d_vh[hp], 16 * (h // 2 + 1))] +
                       (pv_war if kp == 0 else []),
                       kp == 0, kp == kc_max - 1)
                    buf("expb%d" % parp).used(st, st.v)
                    buf("vh_s%d" % hp).used(st, st.v)
                    if ds_last[cp] is None:
                        P.op("vector", (lambda e, parp=parp, cp=cp: e.tensor_copy(
                                dsum[cp][:], expb[parp][:])),
                             deps=[dep_p] + buf("dsum%d" % cp).take(), inc=sv)
                    else:
                        P.op("vector", (lambda e, parp=parp, cp=cp: e.tensor_add(
                                dsum[cp][:], dsum[cp][:], expb[parp][:])),
                             deps=[dep_p, (sv, ds_last[cp])], inc=sv)
                    ds_last[cp] = sv.v
                    buf("expb%d" % parp).used(sv, sv.v)
                    if kp == (kc_max - 1 if (kc_max - 1) % 2 == cp else kc_max - 2):
                        P.op("vector", (lambda e, cp=cp: e.tensor_copy(
                                dsumb[cp][:], dsum[cp][:])),
                             deps=[(sv, ds_last[cp])] + buf("dsumb%d" % cp).take(), inc=sv)
                        buf("dsum%d" % cp).used(sv, sv.v)
                        ds_cast[cp] = sv.v
                if kc < kc_max:
                    pv_pend = (kc, dep_exp)
            PVD = st.v
            MM(pT[0:1, :], c_oneskb[:], dsumb[0][:],
               [(sv, ds_cast[0]), CONST] + pT_war, True, False)
            buf("dsumb0").used(st, st.v)
            MM(pT[0:1, :], c_oneskb[:], dsumb[1][:],
               [(sv, ds_cast[1]), CONST], False, True)
            buf("dsumb1").used(st, st.v)
            DEN = st.v
            def _recip(e, rb=rec1b[qb % 2]):
                with nc.allow_low_precision(reason="bf16 softmax scale, within tolerance"):
                    return e.reciprocal(rb[:], pT[0:1, :])
            P.op("vector", _recip,
                 deps=[(st, DEN)] + buf("rec1b%d" % (qb % 2)).take(), inc=sv)
            buf("pT").used(sv, sv.v)
            RCP = sv.v
            if pend is not None:
                emit_norm(pend)
            pend = (qb, PVD, RCP)
        emit_norm(pend)
        if debug and h == 0:
            DMA("scalar", dbg["dbg_att0"], attb_d[0],
                [(d_atw[0], d_atw[0].v), (d_atw[1], d_atw[1].v), (d_dbg, d_dbg.v)], d_dbg)

    def emit_prefetch(hn):
        if hn >= H:
            return
        DMA("gpsimd", wq_s[:], wqTr[:, :, hn * 128:(hn + 1) * 128],
            buf("wq_s").take(), d_wq)
        DMA("gpsimd", wk_s[:], wkTr[:, :, hn * 128:(hn + 1) * 128],
            buf("wk_s").take(), d_wk)
        hpn = hn % 2
        DMA("sync", vh_s[hpn][:], v_dr[:, :, hn * 128:(hn + 1) * 128],
            list(VDONE) + buf("vh_s%d" % hpn).take(), d_vh[hpn])

    for h in range(H):
        emit_proj(h)
        emit_rope(h)
        if h >= 1:
            emit_attn(h - 1)
        emit_stats(h)
        emit_prefetch(h + 1)
    emit_attn(H - 1)

    emit_absmean(n_ab)
    if debug:
        DMA("scalar", dbg["dbg_aa1"], absac[0][:], [(sv, sv.v), (d_dbg, d_dbg.v)], d_dbg)
        DMA("scalar", dbg["dbg_aa2"], absac[1][:], [(sv, sv.v), (d_dbg, d_dbg.v)], d_dbg)
    for w_i, (thi, tlo, tpos, tneg) in enumerate(((TQ_HI, TQ_LO, t_t1, t_t1n),
                                                  (TQ_HI, TQ_LO, t_t2, t_t2n))):
        MM(pT[:, 0:1], c_onesk[:], absac[w_i][:],
           [(sv, sv.v), CONST] + buf("pT").take(), True, True)
        buf("absac%d" % w_i).used(st, st.v)
        P.op("vector", (lambda e, thi=thi: e.tensor_scalar(
                tm11[0][:], pT[0:1, 0:1], thi, None, ALU.mult)),
             deps=[(st, st.v)] + buf("tm0").take(), inc=sv)
        P.op("vector", (lambda e, tlo=tlo: e.scalar_tensor_tensor(
                tm11[0][:], pT[0:1, 0:1], tlo, tm11[0][:], ALU.mult, ALU.add)),
             deps=[(sv, sv.v)], inc=sv)
        buf("pT").used(sv, sv.v)
        MM(pB[:, 0:1], c_ones1[:], tm11[0][:], [(sv, sv.v), CONST] + buf("pB").take(),
           True, True)
        buf("tm0").used(st, st.v)
        P.op("vector", (lambda e, tpos=tpos: e.tensor_copy(tpos[:], pB[:, 0:1])),
             deps=[(st, st.v)], inc=sv)
        buf("pB").used(sv, sv.v)
        P.op("vector", (lambda e, tpos=tpos, tneg=tneg: e.tensor_scalar(
                tneg[:], tpos[:], -1.0, None, ALU.mult)),
             deps=[(sv, sv.v)], inc=sv)
        if debug:
            DMA("scalar", dbg["dbg_t1" if w_i == 0 else "dbg_t2"], tm11[0][:],
                [(sv, sv.v), (d_dbg, d_dbg.v)], d_dbg)
            buf("tm0").used(d_dbg, d_dbg.v)
    THRD = sv.v
    if debug:
        DMA("scalar", dbg["dbg_cc"], c_cc2[:], [TRIG, (d_dbg, d_dbg.v)], d_dbg)
        DMA("scalar", dbg["dbg_ss"], c_ss2[:], [TRIG, (d_dbg, d_dbg.v)], d_dbg)

    P.barrier(extra=[(d_ab[2], d_ab[2].v), (d_ab[3], d_ab[3].v),
                     (d_atw[0], d_atw[0].v), (d_atw[1], d_atw[1].v), (d_x, d_x.v), (d_wv, d_wv.v),
                     (d_wq, d_wq.v), (d_wk, d_wk.v), (d_swq, d_swq.v),
                     (d_swk, d_swk.v), (d_trig, d_trig.v), (d_trigc, d_trigc.v),
                     (d_srow, d_srow.v),
                     (d_cbf, d_cbf.v),
                     (d_vw[0], d_vw[0].v), (d_vw[1], d_vw[1].v),
                     (d_vh[0], d_vh[0].v), (d_vh[1], d_vh[1].v),
                     (d_ab[0], d_ab[0].v), (d_ab[1], d_ab[1].v), (d_dbg, d_dbg.v)])
    esA.close()
    for n in list(B.keys()):
        if n not in ("x1bf",):
            B.pop(n, None)

    esBC = ExitStack()
    x1bf = esBC.enter_context(nc.sbuf_tensor("x1bf", [128, DC, S], BF16))
    esB = ExitStack()
    sbB = lambda n, s_, d: esB.enter_context(nc.sbuf_tensor(n, s_, d))
    psB = lambda n, s_: esB.enter_context(nc.psum_tensor(n, s_, F32))
    wo_s = [sbB("wo_s%d" % i, [128, DC, 128], BF16) for i in range(2)]
    att_s = [sbB("att_s%d" % i, [128, H, 512], BF16) for i in range(2)]
    xf_s = [sbB("xf_s%d" % i, [128, 512], F32) for i in range(2)]
    z_sb = sbB("z_sb", [128, DC, 512], F32)
    zb_b = [sbB("zb_b%d" % i, [128, 512], BF16) for i in range(2)]
    zz_b = [sbB("zz_b%d" % i, [128, 512], BF16) for i in range(2)]
    arow = sbB("arow", [1, 512], F32)
    brow = sbB("brow", [1, 512], F32)
    r0r = sbB("r0r", [1, 512], F32)
    r2r = sbB("r2r", [1, 512], F32)
    aB = sbB("aB", [128, 512], F32)
    bB = sbB("bB", [128, 512], F32)
    x1f = [sbB("x1f%d" % i, [128, 512], F32) for i in range(2)]
    pA2 = [psB("pA2%d" % i, [128, 512]) for i in range(2)]
    pT1 = psB("pT1", [1, 512])
    pT2 = psB("pT2", [1, 512])
    pBa = psB("pBa", [128, 512])
    pBb = psB("pBb", [128, 512])

    woTr = woT.rearrange("(c p) e -> p c e", p=128)
    attb_r = attb_d.rearrange("h p s -> p h s")

    def emit_ln1_tail(sbi, STATS):
        P.op("scalar", lambda e: e.activation(arow[:], pT1[0:1, :], ACT.Identity,
                                              scale=1.0 / D),
             deps=[(st, STATS)] + buf("arow").take(), inc=sc)
        P.op("scalar", lambda e: e.activation(brow[:], pT2[0:1, :], ACT.Identity,
                                              scale=1.0 / D),
             deps=[(st, STATS)] + buf("brow").take(), inc=sc)
        buf("pT1").used(sc, sc.v)
        buf("pT2").used(sc, sc.v)
        MS = sc.v
        P.op("vector", lambda e: e.tensor_mul(r2r[:], arow[:], arow[:]),
             deps=[(sc, MS)] + buf("r2r").take(), inc=sv)
        P.op("vector", lambda e: e.tensor_sub(r0r[:], brow[:], r2r[:]),
             deps=[(sv, sv.v)] + buf("r0r").take(), inc=sv)
        buf("brow").used(sv, sv.v)
        P.op("vector", lambda e: e.tensor_scalar(r0r[:], r0r[:], 1.0, EPS, ALU.mult, ALU.add),
             deps=[(sv, sv.v)], inc=sv)
        VPE = sv.v
        P.op("scalar", lambda e: e.activation(r2r[:], r0r[:], ACT.Sqrt),
             deps=[(sv, VPE)], inc=sc)
        P.op("vector", lambda e: e.reciprocal(r2r[:], r2r[:]),
             deps=[(sc, sc.v)], inc=sv)
        P.op("vector", lambda e: e.tensor_mul(brow[:], r2r[:], r2r[:]),
             deps=[(sv, sv.v)], inc=sv)
        P.op("vector", lambda e: e.tensor_mul(brow[:], brow[:], r0r[:]),
             deps=[(sv, sv.v)], inc=sv)
        buf("r0r").used(sv, sv.v)
        P.op("vector", lambda e: e.tensor_scalar(brow[:], brow[:], -0.5, 1.5,
                                                 ALU.mult, ALU.add),
             deps=[(sv, sv.v)], inc=sv)
        P.op("vector", lambda e: e.tensor_mul(r2r[:], r2r[:], brow[:]),
             deps=[(sv, sv.v)], inc=sv)
        P.op("vector", lambda e: e.scalar_tensor_tensor(brow[:], arow[:], -1.0, r2r[:],
                                                        ALU.mult, ALU.mult),
             deps=[(sv, sv.v)], inc=sv)
        buf("arow").used(sv, sv.v)
        ROWS = sv.v
        MM(pBa[:], c_ones1[:], r2r[:], [(sv, ROWS), CONST] + buf("pBa").take(), True, True)
        buf("r2r").used(st, st.v)
        MM(pBb[:], c_ones1[:], brow[:], [(sv, ROWS), CONST] + buf("pBb").take(), True, True)
        buf("brow").used(st, st.v)
        BC = st.v
        P.op("scalar", lambda e: e.activation(aB[:], pBa[:], ACT.Identity),
             deps=[(st, BC)] + buf("aB").take(), inc=sc)
        buf("pBa").used(sc, sc.v)
        P.op("scalar", lambda e: e.activation(bB[:], pBb[:], ACT.Identity),
             deps=[(st, BC)] + buf("bB").take(), inc=sc)
        buf("pBb").used(sc, sc.v)
        ABB = sc.v
        for ec in range(DC):
            parf = ec % 2
            fb_ = buf("x1f%d" % parf)
            P.op("vector", (lambda e, ec=ec, parf=parf: e.tensor_mul(
                    x1f[parf][:], z_sb[:, ec, :], aB[:])),
                 deps=[(sc, ABB)] + fb_.take(), inc=sv)
            buf("aB").used(sv, sv.v)
            buf("z_sb%d" % ec).used(sv, sv.v)
            P.op("vector", (lambda e, parf=parf: e.tensor_add(
                    x1f[parf][:], x1f[parf][:], bB[:])),
                 deps=[(sv, sv.v)], inc=sv)
            buf("bB").used(sv, sv.v)
            X1D = sv.v
            P.op("scalar", (lambda e, ec=ec, sbi=sbi, parf=parf: e.activation(
                    x1bf[:, ec, sbi * 512:(sbi + 1) * 512], x1f[parf][:], ACT.Identity)),
                 deps=[(sv, X1D)], inc=sc)
            fb_.used(sc, sc.v)
            DMA("scalar", x1_d[ec, :, sbi * 512:(sbi + 1) * 512], x1f[parf][:],
                [(sv, X1D)], d_x1w[parf])
            fb_.used(d_x1w[parf], d_x1w[parf].v)

    pend_tail = [None]
    for sbi in range(SB):
        atp = sbi % 2
        if sbi == 0:
            DMA("sync", att_s[0][:], attb_r[:, :, 0:512],
                [(d_atw[0], d_atw[0].v), (d_atw[1], d_atw[1].v)] + buf("att_s0").take(),
                d_ats[0])
        ATTL = 16 * (sbi // 2 + 1)
        pend_stat = [None]
        for ec in range(DC):
            parw = ec % 2
            DMA("gpsimd", wo_s[parw][:], woTr[:, :, ec * 128:(ec + 1) * 128],
                buf("wo_s%d" % parw).take(), d_wo[parw])
            WOL = d_wo[parw].v
            DMA("sync", xf_s[parw][:], xTr[:, ec, sbi * 512:(sbi + 1) * 512],
                buf("xf_s%d" % parw).take(), d_xr[parw])
            XRL = d_xr[parw].v
            pb_ = buf("pA2%d" % parw)
            war = pb_.take()
            for hc in range(H):
                MM(pA2[parw][:], wo_s[parw][:, hc, :], att_s[atp][:, hc, :],
                   [(d_wo[parw], WOL), (d_ats[atp], ATTL)] + (war if hc == 0 else []),
                   hc == 0, hc == H - 1)
            buf("wo_s%d" % parw).used(st, st.v)
            buf("att_s%d" % atp).used(st, st.v)
            if ec == 0 and pend_tail[0] is not None:
                pend_tail[0]()
                pend_tail[0] = None
            if pend_stat[0] is not None:
                pend_stat[0]()
                pend_stat[0] = None
            if ec == min(2, DC - 1) and sbi + 1 < SB:
                DMA("sync", att_s[(sbi + 1) % 2][:],
                    attb_r[:, :, (sbi + 1) * 512:(sbi + 2) * 512],
                    [(d_atw[0], d_atw[0].v), (d_atw[1], d_atw[1].v)] +
                    buf("att_s%d" % ((sbi + 1) % 2)).take(), d_ats[(sbi + 1) % 2])
            P.op("scalar", (lambda e, parw=parw, ec=ec: e.activation(
                    z_sb[:, ec, :], pA2[parw][:], ACT.Identity, bias=c_bo[:, ec:ec + 1])),
                 deps=[(st, st.v), CONST] + buf("z_sb%d" % ec).take(), inc=sc)
            pb_.used(sc, sc.v)
            P.op("vector", (lambda e, parw=parw, ec=ec: e.tensor_add(
                    z_sb[:, ec, :], z_sb[:, ec, :], xf_s[parw][:])),
                 deps=[(sc, sc.v), (d_xr[parw], XRL)], inc=sv)
            ZD = sv.v
            buf("xf_s%d" % parw).used(sv, ZD)
            P.op("scalar", (lambda e, parw=parw, ec=ec: e.activation(
                    zb_b[parw][:], z_sb[:, ec, :], ACT.Identity)),
                 deps=[(sv, ZD)] + buf("zb_b%d" % parw).take(), inc=sc)
            ZB = sc.v
            buf("z_sb%d" % ec).used(sc, ZB)
            P.op("vector", (lambda e, parw=parw, ec=ec: e.tensor_mul(
                    zz_b[parw][:], z_sb[:, ec, :], z_sb[:, ec, :])),
                 deps=[(sv, ZD)] + buf("zz_b%d" % parw).take(), inc=sv)
            ZZ = sv.v
            buf("z_sb%d" % ec).used(sv, ZZ)

            def _stat(ec=ec, parw=parw, ZB=ZB, ZZ=ZZ):
                MM(pT1[:], c_oneskb[:], zb_b[parw][:],
                   [(sc, ZB), CONST] + (buf("pT1").take() if ec == 0 else []),
                   ec == 0, ec == DC - 1)
                buf("zb_b%d" % parw).used(st, st.v)
                MM(pT2[:], c_oneskb[:], zz_b[parw][:],
                   [(sv, ZZ), CONST] + (buf("pT2").take() if ec == 0 else []),
                   ec == 0, ec == DC - 1)
                buf("zz_b%d" % parw).used(st, st.v)
            pend_stat[0] = _stat
        pend_stat[0]()
        pend_stat[0] = None
        STATS = st.v
        pend_tail[0] = (lambda sbi=sbi, STATS=STATS: emit_ln1_tail(sbi, STATS))
    pend_tail[0]()
    pend_tail[0] = None
    if debug:
        DMA("scalar", dbg["dbg_x1"], x1_d, [(d_x1w[0], d_x1w[0].v), (d_x1w[1], d_x1w[1].v), (d_dbg, d_dbg.v)], d_dbg)

    P.barrier(extra=[(d_x1w[0], d_x1w[0].v), (d_x1w[1], d_x1w[1].v),
                     (d_ats[0], d_ats[0].v), (d_ats[1], d_ats[1].v),
                     (d_wo[0], d_wo[0].v), (d_wo[1], d_wo[1].v),
                     (d_xr[0], d_xr[0].v), (d_xr[1], d_xr[1].v), (d_dbg, d_dbg.v)])
    esB.close()
    for n in list(B.keys()):
        if n not in ("x1bf",):
            B.pop(n, None)

    esC = ExitStack()
    sbC = lambda n, s_, d: esC.enter_context(nc.sbuf_tensor(n, s_, d))
    psC = lambda n, s_: esC.enter_context(nc.psum_tensor(n, s_, F32))
    w1b = [sbC("w1b%d" % i, [128, DC, 128], F32) for i in range(2)]
    a1b = [sbC("a1b%d" % i, [128, DC * 128], F32) for i in range(2)]
    w1q = [sbC("w1q%d" % i, [128, DC, 128], BF16) for i in range(2)]
    hb = [sbC("hb%d" % i, [128, S], BF16) for i in range(2)]
    w2b = [sbC("w2b%d" % i, [128, D], F32) for i in range(2)]
    a2b = [sbC("a2b%d" % i, [128, D], F32) for i in range(2)]
    w2qs = [sbC("w2qs%d" % i, [128, D], BF16) for i in range(2)]
    NPH = 2 * SB
    pH = [psC("pH%d" % i, [128, 512]) for i in range(NPH)]

    for fc in range(FC):
        par = fc % 2
        DMA("sync", w1b[par][:], w1Tr[:, :, fc * 128:(fc + 1) * 128],
            buf("w1b%d" % par).take(), d_w1[par])
        W1L = d_w1[par].v
        P.op("vector", (lambda e, par=par: e.tensor_scalar(
                a1b[par][:], w1b[par][:].rearrange("p c f -> p (c f)"), t_t1n[:],
                None, ALU.is_le)),
             deps=[(d_w1[par], W1L), (sv, THRD)] + buf("a1b%d" % par).take(), inc=sv)
        P.op("vector", (lambda e, par=par: e.scalar_tensor_tensor(
                w1q[par][:].rearrange("p c f -> p (c f)"),
                w1b[par][:].rearrange("p c f -> p (c f)"), t_t1[:],
                a1b[par][:], ALU.is_ge, ALU.subtract)),
             deps=[(sv, sv.v)] + buf("w1q%d" % par).take(), inc=sv)
        buf("w1b%d" % par).used(sv, sv.v)
        buf("a1b%d" % par).used(sv, sv.v)
        QD = sv.v
        for dc in range(DC):
            for sbi in range(SB):
                bank = par * SB + sbi
                MM(pH[bank][:], w1q[par][:, dc, :], x1bf[:, dc, sbi * 512:(sbi + 1) * 512],
                   [(sv, QD)] + (buf("pH%d" % bank).take() if dc == 0 else []),
                   dc == 0, dc == DC - 1)
        buf("w1q%d" % par).used(st, st.v)
        buf("x1bf").used(st, st.v)
        HMM = st.v
        for sbi in range(SB):
            bank = par * SB + sbi
            P.op("scalar", (lambda e, bank=bank, sbi=sbi, par=par, fc=fc: e.activation(
                    hb[par][:, sbi * 512:(sbi + 1) * 512], pH[bank][:], ACT.Gelu,
                    bias=c_b1[:, fc:fc + 1])),
                 deps=[(st, HMM), CONST] + (buf("hb%d" % par).take() if sbi == 0 else []),
                 inc=sc)
            buf("pH%d" % bank).used(sc, sc.v)
        DMA("scalar", h_d[fc], hb[par][:], [(sc, sc.v)], d_hw[par])
        buf("hb%d" % par).used(d_hw[par], d_hw[par].v)
        DMA("sync", w2b[par][:], w2Tr[:, fc, :], buf("w2b%d" % par).take(), d_w2[par])
        W2L = d_w2[par].v
        P.op("vector", (lambda e, par=par: e.tensor_scalar(
                a2b[par][:], w2b[par][:], t_t2n[:], None, ALU.is_le)),
             deps=[(d_w2[par], W2L), (sv, THRD)] + buf("a2b%d" % par).take(), inc=sv)
        P.op("vector", (lambda e, par=par: e.scalar_tensor_tensor(
                w2qs[par][:], w2b[par][:], t_t2[:], a2b[par][:],
                ALU.is_ge, ALU.subtract)),
             deps=[(sv, sv.v)] + buf("w2qs%d" % par).take(), inc=sv)
        buf("w2b%d" % par).used(sv, sv.v)
        buf("a2b%d" % par).used(sv, sv.v)
        DMA("scalar", w2q_d[fc], w2qs[par][:], [(sv, sv.v)], d_w2qw[par])
        buf("w2qs%d" % par).used(d_w2qw[par], d_w2qw[par].v)

    P.barrier(extra=[(d_hw[0], d_hw[0].v), (d_hw[1], d_hw[1].v), (d_w2qw[0], d_w2qw[0].v), (d_w2qw[1], d_w2qw[1].v),
                     (d_w1[0], d_w1[0].v), (d_w1[1], d_w1[1].v),
                     (d_w2[0], d_w2[0].v), (d_w2[1], d_w2[1].v), (d_dbg, d_dbg.v)])
    esC.close()
    esBC.close()
    B.clear()

    esD = ExitStack()
    sbD = lambda n, s_, d: esD.enter_context(nc.sbuf_tensor(n, s_, d))
    psD = lambda n, s_: esD.enter_context(nc.psum_tensor(n, s_, F32))
    EW2 = ECW * 128
    NEH = DC // ECW
    w2r = [sbD("w2r%d" % i, [128, FC, EW2], BF16) for i in range(2)]
    hs = [sbD("hs%d" % i, [128, 512], BF16) for i in range(4)]
    x1r = [sbD("x1r%d" % i, [128, 512], F32) for i in range(ECW)]
    z2t = [sbD("z2t%d" % i, [128, 512], F32) for i in range(2)]
    z2b = [sbD("z2b%d" % i, [128, 512], BF16) for i in range(ECW)]
    zzd = [sbD("zzd%d" % i, [128, 512], BF16) for i in range(ECW)]
    ssum = sbD("ssum", [1, S], F32)
    ssq = sbD("ssq", [1, S], F32)
    ar2 = sbD("ar2", [1, 512], F32)
    br2 = sbD("br2", [1, 512], F32)
    r02 = sbD("r02", [1, 512], F32)
    r22 = sbD("r22", [1, 512], F32)
    aB2 = sbD("aB2", [128, 512], F32)
    bB2 = sbD("bB2", [128, 512], F32)
    z2rd = [sbD("z2rd%d" % i, [128, 512], F32) for i in range(2)]
    outt = [sbD("outt%d" % i, [128, 512], F32) for i in range(2)]
    pY = [psD("pY%d" % i, [128, 512]) for i in range(ECW)]
    pSt_h = [psD("pSt0", [1, 512]), psD("pSt1", [1, 512])]
    pBc_h = [psD("pBc0", [128, 512]), psD("pBc1", [128, 512])]
    pSt = [pSt_h[0][:], pSt_h[1][:]]
    pBc = [pBc_h[0][:], pBc_h[1][:]]
    pSt_names = ["pSt0", "pSt1"]
    pBc_names = ["pBc0", "pBc1"]

    P.op("vector", lambda e: e.memset(ssum[:], 0.0), deps=(), inc=sv)
    P.op("vector", lambda e: e.memset(ssq[:], 0.0), deps=(), inc=sv)

    w2qr = w2q_d.rearrange("f p e -> p f e")
    x1_dr = x1_d
    def _ln2_block(ssl, sbi, SACC):
        P.op("vector", (lambda e, ssl=ssl: e.tensor_scalar(
                ar2[:], ssum[:, ssl], 1.0 / D, None, ALU.mult)),
             deps=[(sv, SACC)] + buf("ar2").take(), inc=sv)
        P.op("vector", (lambda e, ssl=ssl: e.tensor_scalar(
                br2[:], ssq[:, ssl], 1.0 / D, EPS, ALU.mult, ALU.add)),
             deps=[(sv, SACC)] + buf("br2").take(), inc=sv)
        P.op("vector", lambda e: e.tensor_mul(r22[:], ar2[:], ar2[:]),
             deps=[(sv, sv.v)] + buf("r22").take(), inc=sv)
        P.op("vector", lambda e: e.tensor_sub(r02[:], br2[:], r22[:]),
             deps=[(sv, sv.v)] + buf("r02").take(), inc=sv)
        buf("br2").used(sv, sv.v)
        VPE2 = sv.v
        P.op("scalar", lambda e: e.activation(r22[:], r02[:], ACT.Sqrt),
             deps=[(sv, VPE2)], inc=sc)
        P.op("vector", lambda e: e.reciprocal(r22[:], r22[:]),
             deps=[(sc, sc.v)], inc=sv)
        P.op("vector", lambda e: e.tensor_mul(br2[:], r22[:], r22[:]),
             deps=[(sv, sv.v)], inc=sv)
        P.op("vector", lambda e: e.tensor_mul(br2[:], br2[:], r02[:]),
             deps=[(sv, sv.v)], inc=sv)
        buf("r02").used(sv, sv.v)
        P.op("vector", lambda e: e.tensor_scalar(br2[:], br2[:], -0.5, 1.5,
                                                 ALU.mult, ALU.add),
             deps=[(sv, sv.v)], inc=sv)
        P.op("vector", lambda e: e.tensor_mul(r22[:], r22[:], br2[:]),
             deps=[(sv, sv.v)], inc=sv)
        P.op("vector", lambda e: e.scalar_tensor_tensor(br2[:], ar2[:], -1.0, r22[:],
                                                        ALU.mult, ALU.mult),
             deps=[(sv, sv.v)], inc=sv)
        buf("ar2").used(sv, sv.v)
        ROWS2 = sv.v
        MM(pBc[0], c_ones1[:], r22[:],
           [(sv, ROWS2), CONST] + buf(pBc_names[0]).take(), True, True)
        buf("r22").used(st, st.v)
        MM(pBc[1], c_ones1[:], br2[:],
           [(sv, ROWS2), CONST] + buf(pBc_names[1]).take(), True, True)
        buf("br2").used(st, st.v)
        BC2 = st.v
        P.op("scalar", lambda e: e.activation(aB2[:], pBc[0], ACT.Identity),
             deps=[(st, BC2)] + buf("aB2").take(), inc=sc)
        buf(pBc_names[0]).used(sc, sc.v)
        P.op("scalar", lambda e: e.activation(bB2[:], pBc[1], ACT.Identity),
             deps=[(st, BC2)] + buf("bB2").take(), inc=sc)
        buf(pBc_names[1]).used(sc, sc.v)
        ABB2 = sc.v
        for ec in range(DC):
            parr = ec % 2
            DMA("sync", z2rd[parr][:], z2_d[ec, :, sbi * 512:(sbi + 1) * 512],
                [(d_z2w[0], d_z2w[0].v), (d_z2w[1], d_z2w[1].v)] + buf("z2rd%d" % parr).take(), d_z2r[parr])
            Z2RL = d_z2r[parr].v
            P.op("vector", (lambda e, parr=parr: e.tensor_mul(
                    outt[parr][:], z2rd[parr][:], aB2[:])),
                 deps=[(sc, ABB2), (d_z2r[parr], Z2RL)] +
                      buf("outt%d" % parr).take(), inc=sv)
            buf("z2rd%d" % parr).used(sv, sv.v)
            buf("aB2").used(sv, sv.v)
            P.op("vector", (lambda e, parr=parr: e.tensor_add(
                    outt[parr][:], outt[parr][:], bB2[:])),
                 deps=[(sv, sv.v)], inc=sv)
            buf("bB2").used(sv, sv.v)
            DMA("scalar",
                outT.rearrange("(c p) s -> p c s", p=128)[:, ec, sbi * 512:(sbi + 1) * 512],
                outt[parr][:], [(sv, sv.v)], d_out[parr])
            buf("outt%d" % parr).used(d_out[parr], d_out[parr].v)
    defer_ln2 = [None]
    pend_dstat = [None]
    for eh in range(NEH):
        ep = eh % 2
        if eh == 0:
            for ee in range(min(2, NEH)):
                DMA("sync", w2r[ee % 2][:], w2qr[:, :, ee * EW2:(ee + 1) * EW2],
                    [(d_w2qw[0], d_w2qw[0].v), (d_w2qw[1], d_w2qw[1].v)] +
                    buf("w2r%d" % (ee % 2)).take(), d_w2r)
        elif eh + 1 < NEH:
            DMA("sync", w2r[(eh + 1) % 2][:], w2qr[:, :, (eh + 1) * EW2:(eh + 2) * EW2],
                buf("w2r%d" % ((eh + 1) % 2)).take(), d_w2r)
        W2RL = 16 * min(eh + 2, NEH)
        for sbi in range(SB):
            for fc in range(FC):
                hi_ = fc % 4
                DMA("sync", hs[hi_][:], h_d[fc, :, sbi * 512:(sbi + 1) * 512],
                    [(d_hw[0], d_hw[0].v), (d_hw[1], d_hw[1].v)] + buf("hs%d" % hi_).take(), d_hs[hi_])
                HL = d_hs[hi_].v
                for j in range(ECW):
                    MM(pY[j][:], w2r[ep][:, fc, j * 128:(j + 1) * 128], hs[hi_][:],
                       [(d_hs[hi_], HL), (d_w2r, W2RL)] +
                       (buf("pY%d" % j).take() if fc == 0 else []),
                       fc == 0, fc == FC - 1)
                buf("hs%d" % hi_).used(st, st.v)
                if fc == 4 and pend_dstat[0] is not None:
                    pend_dstat[0]()
                    pend_dstat[0] = None
            SWEEP = st.v
            buf("w2r%d" % ep).used(st, SWEEP)
            if pend_dstat[0] is not None:
                pend_dstat[0]()
                pend_dstat[0] = None
            if defer_ln2[0] is not None:
                defer_ln2[0]()
                defer_ln2[0] = None
            stat_deps = []
            x1rl = []
            for j in range(ECW):
                ec = eh * ECW + j
                DMA("sync", x1r[j][:], x1_dr[ec, :, sbi * 512:(sbi + 1) * 512],
                    [(d_x1w[0], d_x1w[0].v), (d_x1w[1], d_x1w[1].v)] + buf("x1r%d" % j).take(), d_x1r[j])
                x1rl.append(d_x1r[j].v)
            for j in range(ECW):
                ec = eh * ECW + j
                parz = j % 2
                X1RL = x1rl[j]
                P.op("scalar", (lambda e, j=j, ec=ec, parz=parz: e.activation(
                        z2t[parz][:], pY[j][:], ACT.Identity, bias=c_b2[:, ec:ec + 1])),
                     deps=[(st, SWEEP), CONST] + buf("z2t%d" % parz).take(), inc=sc)
                buf("pY%d" % j).used(sc, sc.v)
                P.op("vector", (lambda e, parz=parz, j=j: e.tensor_add(
                        z2t[parz][:], z2t[parz][:], x1r[j][:])),
                     deps=[(sc, sc.v), (d_x1r[j], X1RL)], inc=sv)
                Z2D = sv.v
                buf("x1r%d" % j).used(sv, Z2D)
                P.op("scalar", (lambda e, j=j, parz=parz: e.activation(
                        z2b[j][:], z2t[parz][:], ACT.Identity)),
                     deps=[(sv, Z2D)] + buf("z2b%d" % j).take(), inc=sc)
                ZB2 = sc.v
                buf("z2t%d" % parz).used(sc, ZB2)
                P.op("vector", (lambda e, j=j, parz=parz: e.tensor_mul(
                        zzd[j][:], z2t[parz][:], z2t[parz][:])),
                     deps=[(sv, Z2D)] + buf("zzd%d" % j).take(), inc=sv)
                ZZ2 = sv.v
                buf("z2t%d" % parz).used(sv, ZZ2)
                DMA("scalar", z2_d[ec, :, sbi * 512:(sbi + 1) * 512], z2t[parz][:],
                    [(sv, Z2D)], d_z2w[parz])
                buf("z2t%d" % parz).used(d_z2w[parz], d_z2w[parz].v)
                stat_deps.append((j, ZB2, ZZ2))

            def _dstat(stat_deps=stat_deps, sbi=sbi, eh=eh):
                for (j, ZB2, ZZ2) in stat_deps:
                    MM(pSt[0], c_oneskb[:], z2b[j][:],
                       [(sc, ZB2), CONST] + (buf(pSt_names[0]).take() if j == 0 else []),
                       j == 0, j == ECW - 1)
                    buf("z2b%d" % j).used(st, st.v)
                    MM(pSt[1], c_oneskb[:], zzd[j][:],
                       [(sv, ZZ2), CONST] + (buf(pSt_names[1]).take() if j == 0 else []),
                       j == 0, j == ECW - 1)
                    buf("zzd%d" % j).used(st, st.v)
                STT = st.v
                ssl = slice(sbi * 512, (sbi + 1) * 512)
                P.op("vector", (lambda e, ssl=ssl: e.tensor_add(ssum[:, ssl], ssum[:, ssl],
                                                                pSt[0])),
                     deps=[(st, STT)], inc=sv)
                buf(pSt_names[0]).used(sv, sv.v)
                P.op("vector", (lambda e, ssl=ssl: e.tensor_add(ssq[:, ssl], ssq[:, ssl],
                                                                pSt[1])),
                     deps=[(st, STT)], inc=sv)
                buf(pSt_names[1]).used(sv, sv.v)
                SACC = sv.v
                if eh == NEH - 1:
                    defer_ln2[0] = (lambda: _ln2_block(ssl, sbi, SACC))
            pend_dstat[0] = _dstat

    if pend_dstat[0] is not None:
        pend_dstat[0]()
        pend_dstat[0] = None


    if defer_ln2[0] is not None:
        defer_ln2[0]()
        defer_ln2[0] = None

    P.wait("sync", [(d_out[0], d_out[0].v), (d_out[1], d_out[1].v),
                    (d_z2w[0], d_z2w[0].v), (d_z2w[1], d_z2w[1].v), (d_z2r[0], d_z2r[0].v),
                    (d_z2r[1], d_z2r[1].v), (d_w2r, d_w2r.v), (d_dbg, d_dbg.v)]
            + [(d_x1r[i], d_x1r[i].v) for i in range(4)]
            + [(d_hs[i], d_hs[i].v) for i in range(4)])

    with nc.Block() as block:
        block.sync(lambda e: [f(e) for f in P.q["sync"]])
        block.gpsimd(lambda e: [f(e) for f in P.q["gpsimd"]])
        block.vector(lambda e: [f(e) for f in P.q["vector"]])
        block.scalar(lambda e: [f(e) for f in P.q["scalar"]])
        block.tensor(lambda e: [f(e) for f in P.q["tensor"]])

    esD.close()
    return nc



def make_inputs_for_core(cfg, xb, wq, bq_, wk, bk_, wv, bv_, wo, bo_, w1, b1_, w2, b2_):
    D, S = cfg.D, cfg.S
    DC, FC = cfg.DC, cfg.FC
    perm = np.concatenate([h * 128 + np.concatenate([np.arange(0, 128, 2),
                                                     np.arange(1, 128, 2)])
                           for h in range(cfg.H)])
    m = {}
    m["xT"] = np.ascontiguousarray(xb.T)
    m["wqT"] = np.ascontiguousarray(wq.T[:, perm])
    m["wkT"] = np.ascontiguousarray(wk.T[:, perm])
    m["wvT"] = np.ascontiguousarray(wv.T)
    m["woT"] = np.ascontiguousarray(wo.T)
    m["bq"] = np.ascontiguousarray(bq_[perm].reshape(DC, 128).T)
    m["bk"] = np.ascontiguousarray(bk_[perm].reshape(DC, 128).T)
    m["bvr"] = np.ascontiguousarray(bv_.reshape(1, D))
    m["bo"] = np.ascontiguousarray(bo_.reshape(DC, 128).T)
    m["w1T"] = np.ascontiguousarray(w1.T)
    m["b1"] = np.ascontiguousarray(b1_.reshape(FC, 128).T)
    m["w2T"] = np.ascontiguousarray(w2.T)
    m["b2"] = np.ascontiguousarray(b2_.reshape(DC, 128).T)
    m["srow"] = np.arange(S, dtype=np.float32).reshape(1, S)
    m["icol"] = np.arange(64, dtype=np.float32).reshape(64, 1)
    m["ones1"] = np.ones((1, 128), np.float32)
    m["onesk"] = np.ones((128, 1), np.float32)
    m["oneskb"] = np.ones((128, 1), ml_dtypes.bfloat16)
    m["ones1b"] = np.ones((1, 128), ml_dtypes.bfloat16)
    m["ones1sc"] = np.full((1, 128), 2.0 ** 20, ml_dtypes.bfloat16)
    i_ = np.arange(128)[:, None]
    j_ = np.arange(512)[None, :]
    masks = np.stack([(i_ + mi * 128 <= j_) for mi in range(4)], axis=1)
    m["masks"] = masks.astype(ml_dtypes.bfloat16)
    return m


def kernel(L0_attn_x, L0_attn_Wq, L0_attn_bq, L0_attn_Wk, L0_attn_bk,
           L0_attn_Wv, L0_attn_bv, L0_attn_Wo, L0_attn_bo,
           L0_ffn_w1, L0_ffn_b1, L0_ffn_w2, L0_ffn_b2):
    cfg = Cfg(D=2048, S=2048, DFF=8192)
    nc = build(cfg)
    args = [np.asarray(a) for a in
            (L0_attn_Wq, L0_attn_bq, L0_attn_Wk, L0_attn_bk, L0_attn_Wv,
             L0_attn_bv, L0_attn_Wo, L0_attn_bo, L0_ffn_w1, L0_ffn_b1,
             L0_ffn_w2, L0_ffn_b2)]
    x = np.asarray(L0_attn_x)
    in_maps = [make_inputs_for_core(cfg, x[b], *args) for b in range(NCORES)]
    res = run_bass_kernel_spmd(nc, in_maps, core_ids=list(range(NCORES)))
    out = np.empty((NCORES, cfg.S, cfg.D), np.float32)
    for b in range(NCORES):
        out[b] = res.results[b]["outT"].T
    return out


# version 25 (final)
# speedup vs baseline: 1.0364x; 1.0294x over previous
import math
from contextlib import ExitStack

import numpy as np
import ml_dtypes
import concourse.bass as bass
import concourse.mybir as mybir
from concourse.bass_utils import run_bass_kernel_spmd

F32 = mybir.dt.float32
BF16 = mybir.dt.bfloat16
I32 = mybir.dt.int32
AX = mybir.AxisListType.X
ALU = mybir.AluOpType
ACT = mybir.ActivationFunctionType

NCORES = 8


def _hilo(x):
    hi = np.float32(x)
    return float(hi), float(x - float(hi))


class Cfg:
    def __init__(self, D=2048, S=2048, DFF=8192):
        self.D, self.S, self.DFF = D, S, DFF
        self.DC = D // 128
        self.H = self.DC
        self.SB = S // 512
        self.ST = S // 128
        self.FC = DFF // 128
        self.ECW = max(1, min(4, self.DC // 2))
        assert D % 256 == 0 and S % 512 == 0 and DFF % 128 == 0


class Sem:
    __slots__ = ("h", "v")
    def __init__(self, h):
        self.h = h
        self.v = 0


class Buf:
    __slots__ = ("cons",)
    def __init__(self):
        self.cons = []
    def used(self, sem, val):
        self.cons.append((sem, val))
    def take(self):
        c = self.cons
        self.cons = []
        return c


class Prog:
    ENGINES = ("sync", "gpsimd", "vector", "scalar", "tensor")

    def __init__(self, nc, es):
        self.nc = nc
        self.es = es
        self.q = {e: [] for e in self.ENGINES}
        self.waited = {e: {} for e in self.ENGINES}
        self.st = self.sem("c_tensor")
        self.sv = self.sem("c_vector")
        self.sc = self.sem("c_scalar")
        self.sg = self.sem("c_gpsimd")

    def sem(self, name):
        return Sem(self.es.enter_context(self.nc.semaphore(name)))

    def op(self, eng, fn, deps=(), inc=None, n=1):
        w = self.waited[eng]
        own = {"tensor": self.st, "vector": self.sv,
               "scalar": self.sc, "gpsimd": self.sg}.get(eng)
        dl = []
        for s, v in deps:
            if own is not None and s is own and v > own.v:
                raise AssertionError(
                    "emission inversion on %s: dep %d > current %d" % (eng, v, own.v))
            if v and v > w.get(s, 0):
                dl.append((s.h, v))
                w[s] = v
        if inc is not None:
            inc.v += n
        def run(e, fn=fn, dl=tuple(dl), ih=(inc.h if inc is not None else None), n=n):
            for h, v in dl:
                e.wait_ge(h, v)
            ins = fn(e)
            if ih is not None:
                ins.then_inc(ih, n)
        self.q[eng].append(run)
        return inc.v if inc is not None else None

    def wait(self, eng, deps):
        w = self.waited[eng]
        dl = []
        for s, v in deps:
            if v and v > w.get(s, 0):
                dl.append((s.h, v))
                w[s] = v
        if dl:
            def run(e, dl=tuple(dl)):
                for h, v in dl:
                    e.wait_ge(h, v)
            self.q[eng].append(run)

    def barrier(self, extra=()):
        deps = [(c, c.v) for c in (self.st, self.sv, self.sc, self.sg)] + list(extra)
        for e in self.ENGINES:
            self.wait(e, deps)


def build(cfg, debug=False):
    D, S, DFF = cfg.D, cfg.S, cfg.DFF
    DC, H, SB, ST, FC, ECW = cfg.DC, cfg.H, cfg.SB, cfg.ST, cfg.FC, cfg.ECW

    nc = bass.Bass("TRN2", target_bir_lowering=False)
    es = ExitStack()
    P = Prog(nc, es)
    st, sv, sc, sg = P.st, P.sv, P.sc, P.sg

    def din(name, shape, dt=F32):
        return nc.dram_tensor(name, shape, dt, kind="ExternalInput").ap()

    xT = din("xT", [D, S])
    wqT = din("wqT", [D, D]); wkT = din("wkT", [D, D])
    wvT = din("wvT", [D, D]); woT = din("woT", [D, D])
    bq_i = din("bq", [128, DC]); bk_i = din("bk", [128, DC])
    bvr_i = din("bvr", [1, D]); bo_i = din("bo", [128, DC])
    w1T = din("w1T", [D, DFF]); b1_i = din("b1", [128, FC])
    w2T = din("w2T", [DFF, D]); b2_i = din("b2", [128, DC])
    srow_i = din("srow", [1, S])
    icol_i = din("icol", [64, 1])
    ones1_i = din("ones1", [1, 128])
    onesk_i = din("onesk", [128, 1])
    oneskb_i = nc.dram_tensor("oneskb", [128, 1], BF16, kind="ExternalInput").ap()
    ones1b_i = nc.dram_tensor("ones1b", [1, 128], BF16, kind="ExternalInput").ap()
    ones1sc_i = nc.dram_tensor("ones1sc", [1, 128], BF16, kind="ExternalInput").ap()
    masks_i = din("masks", [128, 4, 512], BF16)

    outT = nc.dram_tensor("outT", [D, S], F32, kind="ExternalOutput").ap()
    dbg = {}
    if debug:
        def dout(name, shape, dt=F32):
            dbg[name] = nc.dram_tensor(name, shape, dt, kind="ExternalOutput").ap()
        dout("dbg_vtm", [ST, 128, D], BF16)
        dout("dbg_q0", [128, S], BF16); dout("dbg_k0", [128, S], BF16)
        dout("dbg_aa1", [128, 1]); dout("dbg_aa2", [128, 1])
        dout("dbg_att0", [128, S], BF16)
        dout("dbg_x1", [DC, 128, S])
        dout("dbg_cc", [128, S]); dout("dbg_ss", [128, S])
        dout("dbg_t1", [1, 1]); dout("dbg_t2", [1, 1])

    v_d = nc.dram_tensor("v_d", [ST, 128, D], BF16).ap()
    attb_d = nc.dram_tensor("attb_d", [H, 128, S], BF16).ap()
    x1_d = nc.dram_tensor("x1_d", [DC, 128, S], F32).ap()
    h_d = nc.dram_tensor("h_d", [FC, 128, S], BF16).ap()
    w2q_d = nc.dram_tensor("w2q_d", [FC, 128, D], BF16).ap()
    z2_d = nc.dram_tensor("z2_d", [DC, 128, S], F32).ap()

    d_const = P.sem("d_const")
    d_x = P.sem("d_x")
    d_wv = P.sem("d_wv")
    d_vw = [P.sem("d_vw0"), P.sem("d_vw1")]
    d_vh = [P.sem("d_vh0"), P.sem("d_vh1")]
    d_wq = P.sem("d_wq"); d_wk = P.sem("d_wk")
    d_wo = [P.sem("d_wo0"), P.sem("d_wo1")]
    d_swq = P.sem("d_swq"); d_swk = P.sem("d_swk")
    d_trig = P.sem("d_trig")
    d_trigc = P.sem("d_trigc")
    d_cbf = P.sem("d_cbf")
    d_ab = [P.sem("d_ab%d" % i) for i in range(4)]
    d_atw = [P.sem("d_atw0"), P.sem("d_atw1")]
    d_ats = [P.sem("d_ats0"), P.sem("d_ats1")]
    d_xr = [P.sem("d_xr0"), P.sem("d_xr1")]
    d_x1w = [P.sem("d_x1w0"), P.sem("d_x1w1")]
    d_w1 = [P.sem("d_w10"), P.sem("d_w11")]
    d_w2 = [P.sem("d_w20"), P.sem("d_w21")]
    d_hw = [P.sem("d_hw0"), P.sem("d_hw1")]
    d_w2qw = [P.sem("d_w2qw0"), P.sem("d_w2qw1")]
    d_w2r = P.sem("d_w2r")
    d_hs = [P.sem("d_hs%d" % i) for i in range(4)]
    d_x1r = [P.sem("d_x1r%d" % i) for i in range(4)]
    d_z2w = [P.sem("d_z2w0"), P.sem("d_z2w1")]
    d_z2r = [P.sem("d_z2r0"), P.sem("d_z2r1")]
    d_out = [P.sem("d_out0"), P.sem("d_out1")]
    d_dbg = P.sem("d_dbg")

    C2PI_HI, C2PI_LO = _hilo(2.0 * math.pi)
    CPI2_HI, CPI2_LO = _hilo(math.pi / 2.0)
    INV2PI = float(np.float32(1.0 / (2.0 * math.pi)))
    SC_HI, SC_LO = _hilo(1.0 / math.sqrt(128.0))
    LN4_HI, LN4_LO = _hilo(math.log(10000.0) / 64.0)
    PI_F = float(np.float32(math.pi))
    TQ_HI, TQ_LO = _hilo(0.7 / (D * DFF))
    EPS = 1e-5

    def DMA(eng, out_ap, in_ap, deps, inc):
        return P.op(eng, (lambda e, o=out_ap, i=in_ap: e.dma_start(out=o, in_=i)),
                    deps=deps, inc=inc, n=16)

    def MM(out_ap, lhsT_ap, rhs_ap, deps, start, stop):
        return P.op("tensor", (lambda e, o=out_ap, l=lhsT_ap, r=rhs_ap, a=start, b=stop:
                               e.matmul(o, l, r, start=a, stop=b, skip_group_check=True)),
                    deps=deps, inc=st)

    sb = nc.alloc_sbuf_tensor
    c_ones1 = sb("c_ones1", [1, 128], F32)
    c_onesk = sb("c_onesk", [128, 1], F32)
    c_oneskb = sb("c_oneskb", [128, 1], BF16)
    c_ones1b = sb("c_ones1b", [1, 128], BF16)
    c_ones1sc = sb("c_ones1sc", [1, 128], BF16)
    c_bq = sb("c_bq", [128, DC], F32); c_bk = sb("c_bk", [128, DC], F32)
    c_bo = sb("c_bo", [128, DC], F32); c_b2 = sb("c_b2", [128, DC], F32)
    c_b1 = sb("c_b1", [128, FC], F32)
    t_t1 = sb("t_t1", [128, 1], F32); t_t1n = sb("t_t1n", [128, 1], F32)
    t_t2 = sb("t_t2", [128, 1], F32); t_t2n = sb("t_t2n", [128, 1], F32)
    absac = [sb("absac1", [128, 1], F32), sb("absac2", [128, 1], F32)]
    absred = sb("absred", [128, 1], F32)
    tm11 = [sb("tm11a", [1, 1], F32), sb("tm11b", [1, 1], F32)]

    B = {}
    def buf(name):
        if name not in B:
            B[name] = Buf()
        return B[name]

    nconst = 0

    esA = ExitStack()
    sbA = lambda n, s_, d: esA.enter_context(nc.sbuf_tensor(n, s_, d))
    psA = lambda n, s_: esA.enter_context(nc.psum_tensor(n, s_, F32))
    x_bf = sbA("x_bf", [128, DC, S], BF16)
    c_masks = sbA("c_masks", [128, 4, 512], BF16)
    c_cc2 = sbA("c_cc2", [128, S], F32)
    c_ss2 = sbA("c_ss2", [128, S], F32)
    c_cc2k = sbA("c_cc2k", [128, S], F32)
    c_ss2k = sbA("c_ss2k", [128, S], F32)
    c_bvr = sbA("c_bvr", [1, D], BF16)
    vh_s = [sbA("vh_s%d" % i, [128, ST, 128], BF16) for i in range(2)]
    abuf = [sbA("abuf%d" % i, [128, 512], F32) for i in range(4)]
    absscr = sbA("absscr", [128, 512], F32)
    absred2 = sbA("absred2", [128, 1], F32)
    wq_s = sbA("wq_s", [128, DC, 128], BF16)
    wk_s = sbA("wk_s", [128, DC, 128], BF16)
    esT = ExitStack()
    sbT = lambda n, s_, d: esT.enter_context(nc.sbuf_tensor(n, s_, d))
    c_srow = sbT("c_srow", [1, 512], F32)
    c_icol = sbT("c_icol", [64, 1], F32)
    g_th = sbT("g_th", [64, 512], F32)
    g_a = sbT("g_a", [64, 512], F32)
    g_b = sbT("g_b", [64, 512], F32)
    g_ki = sbT("g_ki", [64, 512], I32)
    g_f = sbT("g_f", [64, 1], F32)
    g_sin = sbT("g_sin", [64, 512], F32)
    esV = ExitStack()
    sbV = lambda n, s_, d: esV.enter_context(nc.sbuf_tensor(n, s_, d))
    vstage = [sbV("vstage%d" % i, [128, 512], BF16) for i in range(2)]
    EW = min(512, D)
    EH = D // EW
    wv_s = sbV("wv_s", [128, DC, EW], BF16)

    pA = [psA("pA%d" % i, [128, 512]) for i in range(2)]
    pS = [psA("pS%d" % i, [128, 512]) for i in range(2)]
    pV = [psA("pV%d" % i, [128, 512]) for i in range(2)]
    pB = psA("pB", [128, 512])
    pT = psA("pT", [1, 512])
    pG = pV[1]

    for o, i in ((c_ones1[:], ones1_i), (c_onesk[:], onesk_i), (c_oneskb[:], oneskb_i),
                 (c_ones1b[:], ones1b_i), (c_ones1sc[:], ones1sc_i),
                 (c_masks[:], masks_i),
                 (c_bq[:], bq_i), (c_bk[:], bk_i), (c_bo[:], bo_i), (c_b2[:], b2_i),
                 (c_b1[:], b1_i), (c_icol[:], icol_i)):
        DMA("sync", o, i, (), d_const)
        nconst += 16
    CONST = (d_const, nconst)
    DMA("gpsimd", c_bvr[:], bvr_i, (), d_cbf)
    CBVR = (d_cbf, 16)
    d_srow = P.sem("d_srow")

    P.op("vector", lambda e: e.tensor_scalar(g_f[:], c_icol[:], -LN4_HI, None, ALU.mult),
         deps=[CONST] + buf("g_f").take(), inc=sv)
    P.op("vector", lambda e: e.scalar_tensor_tensor(g_f[:], c_icol[:], -LN4_LO, g_f[:],
                                                    ALU.mult, ALU.add),
         deps=[(sv, sv.v), CONST], inc=sv)
    P.op("scalar", lambda e: e.activation(g_f[:], g_f[:], ACT.Exp),
         deps=[(sv, sv.v)], inc=sc)
    FREQ = sc.v

    n_trig = 0
    for cki in range(S // 512):
        sl = slice(cki * 512, cki * 512 + 512)
        DMA("sync", c_srow[:], srow_i[:, sl], buf("c_srow").take(), d_srow)
        MM(pG[0:64, :], c_ones1[:, 0:64], c_srow[:],
           [CONST, (d_srow, d_srow.v)] + buf("pV1").take(), True, True)
        buf("c_srow").used(st, st.v)
        P.op("vector", lambda e: e.tensor_scalar(g_th[:], pG[0:64, :], g_f[:], None, ALU.mult),
             deps=[(st, st.v), (sc, FREQ)] + buf("g_th").take(), inc=sv)
        buf("pV1").used(sv, sv.v)
        TH = sv.v
        for half, target in ((0, "sin"), (1, "cos")):
            if half == 0:
                P.op("vector", lambda e: e.tensor_scalar(g_a[:], g_th[:], INV2PI,
                                                         None, ALU.mult),
                     deps=[(sv, TH)] + buf("g_a").take(), inc=sv)
            else:
                P.op("vector", lambda e: e.tensor_scalar(g_b[:], g_th[:], CPI2_HI, CPI2_LO,
                                                         ALU.add, ALU.add),
                     deps=[(sv, TH)] + buf("g_b").take(), inc=sv)
                P.op("vector", lambda e: e.tensor_scalar(g_a[:], g_b[:], INV2PI,
                                                         None, ALU.mult),
                     deps=[(sv, sv.v)] + buf("g_a").take(), inc=sv)
            P.op("vector", lambda e: e.tensor_copy(g_ki[:], g_a[:]), deps=[(sv, sv.v)], inc=sv)
            P.op("vector", lambda e: e.tensor_copy(g_a[:], g_ki[:]), deps=[(sv, sv.v)], inc=sv)
            src = g_th if half == 0 else g_b
            P.op("vector", (lambda e, src=src: e.scalar_tensor_tensor(
                    g_b[:], g_a[:], -C2PI_HI, src[:], ALU.mult, ALU.add)),
                 deps=[(sv, sv.v)] + (buf("g_b").take() if half == 0 else []), inc=sv)
            P.op("vector", lambda e: e.scalar_tensor_tensor(g_b[:], g_a[:], -C2PI_LO, g_b[:],
                                                            ALU.mult, ALU.add),
                 deps=[(sv, sv.v)], inc=sv)
            P.op("vector", lambda e: e.tensor_scalar(g_a[:], g_b[:], PI_F, None, ALU.is_gt),
                 deps=[(sv, sv.v)], inc=sv)
            P.op("vector", lambda e: e.scalar_tensor_tensor(g_b[:], g_a[:], -C2PI_HI, g_b[:],
                                                            ALU.mult, ALU.add),
                 deps=[(sv, sv.v)], inc=sv)
            P.op("vector", lambda e: e.scalar_tensor_tensor(g_b[:], g_a[:], -C2PI_LO, g_b[:],
                                                            ALU.mult, ALU.add),
                 deps=[(sv, sv.v)], inc=sv)
            P.op("vector", lambda e: e.tensor_scalar(g_b[:], g_b[:], PI_F, -PI_F,
                                                     ALU.min, ALU.max),
                 deps=[(sv, sv.v)], inc=sv)
            buf("g_a").used(sv, sv.v)
            RED = sv.v
            if half == 0:
                P.op("scalar", lambda e: e.activation(g_sin[:], g_b[:], ACT.Sin),
                     deps=[(sv, RED)] + buf("g_sin").take(), inc=sc)
                buf("g_b").used(sc, sc.v)
                P.op("vector", (lambda e, sl=sl: e.tensor_scalar(
                        c_ss2[0:64, sl], g_sin[:], -1.0, None, ALU.mult)),
                     deps=[(sc, sc.v)], inc=sv)
                DMA("gpsimd", c_ss2[64:128, sl], g_sin[:], [(sc, sc.v)], d_trig)
                n_trig += 16
                buf("g_sin").used(d_trig, n_trig)
                buf("g_sin").used(sv, sv.v)
            else:
                P.op("scalar", (lambda e, sl=sl: e.activation(
                        c_cc2[0:64, sl], g_b[:], ACT.Sin)),
                     deps=[(sv, RED)], inc=sc)
                buf("g_b").used(sc, sc.v)
                DMA("gpsimd", c_cc2[64:128, sl], c_cc2[0:64, sl], [(sc, sc.v)], d_trigc)
        buf("g_th").used(sv, sv.v)
        buf("g_th").used(sc, sc.v)
    TRIG0 = (d_trig, n_trig)
    TRIG0C = (d_trigc, d_trigc.v)
    for (srct, dstt, dstn) in ((c_cc2, c_cc2k, "c_cc2k"), (c_ss2, c_ss2k, "c_ss2k")):
        P.op("vector", (lambda e, srct=srct, dstt=dstt: e.tensor_scalar(
                dstt[:], srct[:], SC_HI, None, ALU.mult)),
             deps=[TRIG0, TRIG0C, (sc, sc.v), (sv, sv.v)] + buf(dstn).take(), inc=sv)
        P.op("vector", (lambda e, srct=srct, dstt=dstt: e.scalar_tensor_tensor(
                dstt[:], srct[:], SC_LO, dstt[:], ALU.mult, ALU.add)),
             deps=[(sv, sv.v)], inc=sv)
    TRIG = (sv, sv.v)

    xTr = xT.rearrange("(c p) s -> p c s", p=128)
    for dc in range(DC):
        DMA("gpsimd", x_bf[:, dc, :], xTr[:, dc, :], (), d_x)
    XL = (d_x, 16 * DC)

    ab_tiles = []
    t1c = min(512, DFF)
    w1Tr = w1T.rearrange("(c p) f -> p c f", p=128)
    for dc in range(DC):
        for j in range(DFF // t1c):
            ab_tiles.append((0, w1Tr[:, dc, j * t1c:(j + 1) * t1c], t1c))
    t2c = min(512, D)
    w2Tr = w2T.rearrange("(c p) e -> p c e", p=128)
    for fc in range(FC):
        for j in range(D // t2c):
            ab_tiles.append((1, w2Tr[:, fc, j * t2c:(j + 1) * t2c], t2c))
    n_ab = len(ab_tiles)
    ab_i = [0]

    P.op("vector", lambda e: e.memset(absac[0][:], 0.0), deps=(), inc=sv)
    P.op("vector", lambda e: e.memset(absac[1][:], 0.0), deps=(), inc=sv)
    ABZ = sv.v
    absac_last = [ABZ, ABZ]

    def emit_absmean(k):
        for _ in range(k):
            i = ab_i[0]
            if i >= n_ab:
                return
            ab_i[0] = i + 1
            which, iap, tc = ab_tiles[i]
            par = i % 4
            DMA("sync", abuf[par][:, 0:tc], iap, buf("abuf%d" % par).take(), d_ab[par])
            nl = d_ab[par].v
            if par % 2 == 0:
                P.op("vector", (lambda e, par=par, tc=tc: e.reduce_sum(
                        absred[:], abuf[par][:, 0:tc], AX, apply_absolute_value=True)),
                     deps=[(d_ab[par], nl)] + buf("absred").take(), inc=sv)
                buf("abuf%d" % par).used(sv, sv.v)
                P.op("vector", (lambda e, which=which: e.tensor_add(
                        absac[which][:], absac[which][:], absred[:])),
                     deps=[(sv, sv.v), (sv, absac_last[which])], inc=sv)
                absac_last[which] = sv.v
                buf("absred").used(sv, sv.v)
            else:
                P.op("scalar", (lambda e, par=par, tc=tc: e.activation(
                        absscr[:, 0:tc], abuf[par][:, 0:tc], ACT.Abs,
                        accum_out=absred2[:])),
                     deps=[(d_ab[par], nl)] + buf("absred2").take() + buf("absscr").take(),
                     inc=sc)
                buf("abuf%d" % par).used(sc, sc.v)
                buf("absscr").used(sc, sc.v)
                P.op("vector", (lambda e, which=which: e.tensor_add(
                        absac[which][:], absac[which][:], absred2[:])),
                     deps=[(sc, sc.v), (sv, absac_last[which])], inc=sv)
                absac_last[which] = sv.v
                buf("absred2").used(sv, sv.v)

    wvTr = wvT.rearrange("(c p) e -> p c e", p=128)
    EBW = min(512, EW)
    for eh in range(EH):
        DMA("gpsimd", wv_s[:], wvTr[:, :, eh * EW:(eh + 1) * EW],
            buf("wv_s").take(), d_wv)
        WVL = d_wv.v
        for sti in range(ST):
            for ebi in range(EW // EBW):
                pvp = pS[ebi % 2]
                pb_ = buf("pS%d" % (ebi % 2))
                e0 = eh * EW + ebi * EBW
                MM(pvp[:, 0:EBW], c_ones1b[:, 0:128], c_bvr[:, e0:e0 + EBW],
                   [CONST, CBVR] + pb_.take(), True, False)
                for dc in range(DC):
                    MM(pvp[:, 0:EBW], x_bf[:, dc, sti * 128:(sti + 1) * 128],
                       wv_s[:, dc, ebi * EBW:(ebi + 1) * EBW],
                       [XL, (d_wv, WVL)], False, dc == DC - 1)
                buf("wv_s").used(st, st.v)
                buf("x_bf").used(st, st.v)
                pstg = (sti * (EW // EBW) + ebi) % 2
                P.op("scalar", (lambda e, p=pvp, pstg=pstg: e.activation(
                        vstage[pstg][:, 0:EBW], p[:, 0:EBW], ACT.Identity)),
                     deps=[(st, st.v)] + buf("vstage%d" % pstg).take(), inc=sc)
                pb_.used(sc, sc.v)
                DMA("scalar", v_d[sti, :, e0:e0 + EBW], vstage[pstg][:, 0:EBW],
                    [(sc, sc.v)], d_vw[pstg])
                buf("vstage%d" % pstg).used(d_vw[pstg], d_vw[pstg].v)
            emit_absmean(1)
    VDONE = [(d_vw[0], d_vw[0].v), (d_vw[1], d_vw[1].v)]
    esV.close()
    esT.close()
    qf = sbA("qf", [128, S], F32)
    kf = sbA("kf", [128, S], F32)
    qsw = sbA("qsw", [128, S], F32)
    rt1 = sbA("rt1", [128, S], F32)
    q_bf = [sbA("q_bf%d" % i, [128, S], BF16) for i in range(2)]
    k_bf = [sbA("k_bf%d" % i, [128, S], BF16) for i in range(2)]
    sqbs = [sbA("sqbs%d" % i, [128, 512], BF16) for i in range(2 * SB)]
    dsum = [sbA("dsum%d" % i, [128, 512], F32) for i in range(2)]
    dsumb = [sbA("dsumb%d" % i, [128, 512], BF16) for i in range(2)]
    negb2 = [sbA("negb2_%d" % i, [1, 1], F32) for i in range(2)]
    expb = [sbA("expb%d" % i, [128, 512], BF16) for i in range(4)]
    recb = sbA("recb", [128, 512], F32)
    attb_t = [sbA("attbt%d" % i, [128, 512], BF16) for i in range(2)]
    qmx = sbA("qmx", [1, 1], F32)
    kmx = sbA("kmx", [1, 1], F32)
    tmx = sbA("tmx", [1, 1], F32)
    nB128 = [sbA("nB128_%d" % i, [128, 1], F32) for i in range(2)]
    rec1b = [sbA("rec1b%d" % i, [1, 512], BF16) for i in range(2)]
    v_dr = v_d.rearrange("t p e -> p t e")
    if debug:
        DMA("scalar", dbg["dbg_vtm"], v_d, VDONE + [(d_dbg, d_dbg.v)], d_dbg)

    wqTr = wqT.rearrange("(c p) e -> p c e", p=128)
    wkTr = wkT.rearrange("(c p) e -> p c e", p=128)
    DMA("gpsimd", wq_s[:], wqTr[:, :, 0:128], (), d_wq)
    DMA("gpsimd", wk_s[:], wkTr[:, :, 0:128], (), d_wk)
    DMA("sync", vh_s[0][:], v_dr[:, :, 0:128], list(VDONE), d_vh[0])

    head_state = {}

    def emit_proj(h):
        for (wsem, wbuf_s, wbufn, bias_s, outf, outn) in (
                (d_wq, wq_s, "wq_s", c_bq, qf, "qf"),
                (d_wk, wk_s, "wk_s", c_bk, kf, "kf")):
            WL = 16 * (h + 1)
            for sbi in range(SB):
                par = sbi % 2
                pb_ = buf("pA%d" % par)
                war = pb_.take()
                for dc in range(DC):
                    MM(pA[par][:], wbuf_s[:, dc, :], x_bf[:, dc, sbi * 512:(sbi + 1) * 512],
                       [XL, (wsem, WL)] + (war if dc == 0 else []), dc == 0, dc == DC - 1)
                buf(wbufn).used(st, st.v)
                buf("x_bf").used(st, st.v)
                P.op("scalar", (lambda e, par=par, sbi=sbi, outf=outf, bias_s=bias_s, h=h:
                        e.activation(outf[:, sbi * 512:(sbi + 1) * 512], pA[par][:],
                                     ACT.Identity, bias=bias_s[:, h:h + 1])),
                     deps=[(st, st.v), CONST] + (buf(outn).take() if sbi == 0 else []), inc=sc)
                pb_.used(sc, sc.v)
                emit_absmean(4)
        QKEV = sc.v
        sqv = {}
        for si, (outf, outn) in enumerate(((qf, "qf"), (kf, "kf"))):
            for sbi in range(SB):
                j = si * SB + sbi
                sq_ = sqbs[j]
                sqn = "sqbs%d" % j
                P.op("vector", (lambda e, outf=outf, sbi=sbi, sq_=sq_: e.tensor_mul(
                        sq_[:], outf[:, sbi * 512:(sbi + 1) * 512],
                        outf[:, sbi * 512:(sbi + 1) * 512])),
                     deps=[(sc, QKEV)] + buf(sqn).take(), inc=sv)
                buf(outn).used(sv, sv.v)
                sqv[(si, sbi)] = (sqn, sq_, sv.v)
        head_state[h] = {"QKEV": QKEV, "sqv": sqv}

    def emit_stats(h):
        hp = h % 2
        sqv = head_state[h]["sqv"]
        for si, (mxb, mxn) in enumerate(((qmx, "qmx"), (kmx, "kmx"))):
            for sbi in range(SB):
                sp = (si * SB + sbi) % 2
                sqn, sq_, sqdone = sqv[(si, sbi)]
                pst = pT if sp == 0 else pB
                pstn = "pT" if sp == 0 else "pB"
                MM(pst[0:1, :], c_oneskb[:], sq_[:],
                   [(sv, sqdone), CONST] + buf(pstn).take(), True, True)
                buf(sqn).used(st, st.v)
                P.op("vector", (lambda e, pst=pst: e.reduce_max(tmx[:], pst[0:1, :], AX)),
                     deps=[(st, st.v)] + buf("tmx").take(), inc=sv)
                buf(pstn).used(sv, sv.v)
                if sbi == 0:
                    P.op("vector", (lambda e, mxb=mxb: e.tensor_copy(mxb[:], tmx[:])),
                         deps=[(sv, sv.v)] + buf(mxn).take(), inc=sv)
                else:
                    P.op("vector", (lambda e, mxb=mxb: e.tensor_max(mxb[:], mxb[:], tmx[:])),
                         deps=[(sv, sv.v)], inc=sv)
                buf("tmx").used(sv, sv.v)
        NRM = sv.v
        P.op("vector", lambda e: e.tensor_mul(tm11[0][:], qmx[:], kmx[:]),
             deps=[(sv, NRM)] + buf("tm0").take(), inc=sv)
        buf("qmx").used(sv, sv.v)
        buf("kmx").used(sv, sv.v)
        P.op("scalar", lambda e: e.activation(tm11[1][:], tm11[0][:], ACT.Sqrt,
                                              scale=float(1.02 * 1.02 / 128.0)),
             deps=[(sv, sv.v)] + buf("tm1").take(), inc=sc)
        buf("tm0").used(sc, sc.v)
        P.op("vector", (lambda e, hp=hp: e.tensor_scalar(negb2[hp][:], tm11[1][:],
                                                         -1.0, None, ALU.mult)),
             deps=[(sc, sc.v)] + buf("negb2_%d" % hp).take(), inc=sv)
        buf("tm1").used(sv, sv.v)
        NEGB = sv.v
        MM(pB[:, 0:1], c_ones1[:], negb2[hp][:], [(sv, NEGB), CONST] + buf("pB").take(),
           True, True)
        buf("negb2_%d" % hp).used(st, st.v)
        P.op("vector", (lambda e, hp=hp: e.tensor_copy(nB128[hp][:], pB[:, 0:1])),
             deps=[(st, st.v)] + buf("nB128_%d" % hp).take(), inc=sv)
        buf("pB").used(sv, sv.v)
        head_state[h]["NBB"] = sv.v

    def emit_rope(h):
        hp = h % 2
        QKEV = head_state[h]["QKEV"]
        war = buf("qsw").take()
        DMA("gpsimd", qsw[0:64, :], qf[64:128, :], [(sc, QKEV)] + war, d_swq)
        DMA("gpsimd", qsw[64:128, :], qf[0:64, :], [(sc, QKEV)] + war, d_swq)
        SWQ = d_swq.v
        buf("qf").used(d_swq, SWQ)
        P.op("vector", lambda e: e.tensor_mul(rt1[:], qf[:], c_cc2[:]),
             deps=[(sc, QKEV), TRIG, CONST] + buf("rt1").take(), inc=sv)
        buf("qf").used(sv, sv.v)
        P.op("vector", lambda e: e.tensor_mul(qsw[:], qsw[:], c_ss2[:]),
             deps=[(d_swq, SWQ), TRIG], inc=sv)
        P.op("vector", (lambda e, hp=hp: e.tensor_add(q_bf[hp][:], rt1[:], qsw[:])),
             deps=[(sv, sv.v)] + buf("q_bf%d" % hp).take(), inc=sv)
        buf("rt1").used(sv, sv.v)
        buf("qsw").used(sv, sv.v)
        QROPE = sv.v
        war = buf("qsw").take()
        DMA("gpsimd", qsw[0:64, :], kf[64:128, :], [(sc, QKEV), (sv, QROPE)] + war, d_swk)
        DMA("gpsimd", qsw[64:128, :], kf[0:64, :], [(sc, QKEV), (sv, QROPE)] + war, d_swk)
        SWK = d_swk.v
        buf("kf").used(d_swk, SWK)
        P.op("vector", lambda e: e.tensor_mul(rt1[:], kf[:], c_cc2k[:]),
             deps=[(sc, QKEV)] + buf("rt1").take(), inc=sv)
        buf("kf").used(sv, sv.v)
        P.op("vector", lambda e: e.tensor_mul(qsw[:], qsw[:], c_ss2k[:]),
             deps=[(d_swk, SWK)], inc=sv)
        P.op("vector", (lambda e, hp=hp: e.tensor_add(k_bf[hp][:], rt1[:], qsw[:])),
             deps=[(sv, sv.v)] + buf("k_bf%d" % hp).take(), inc=sv)
        buf("rt1").used(sv, sv.v)
        buf("qsw").used(sv, sv.v)
        head_state[h]["KROPE"] = sv.v
        if debug and h == 0:
            DMA("scalar", dbg["dbg_q0"], q_bf[0][:],
                [(sv, sv.v), (d_dbg, d_dbg.v)], d_dbg)
            DMA("scalar", dbg["dbg_k0"], k_bf[0][:],
                [(sv, sv.v), (d_dbg, d_dbg.v)], d_dbg)
            buf("q_bf0").used(d_dbg, d_dbg.v)
            buf("k_bf0").used(d_dbg, d_dbg.v)

    def emit_attn(h):
        hp = h % 2
        KROPE = head_state[h]["KROPE"]
        NBB = head_state[h]["NBB"]

        pend = None

        def emit_norm(p):
            qb0, PVD0, RCP0 = p
            MM(pB[:], c_ones1b[:], rec1b[qb0 % 2][:],
               [(sv, RCP0), CONST] + buf("pB").take(), True, True)
            buf("rec1b%d" % (qb0 % 2)).used(st, st.v)
            P.op("scalar", lambda e: e.activation(recb[:], pB[:], ACT.Identity),
                 deps=[(st, st.v)] + buf("recb").take(), inc=sc)
            buf("pB").used(sc, sc.v)
            RECB = sc.v
            pa = qb0 % 2
            ab_ = buf("attbt%d" % pa)
            P.op("vector", (lambda e, pa=pa, pv_=pV[qb0 % 2]: e.tensor_mul(
                    attb_t[pa][:], pv_[:], recb[:])),
                 deps=[(sc, RECB), (st, PVD0)] + ab_.take(), inc=sv)
            buf("pV%d" % (qb0 % 2)).used(sv, sv.v)
            buf("recb").used(sv, sv.v)
            DMA("scalar", attb_d[h, :, qb0 * 512:(qb0 + 1) * 512], attb_t[pa][:],
                [(sv, sv.v)], d_atw[pa])
            ab_.used(d_atw[pa], d_atw[pa].v)

        for qb in range(SB):
            kc_max = ((qb + 1) * 512) // 128
            pvp = qb % 2
            pv_war = buf("pV%d" % pvp).take()
            pT_war = buf("pT").take()
            pv_pend = None
            ds_last = [None, None]
            ds_cast = [None, None]
            sc_banks = (pS[0], pS[1], pA[0], pA[1])
            sc_names = ("pS0", "pS1", "pA0", "pA1")
            for kc in range(kc_max + 1):
                if kc < kc_max:
                    par = kc % 4
                    pb_ = buf(sc_names[par])
                    eb_ = buf("expb%d" % par)
                    MM(sc_banks[par][:], k_bf[hp][:, kc * 128:(kc + 1) * 128],
                       q_bf[hp][:, qb * 512:(qb + 1) * 512],
                       [(sv, KROPE)] + pb_.take(), True, True)
                    SCO = st.v
                    buf("k_bf%d" % hp).used(st, SCO)
                    buf("q_bf%d" % hp).used(st, SCO)
                    P.op("scalar", (lambda e, par=par, hp=hp: e.activation(
                            expb[par][:], sc_banks[par][:], ACT.Exp,
                            bias=nB128[hp][:], scale=1.0)),
                         deps=[(st, SCO), (sv, NBB)] + eb_.take(), inc=sc)
                    pb_.used(sc, sc.v)
                    buf("nB128_%d" % hp).used(sc, sc.v)
                    dep_exp = (sc, sc.v)
                    off = kc * 128 - qb * 512
                    if off >= 0:
                        mi = off // 128
                        P.op("vector", (lambda e, par=par, mi=mi: e.tensor_mul(
                                expb[par][:], expb[par][:], c_masks[:, mi, :])),
                             deps=[dep_exp, CONST], inc=sv)
                        dep_exp = (sv, sv.v)
                if pv_pend is not None:
                    kp, dep_p = pv_pend
                    parp = kp % 4
                    cp = kp % 2
                    MM(pV[pvp][:], vh_s[hp][:, kp, :], expb[parp][:],
                       [dep_p, (d_vh[hp], 16 * (h // 2 + 1))] +
                       (pv_war if kp == 0 else []),
                       kp == 0, kp == kc_max - 1)
                    buf("expb%d" % parp).used(st, st.v)
                    buf("vh_s%d" % hp).used(st, st.v)
                    if ds_last[cp] is None:
                        P.op("vector", (lambda e, parp=parp, cp=cp: e.tensor_copy(
                                dsum[cp][:], expb[parp][:])),
                             deps=[dep_p] + buf("dsum%d" % cp).take(), inc=sv)
                    else:
                        P.op("vector", (lambda e, parp=parp, cp=cp: e.tensor_add(
                                dsum[cp][:], dsum[cp][:], expb[parp][:])),
                             deps=[dep_p, (sv, ds_last[cp])], inc=sv)
                    ds_last[cp] = sv.v
                    buf("expb%d" % parp).used(sv, sv.v)
                    if kp == (kc_max - 1 if (kc_max - 1) % 2 == cp else kc_max - 2):
                        P.op("vector", (lambda e, cp=cp: e.tensor_copy(
                                dsumb[cp][:], dsum[cp][:])),
                             deps=[(sv, ds_last[cp])] + buf("dsumb%d" % cp).take(), inc=sv)
                        buf("dsum%d" % cp).used(sv, sv.v)
                        ds_cast[cp] = sv.v
                if kc < kc_max:
                    pv_pend = (kc, dep_exp)
            PVD = st.v
            MM(pT[0:1, :], c_oneskb[:], dsumb[0][:],
               [(sv, ds_cast[0]), CONST] + pT_war, True, False)
            buf("dsumb0").used(st, st.v)
            MM(pT[0:1, :], c_oneskb[:], dsumb[1][:],
               [(sv, ds_cast[1]), CONST], False, True)
            buf("dsumb1").used(st, st.v)
            DEN = st.v
            def _recip(e, rb=rec1b[qb % 2]):
                with nc.allow_low_precision(reason="bf16 softmax scale, within tolerance"):
                    return e.reciprocal(rb[:], pT[0:1, :])
            P.op("vector", _recip,
                 deps=[(st, DEN)] + buf("rec1b%d" % (qb % 2)).take(), inc=sv)
            buf("pT").used(sv, sv.v)
            RCP = sv.v
            if pend is not None:
                emit_norm(pend)
            pend = (qb, PVD, RCP)
        emit_norm(pend)
        if debug and h == 0:
            DMA("scalar", dbg["dbg_att0"], attb_d[0],
                [(d_atw[0], d_atw[0].v), (d_atw[1], d_atw[1].v), (d_dbg, d_dbg.v)], d_dbg)

    def emit_prefetch(hn):
        if hn >= H:
            return
        DMA("gpsimd", wq_s[:], wqTr[:, :, hn * 128:(hn + 1) * 128],
            buf("wq_s").take(), d_wq)
        DMA("gpsimd", wk_s[:], wkTr[:, :, hn * 128:(hn + 1) * 128],
            buf("wk_s").take(), d_wk)
        hpn = hn % 2
        DMA("sync", vh_s[hpn][:], v_dr[:, :, hn * 128:(hn + 1) * 128],
            list(VDONE) + buf("vh_s%d" % hpn).take(), d_vh[hpn])

    for h in range(H):
        emit_proj(h)
        emit_rope(h)
        if h >= 1:
            emit_attn(h - 1)
        emit_stats(h)
        emit_prefetch(h + 1)
    emit_attn(H - 1)

    emit_absmean(n_ab)
    if debug:
        DMA("scalar", dbg["dbg_aa1"], absac[0][:], [(sv, sv.v), (d_dbg, d_dbg.v)], d_dbg)
        DMA("scalar", dbg["dbg_aa2"], absac[1][:], [(sv, sv.v), (d_dbg, d_dbg.v)], d_dbg)
    for w_i, (thi, tlo, tpos, tneg) in enumerate(((TQ_HI, TQ_LO, t_t1, t_t1n),
                                                  (TQ_HI, TQ_LO, t_t2, t_t2n))):
        MM(pT[:, 0:1], c_onesk[:], absac[w_i][:],
           [(sv, sv.v), CONST] + buf("pT").take(), True, True)
        buf("absac%d" % w_i).used(st, st.v)
        P.op("vector", (lambda e, thi=thi: e.tensor_scalar(
                tm11[0][:], pT[0:1, 0:1], thi, None, ALU.mult)),
             deps=[(st, st.v)] + buf("tm0").take(), inc=sv)
        P.op("vector", (lambda e, tlo=tlo: e.scalar_tensor_tensor(
                tm11[0][:], pT[0:1, 0:1], tlo, tm11[0][:], ALU.mult, ALU.add)),
             deps=[(sv, sv.v)], inc=sv)
        buf("pT").used(sv, sv.v)
        MM(pB[:, 0:1], c_ones1[:], tm11[0][:], [(sv, sv.v), CONST] + buf("pB").take(),
           True, True)
        buf("tm0").used(st, st.v)
        P.op("vector", (lambda e, tpos=tpos: e.tensor_copy(tpos[:], pB[:, 0:1])),
             deps=[(st, st.v)], inc=sv)
        buf("pB").used(sv, sv.v)
        P.op("vector", (lambda e, tpos=tpos, tneg=tneg: e.tensor_scalar(
                tneg[:], tpos[:], -1.0, None, ALU.mult)),
             deps=[(sv, sv.v)], inc=sv)
        if debug:
            DMA("scalar", dbg["dbg_t1" if w_i == 0 else "dbg_t2"], tm11[0][:],
                [(sv, sv.v), (d_dbg, d_dbg.v)], d_dbg)
            buf("tm0").used(d_dbg, d_dbg.v)
    THRD = sv.v
    if debug:
        DMA("scalar", dbg["dbg_cc"], c_cc2[:], [TRIG, (d_dbg, d_dbg.v)], d_dbg)
        DMA("scalar", dbg["dbg_ss"], c_ss2[:], [TRIG, (d_dbg, d_dbg.v)], d_dbg)

    P.barrier(extra=[(d_ab[2], d_ab[2].v), (d_ab[3], d_ab[3].v),
                     (d_atw[0], d_atw[0].v), (d_atw[1], d_atw[1].v), (d_x, d_x.v), (d_wv, d_wv.v),
                     (d_wq, d_wq.v), (d_wk, d_wk.v), (d_swq, d_swq.v),
                     (d_swk, d_swk.v), (d_trig, d_trig.v), (d_trigc, d_trigc.v),
                     (d_srow, d_srow.v),
                     (d_cbf, d_cbf.v),
                     (d_vw[0], d_vw[0].v), (d_vw[1], d_vw[1].v),
                     (d_vh[0], d_vh[0].v), (d_vh[1], d_vh[1].v),
                     (d_ab[0], d_ab[0].v), (d_ab[1], d_ab[1].v), (d_dbg, d_dbg.v)])
    esA.close()
    for n in list(B.keys()):
        if n not in ("x1bf",):
            B.pop(n, None)

    esBC = ExitStack()
    x1bf = esBC.enter_context(nc.sbuf_tensor("x1bf", [128, DC, S], BF16))
    esB = ExitStack()
    sbB = lambda n, s_, d: esB.enter_context(nc.sbuf_tensor(n, s_, d))
    psB = lambda n, s_: esB.enter_context(nc.psum_tensor(n, s_, F32))
    wo_s = [sbB("wo_s%d" % i, [128, DC, 128], BF16) for i in range(2)]
    att_s = [sbB("att_s%d" % i, [128, H, 512], BF16) for i in range(2)]
    xf_s = [sbB("xf_s%d" % i, [128, 512], F32) for i in range(2)]
    z_sb = sbB("z_sb", [128, DC, 512], F32)
    zb_b = [sbB("zb_b%d" % i, [128, 512], BF16) for i in range(2)]
    zz_b = [sbB("zz_b%d" % i, [128, 512], BF16) for i in range(2)]
    arow = sbB("arow", [1, 512], F32)
    brow = sbB("brow", [1, 512], F32)
    r0r = sbB("r0r", [1, 512], F32)
    r2r = sbB("r2r", [1, 512], F32)
    aB = sbB("aB", [128, 512], F32)
    bB = sbB("bB", [128, 512], F32)
    x1f = [sbB("x1f%d" % i, [128, 512], F32) for i in range(2)]
    pA2 = [psB("pA2%d" % i, [128, 512]) for i in range(2)]
    pT1 = psB("pT1", [1, 512])
    pT2 = psB("pT2", [1, 512])
    pBa = psB("pBa", [128, 512])
    pBb = psB("pBb", [128, 512])

    woTr = woT.rearrange("(c p) e -> p c e", p=128)
    attb_r = attb_d.rearrange("h p s -> p h s")

    def emit_ln1_tail(sbi, STATS):
        P.op("scalar", lambda e: e.activation(arow[:], pT1[0:1, :], ACT.Identity,
                                              scale=1.0 / D),
             deps=[(st, STATS)] + buf("arow").take(), inc=sc)
        P.op("scalar", lambda e: e.activation(brow[:], pT2[0:1, :], ACT.Identity,
                                              scale=1.0 / D),
             deps=[(st, STATS)] + buf("brow").take(), inc=sc)
        buf("pT1").used(sc, sc.v)
        buf("pT2").used(sc, sc.v)
        MS = sc.v
        P.op("vector", lambda e: e.tensor_mul(r2r[:], arow[:], arow[:]),
             deps=[(sc, MS)] + buf("r2r").take(), inc=sv)
        P.op("vector", lambda e: e.tensor_sub(r0r[:], brow[:], r2r[:]),
             deps=[(sv, sv.v)] + buf("r0r").take(), inc=sv)
        buf("brow").used(sv, sv.v)
        P.op("vector", lambda e: e.tensor_scalar(r0r[:], r0r[:], 1.0, EPS, ALU.mult, ALU.add),
             deps=[(sv, sv.v)], inc=sv)
        VPE = sv.v
        P.op("scalar", lambda e: e.activation(r2r[:], r0r[:], ACT.Sqrt),
             deps=[(sv, VPE)], inc=sc)
        P.op("vector", lambda e: e.reciprocal(r2r[:], r2r[:]),
             deps=[(sc, sc.v)], inc=sv)
        P.op("vector", lambda e: e.tensor_mul(brow[:], r2r[:], r2r[:]),
             deps=[(sv, sv.v)], inc=sv)
        P.op("vector", lambda e: e.tensor_mul(brow[:], brow[:], r0r[:]),
             deps=[(sv, sv.v)], inc=sv)
        buf("r0r").used(sv, sv.v)
        P.op("vector", lambda e: e.tensor_scalar(brow[:], brow[:], -0.5, 1.5,
                                                 ALU.mult, ALU.add),
             deps=[(sv, sv.v)], inc=sv)
        P.op("vector", lambda e: e.tensor_mul(r2r[:], r2r[:], brow[:]),
             deps=[(sv, sv.v)], inc=sv)
        P.op("vector", lambda e: e.scalar_tensor_tensor(brow[:], arow[:], -1.0, r2r[:],
                                                        ALU.mult, ALU.mult),
             deps=[(sv, sv.v)], inc=sv)
        buf("arow").used(sv, sv.v)
        ROWS = sv.v
        MM(pBa[:], c_ones1[:], r2r[:], [(sv, ROWS), CONST] + buf("pBa").take(), True, True)
        buf("r2r").used(st, st.v)
        MM(pBb[:], c_ones1[:], brow[:], [(sv, ROWS), CONST] + buf("pBb").take(), True, True)
        buf("brow").used(st, st.v)
        BC = st.v
        P.op("scalar", lambda e: e.activation(aB[:], pBa[:], ACT.Identity),
             deps=[(st, BC)] + buf("aB").take(), inc=sc)
        buf("pBa").used(sc, sc.v)
        P.op("scalar", lambda e: e.activation(bB[:], pBb[:], ACT.Identity),
             deps=[(st, BC)] + buf("bB").take(), inc=sc)
        buf("pBb").used(sc, sc.v)
        ABB = sc.v
        for ec in range(DC):
            parf = ec % 2
            fb_ = buf("x1f%d" % parf)
            P.op("vector", (lambda e, ec=ec, parf=parf: e.tensor_mul(
                    x1f[parf][:], z_sb[:, ec, :], aB[:])),
                 deps=[(sc, ABB)] + fb_.take(), inc=sv)
            buf("aB").used(sv, sv.v)
            buf("z_sb%d" % ec).used(sv, sv.v)
            P.op("vector", (lambda e, parf=parf: e.tensor_add(
                    x1f[parf][:], x1f[parf][:], bB[:])),
                 deps=[(sv, sv.v)], inc=sv)
            buf("bB").used(sv, sv.v)
            X1D = sv.v
            P.op("scalar", (lambda e, ec=ec, sbi=sbi, parf=parf: e.activation(
                    x1bf[:, ec, sbi * 512:(sbi + 1) * 512], x1f[parf][:], ACT.Identity)),
                 deps=[(sv, X1D)], inc=sc)
            fb_.used(sc, sc.v)
            DMA("scalar", x1_d[ec, :, sbi * 512:(sbi + 1) * 512], x1f[parf][:],
                [(sv, X1D)], d_x1w[parf])
            fb_.used(d_x1w[parf], d_x1w[parf].v)

    pend_tail = [None]
    for sbi in range(SB):
        atp = sbi % 2
        if sbi == 0:
            DMA("sync", att_s[0][:], attb_r[:, :, 0:512],
                [(d_atw[0], d_atw[0].v), (d_atw[1], d_atw[1].v)] + buf("att_s0").take(),
                d_ats[0])
        ATTL = 16 * (sbi // 2 + 1)
        pend_stat = [None]
        for ec in range(DC):
            parw = ec % 2
            DMA("gpsimd", wo_s[parw][:], woTr[:, :, ec * 128:(ec + 1) * 128],
                buf("wo_s%d" % parw).take(), d_wo[parw])
            WOL = d_wo[parw].v
            DMA("sync", xf_s[parw][:], xTr[:, ec, sbi * 512:(sbi + 1) * 512],
                buf("xf_s%d" % parw).take(), d_xr[parw])
            XRL = d_xr[parw].v
            pb_ = buf("pA2%d" % parw)
            war = pb_.take()
            for hc in range(H):
                MM(pA2[parw][:], wo_s[parw][:, hc, :], att_s[atp][:, hc, :],
                   [(d_wo[parw], WOL), (d_ats[atp], ATTL)] + (war if hc == 0 else []),
                   hc == 0, hc == H - 1)
            buf("wo_s%d" % parw).used(st, st.v)
            buf("att_s%d" % atp).used(st, st.v)
            if ec == 0 and pend_tail[0] is not None:
                pend_tail[0]()
                pend_tail[0] = None
            if pend_stat[0] is not None:
                pend_stat[0]()
                pend_stat[0] = None
            if ec == min(2, DC - 1) and sbi + 1 < SB:
                DMA("sync", att_s[(sbi + 1) % 2][:],
                    attb_r[:, :, (sbi + 1) * 512:(sbi + 2) * 512],
                    [(d_atw[0], d_atw[0].v), (d_atw[1], d_atw[1].v)] +
                    buf("att_s%d" % ((sbi + 1) % 2)).take(), d_ats[(sbi + 1) % 2])
            P.op("scalar", (lambda e, parw=parw, ec=ec: e.activation(
                    z_sb[:, ec, :], pA2[parw][:], ACT.Identity, bias=c_bo[:, ec:ec + 1])),
                 deps=[(st, st.v), CONST] + buf("z_sb%d" % ec).take(), inc=sc)
            pb_.used(sc, sc.v)
            P.op("vector", (lambda e, parw=parw, ec=ec: e.tensor_add(
                    z_sb[:, ec, :], z_sb[:, ec, :], xf_s[parw][:])),
                 deps=[(sc, sc.v), (d_xr[parw], XRL)], inc=sv)
            ZD = sv.v
            buf("xf_s%d" % parw).used(sv, ZD)
            P.op("scalar", (lambda e, parw=parw, ec=ec: e.activation(
                    zb_b[parw][:], z_sb[:, ec, :], ACT.Identity)),
                 deps=[(sv, ZD)] + buf("zb_b%d" % parw).take(), inc=sc)
            ZB = sc.v
            buf("z_sb%d" % ec).used(sc, ZB)
            P.op("vector", (lambda e, parw=parw, ec=ec: e.tensor_mul(
                    zz_b[parw][:], z_sb[:, ec, :], z_sb[:, ec, :])),
                 deps=[(sv, ZD)] + buf("zz_b%d" % parw).take(), inc=sv)
            ZZ = sv.v
            buf("z_sb%d" % ec).used(sv, ZZ)

            def _stat(ec=ec, parw=parw, ZB=ZB, ZZ=ZZ):
                MM(pT1[:], c_oneskb[:], zb_b[parw][:],
                   [(sc, ZB), CONST] + (buf("pT1").take() if ec == 0 else []),
                   ec == 0, ec == DC - 1)
                buf("zb_b%d" % parw).used(st, st.v)
                MM(pT2[:], c_oneskb[:], zz_b[parw][:],
                   [(sv, ZZ), CONST] + (buf("pT2").take() if ec == 0 else []),
                   ec == 0, ec == DC - 1)
                buf("zz_b%d" % parw).used(st, st.v)
            pend_stat[0] = _stat
        pend_stat[0]()
        pend_stat[0] = None
        STATS = st.v
        pend_tail[0] = (lambda sbi=sbi, STATS=STATS: emit_ln1_tail(sbi, STATS))
    pend_tail[0]()
    pend_tail[0] = None
    if debug:
        DMA("scalar", dbg["dbg_x1"], x1_d, [(d_x1w[0], d_x1w[0].v), (d_x1w[1], d_x1w[1].v), (d_dbg, d_dbg.v)], d_dbg)

    P.barrier(extra=[(d_x1w[0], d_x1w[0].v), (d_x1w[1], d_x1w[1].v),
                     (d_ats[0], d_ats[0].v), (d_ats[1], d_ats[1].v),
                     (d_wo[0], d_wo[0].v), (d_wo[1], d_wo[1].v),
                     (d_xr[0], d_xr[0].v), (d_xr[1], d_xr[1].v), (d_dbg, d_dbg.v)])
    esB.close()
    for n in list(B.keys()):
        if n not in ("x1bf",):
            B.pop(n, None)

    esC = ExitStack()
    sbC = lambda n, s_, d: esC.enter_context(nc.sbuf_tensor(n, s_, d))
    psC = lambda n, s_: esC.enter_context(nc.psum_tensor(n, s_, F32))
    w1b = [sbC("w1b%d" % i, [128, DC, 128], F32) for i in range(2)]
    a1b = [sbC("a1b%d" % i, [128, DC * 128], F32) for i in range(2)]
    w1q = [sbC("w1q%d" % i, [128, DC, 128], BF16) for i in range(2)]
    hb = [sbC("hb%d" % i, [128, S], BF16) for i in range(2)]
    w2b = [sbC("w2b%d" % i, [128, D], F32) for i in range(2)]
    a2b = [sbC("a2b%d" % i, [128, D], F32) for i in range(2)]
    w2qs = [sbC("w2qs%d" % i, [128, D], BF16) for i in range(2)]
    NPH = 2 * SB
    pH = [psC("pH%d" % i, [128, 512]) for i in range(NPH)]

    for fc in range(FC):
        par = fc % 2
        DMA("sync", w1b[par][:], w1Tr[:, :, fc * 128:(fc + 1) * 128],
            buf("w1b%d" % par).take(), d_w1[par])
        W1L = d_w1[par].v
        P.op("vector", (lambda e, par=par: e.tensor_scalar(
                a1b[par][:], w1b[par][:].rearrange("p c f -> p (c f)"), t_t1n[:],
                None, ALU.is_le)),
             deps=[(d_w1[par], W1L), (sv, THRD)] + buf("a1b%d" % par).take(), inc=sv)
        P.op("vector", (lambda e, par=par: e.scalar_tensor_tensor(
                w1q[par][:].rearrange("p c f -> p (c f)"),
                w1b[par][:].rearrange("p c f -> p (c f)"), t_t1[:],
                a1b[par][:], ALU.is_ge, ALU.subtract)),
             deps=[(sv, sv.v)] + buf("w1q%d" % par).take(), inc=sv)
        buf("w1b%d" % par).used(sv, sv.v)
        buf("a1b%d" % par).used(sv, sv.v)
        QD = sv.v
        for dc in range(DC):
            for sbi in range(SB):
                bank = par * SB + sbi
                MM(pH[bank][:], w1q[par][:, dc, :], x1bf[:, dc, sbi * 512:(sbi + 1) * 512],
                   [(sv, QD)] + (buf("pH%d" % bank).take() if dc == 0 else []),
                   dc == 0, dc == DC - 1)
        buf("w1q%d" % par).used(st, st.v)
        buf("x1bf").used(st, st.v)
        HMM = st.v
        for sbi in range(SB):
            bank = par * SB + sbi
            P.op("scalar", (lambda e, bank=bank, sbi=sbi, par=par, fc=fc: e.activation(
                    hb[par][:, sbi * 512:(sbi + 1) * 512], pH[bank][:], ACT.Gelu,
                    bias=c_b1[:, fc:fc + 1])),
                 deps=[(st, HMM), CONST] + (buf("hb%d" % par).take() if sbi == 0 else []),
                 inc=sc)
            buf("pH%d" % bank).used(sc, sc.v)
        DMA("scalar", h_d[fc], hb[par][:], [(sc, sc.v)], d_hw[par])
        buf("hb%d" % par).used(d_hw[par], d_hw[par].v)
        DMA("sync", w2b[par][:], w2Tr[:, fc, :], buf("w2b%d" % par).take(), d_w2[par])
        W2L = d_w2[par].v
        P.op("vector", (lambda e, par=par: e.tensor_scalar(
                a2b[par][:], w2b[par][:], t_t2n[:], None, ALU.is_le)),
             deps=[(d_w2[par], W2L), (sv, THRD)] + buf("a2b%d" % par).take(), inc=sv)
        P.op("vector", (lambda e, par=par: e.scalar_tensor_tensor(
                w2qs[par][:], w2b[par][:], t_t2[:], a2b[par][:],
                ALU.is_ge, ALU.subtract)),
             deps=[(sv, sv.v)] + buf("w2qs%d" % par).take(), inc=sv)
        buf("w2b%d" % par).used(sv, sv.v)
        buf("a2b%d" % par).used(sv, sv.v)
        DMA("scalar", w2q_d[fc], w2qs[par][:], [(sv, sv.v)], d_w2qw[par])
        buf("w2qs%d" % par).used(d_w2qw[par], d_w2qw[par].v)

    P.barrier(extra=[(d_hw[0], d_hw[0].v), (d_hw[1], d_hw[1].v), (d_w2qw[0], d_w2qw[0].v), (d_w2qw[1], d_w2qw[1].v),
                     (d_w1[0], d_w1[0].v), (d_w1[1], d_w1[1].v),
                     (d_w2[0], d_w2[0].v), (d_w2[1], d_w2[1].v), (d_dbg, d_dbg.v)])
    esC.close()
    esBC.close()
    B.clear()

    esD = ExitStack()
    sbD = lambda n, s_, d: esD.enter_context(nc.sbuf_tensor(n, s_, d))
    psD = lambda n, s_: esD.enter_context(nc.psum_tensor(n, s_, F32))
    EW2 = ECW * 128
    NEH = DC // ECW
    w2r = [sbD("w2r%d" % i, [128, FC, EW2], BF16) for i in range(2)]
    hs = [sbD("hs%d" % i, [128, 512], BF16) for i in range(4)]
    x1r = [sbD("x1r%d" % i, [128, 512], F32) for i in range(ECW)]
    z2t = [sbD("z2t%d" % i, [128, 512], F32) for i in range(2)]
    z2b = [sbD("z2b%d" % i, [128, 512], BF16) for i in range(ECW)]
    zzd = [sbD("zzd%d" % i, [128, 512], BF16) for i in range(ECW)]
    ssum = sbD("ssum", [1, S], F32)
    ssq = sbD("ssq", [1, S], F32)
    ar2 = sbD("ar2", [1, 512], F32)
    br2 = sbD("br2", [1, 512], F32)
    r02 = sbD("r02", [1, 512], F32)
    r22 = sbD("r22", [1, 512], F32)
    aB2 = sbD("aB2", [128, 512], F32)
    bB2 = sbD("bB2", [128, 512], F32)
    z2rd = [sbD("z2rd%d" % i, [128, 512], F32) for i in range(2)]
    outt = [sbD("outt%d" % i, [128, 512], F32) for i in range(2)]
    pY = [psD("pY%d" % i, [128, 512]) for i in range(ECW)]
    pSt_h = [psD("pSt0", [1, 512]), psD("pSt1", [1, 512])]
    pBc_h = [psD("pBc0", [128, 512]), psD("pBc1", [128, 512])]
    pSt = [pSt_h[0][:], pSt_h[1][:]]
    pBc = [pBc_h[0][:], pBc_h[1][:]]
    pSt_names = ["pSt0", "pSt1"]
    pBc_names = ["pBc0", "pBc1"]

    P.op("vector", lambda e: e.memset(ssum[:], 0.0), deps=(), inc=sv)
    P.op("vector", lambda e: e.memset(ssq[:], 0.0), deps=(), inc=sv)

    w2qr = w2q_d.rearrange("f p e -> p f e")
    x1_dr = x1_d
    def _ln2_block(ssl, sbi, SACC):
        P.op("vector", (lambda e, ssl=ssl: e.tensor_scalar(
                ar2[:], ssum[:, ssl], 1.0 / D, None, ALU.mult)),
             deps=[(sv, SACC)] + buf("ar2").take(), inc=sv)
        P.op("vector", (lambda e, ssl=ssl: e.tensor_scalar(
                br2[:], ssq[:, ssl], 1.0 / D, EPS, ALU.mult, ALU.add)),
             deps=[(sv, SACC)] + buf("br2").take(), inc=sv)
        P.op("vector", lambda e: e.tensor_mul(r22[:], ar2[:], ar2[:]),
             deps=[(sv, sv.v)] + buf("r22").take(), inc=sv)
        P.op("vector", lambda e: e.tensor_sub(r02[:], br2[:], r22[:]),
             deps=[(sv, sv.v)] + buf("r02").take(), inc=sv)
        buf("br2").used(sv, sv.v)
        VPE2 = sv.v
        P.op("scalar", lambda e: e.activation(r22[:], r02[:], ACT.Sqrt),
             deps=[(sv, VPE2)], inc=sc)
        P.op("vector", lambda e: e.reciprocal(r22[:], r22[:]),
             deps=[(sc, sc.v)], inc=sv)
        P.op("vector", lambda e: e.tensor_mul(br2[:], r22[:], r22[:]),
             deps=[(sv, sv.v)], inc=sv)
        P.op("vector", lambda e: e.tensor_mul(br2[:], br2[:], r02[:]),
             deps=[(sv, sv.v)], inc=sv)
        buf("r02").used(sv, sv.v)
        P.op("vector", lambda e: e.tensor_scalar(br2[:], br2[:], -0.5, 1.5,
                                                 ALU.mult, ALU.add),
             deps=[(sv, sv.v)], inc=sv)
        P.op("vector", lambda e: e.tensor_mul(r22[:], r22[:], br2[:]),
             deps=[(sv, sv.v)], inc=sv)
        P.op("vector", lambda e: e.scalar_tensor_tensor(br2[:], ar2[:], -1.0, r22[:],
                                                        ALU.mult, ALU.mult),
             deps=[(sv, sv.v)], inc=sv)
        buf("ar2").used(sv, sv.v)
        ROWS2 = sv.v
        MM(pBc[0], c_ones1[:], r22[:],
           [(sv, ROWS2), CONST] + buf(pBc_names[0]).take(), True, True)
        buf("r22").used(st, st.v)
        MM(pBc[1], c_ones1[:], br2[:],
           [(sv, ROWS2), CONST] + buf(pBc_names[1]).take(), True, True)
        buf("br2").used(st, st.v)
        BC2 = st.v
        P.op("scalar", lambda e: e.activation(aB2[:], pBc[0], ACT.Identity),
             deps=[(st, BC2)] + buf("aB2").take(), inc=sc)
        buf(pBc_names[0]).used(sc, sc.v)
        P.op("scalar", lambda e: e.activation(bB2[:], pBc[1], ACT.Identity),
             deps=[(st, BC2)] + buf("bB2").take(), inc=sc)
        buf(pBc_names[1]).used(sc, sc.v)
        ABB2 = sc.v
        for ec in range(DC):
            parr = ec % 2
            DMA("sync", z2rd[parr][:], z2_d[ec, :, sbi * 512:(sbi + 1) * 512],
                [(d_z2w[0], d_z2w[0].v), (d_z2w[1], d_z2w[1].v)] + buf("z2rd%d" % parr).take(), d_z2r[parr])
            Z2RL = d_z2r[parr].v
            P.op("vector", (lambda e, parr=parr: e.tensor_mul(
                    outt[parr][:], z2rd[parr][:], aB2[:])),
                 deps=[(sc, ABB2), (d_z2r[parr], Z2RL)] +
                      buf("outt%d" % parr).take(), inc=sv)
            buf("z2rd%d" % parr).used(sv, sv.v)
            buf("aB2").used(sv, sv.v)
            P.op("vector", (lambda e, parr=parr: e.tensor_add(
                    outt[parr][:], outt[parr][:], bB2[:])),
                 deps=[(sv, sv.v)], inc=sv)
            buf("bB2").used(sv, sv.v)
            DMA("scalar",
                outT.rearrange("(c p) s -> p c s", p=128)[:, ec, sbi * 512:(sbi + 1) * 512],
                outt[parr][:], [(sv, sv.v)], d_out[parr])
            buf("outt%d" % parr).used(d_out[parr], d_out[parr].v)
    defer_ln2 = [None]
    pend_dstat = [None]
    for eh in range(NEH):
        ep = eh % 2
        if eh == 0:
            for ee in range(min(2, NEH)):
                DMA("sync", w2r[ee % 2][:], w2qr[:, :, ee * EW2:(ee + 1) * EW2],
                    [(d_w2qw[0], d_w2qw[0].v), (d_w2qw[1], d_w2qw[1].v)] +
                    ([(d_w2r, 16)] if ee == 1 else []) +
                    buf("w2r%d" % (ee % 2)).take(), d_w2r)
        elif eh + 1 < NEH:
            DMA("sync", w2r[(eh + 1) % 2][:], w2qr[:, :, (eh + 1) * EW2:(eh + 2) * EW2],
                [(d_w2r, 16 * (eh + 1))] + buf("w2r%d" % ((eh + 1) % 2)).take(), d_w2r)
        W2RL = 16 * (eh + 1)
        for sbi in range(SB):
            for fc in range(FC):
                hi_ = fc % 4
                DMA("sync", hs[hi_][:], h_d[fc, :, sbi * 512:(sbi + 1) * 512],
                    [(d_hw[0], d_hw[0].v), (d_hw[1], d_hw[1].v)] + buf("hs%d" % hi_).take(), d_hs[hi_])
                HL = d_hs[hi_].v
                for j in range(ECW):
                    MM(pY[j][:], w2r[ep][:, fc, j * 128:(j + 1) * 128], hs[hi_][:],
                       [(d_hs[hi_], HL), (d_w2r, W2RL)] +
                       (buf("pY%d" % j).take() if fc == 0 else []),
                       fc == 0, fc == FC - 1)
                buf("hs%d" % hi_).used(st, st.v)
                if fc == 4 and pend_dstat[0] is not None:
                    pend_dstat[0]()
                    pend_dstat[0] = None
            SWEEP = st.v
            buf("w2r%d" % ep).used(st, SWEEP)
            if pend_dstat[0] is not None:
                pend_dstat[0]()
                pend_dstat[0] = None
            if defer_ln2[0] is not None:
                defer_ln2[0]()
                defer_ln2[0] = None
            stat_deps = []
            x1rl = []
            for j in range(ECW):
                ec = eh * ECW + j
                DMA("sync", x1r[j][:], x1_dr[ec, :, sbi * 512:(sbi + 1) * 512],
                    [(d_x1w[0], d_x1w[0].v), (d_x1w[1], d_x1w[1].v)] + buf("x1r%d" % j).take(), d_x1r[j])
                x1rl.append(d_x1r[j].v)
            for j in range(ECW):
                ec = eh * ECW + j
                parz = j % 2
                X1RL = x1rl[j]
                P.op("scalar", (lambda e, j=j, ec=ec, parz=parz: e.activation(
                        z2t[parz][:], pY[j][:], ACT.Identity, bias=c_b2[:, ec:ec + 1])),
                     deps=[(st, SWEEP), CONST] + buf("z2t%d" % parz).take(), inc=sc)
                buf("pY%d" % j).used(sc, sc.v)
                P.op("vector", (lambda e, parz=parz, j=j: e.tensor_add(
                        z2t[parz][:], z2t[parz][:], x1r[j][:])),
                     deps=[(sc, sc.v), (d_x1r[j], X1RL)], inc=sv)
                Z2D = sv.v
                buf("x1r%d" % j).used(sv, Z2D)
                P.op("scalar", (lambda e, j=j, parz=parz: e.activation(
                        z2b[j][:], z2t[parz][:], ACT.Identity)),
                     deps=[(sv, Z2D)] + buf("z2b%d" % j).take(), inc=sc)
                ZB2 = sc.v
                buf("z2t%d" % parz).used(sc, ZB2)
                P.op("vector", (lambda e, j=j, parz=parz: e.tensor_mul(
                        zzd[j][:], z2t[parz][:], z2t[parz][:])),
                     deps=[(sv, Z2D)] + buf("zzd%d" % j).take(), inc=sv)
                ZZ2 = sv.v
                buf("z2t%d" % parz).used(sv, ZZ2)
                DMA("scalar", z2_d[ec, :, sbi * 512:(sbi + 1) * 512], z2t[parz][:],
                    [(sv, Z2D)], d_z2w[parz])
                buf("z2t%d" % parz).used(d_z2w[parz], d_z2w[parz].v)
                stat_deps.append((j, ZB2, ZZ2))

            def _dstat(stat_deps=stat_deps, sbi=sbi, eh=eh):
                for (j, ZB2, ZZ2) in stat_deps:
                    MM(pSt[0], c_oneskb[:], z2b[j][:],
                       [(sc, ZB2), CONST] + (buf(pSt_names[0]).take() if j == 0 else []),
                       j == 0, j == ECW - 1)
                    buf("z2b%d" % j).used(st, st.v)
                    MM(pSt[1], c_oneskb[:], zzd[j][:],
                       [(sv, ZZ2), CONST] + (buf(pSt_names[1]).take() if j == 0 else []),
                       j == 0, j == ECW - 1)
                    buf("zzd%d" % j).used(st, st.v)
                STT = st.v
                ssl = slice(sbi * 512, (sbi + 1) * 512)
                P.op("vector", (lambda e, ssl=ssl: e.tensor_add(ssum[:, ssl], ssum[:, ssl],
                                                                pSt[0])),
                     deps=[(st, STT)], inc=sv)
                buf(pSt_names[0]).used(sv, sv.v)
                P.op("vector", (lambda e, ssl=ssl: e.tensor_add(ssq[:, ssl], ssq[:, ssl],
                                                                pSt[1])),
                     deps=[(st, STT)], inc=sv)
                buf(pSt_names[1]).used(sv, sv.v)
                SACC = sv.v
                if eh == NEH - 1:
                    defer_ln2[0] = (lambda: _ln2_block(ssl, sbi, SACC))
            pend_dstat[0] = _dstat

    if pend_dstat[0] is not None:
        pend_dstat[0]()
        pend_dstat[0] = None


    if defer_ln2[0] is not None:
        defer_ln2[0]()
        defer_ln2[0] = None

    P.wait("sync", [(d_out[0], d_out[0].v), (d_out[1], d_out[1].v),
                    (d_z2w[0], d_z2w[0].v), (d_z2w[1], d_z2w[1].v), (d_z2r[0], d_z2r[0].v),
                    (d_z2r[1], d_z2r[1].v), (d_w2r, d_w2r.v), (d_dbg, d_dbg.v)]
            + [(d_x1r[i], d_x1r[i].v) for i in range(4)]
            + [(d_hs[i], d_hs[i].v) for i in range(4)])

    with nc.Block() as block:
        block.sync(lambda e: [f(e) for f in P.q["sync"]])
        block.gpsimd(lambda e: [f(e) for f in P.q["gpsimd"]])
        block.vector(lambda e: [f(e) for f in P.q["vector"]])
        block.scalar(lambda e: [f(e) for f in P.q["scalar"]])
        block.tensor(lambda e: [f(e) for f in P.q["tensor"]])

    esD.close()
    return nc



def make_inputs_for_core(cfg, xb, wq, bq_, wk, bk_, wv, bv_, wo, bo_, w1, b1_, w2, b2_):
    D, S = cfg.D, cfg.S
    DC, FC = cfg.DC, cfg.FC
    perm = np.concatenate([h * 128 + np.concatenate([np.arange(0, 128, 2),
                                                     np.arange(1, 128, 2)])
                           for h in range(cfg.H)])
    m = {}
    m["xT"] = np.ascontiguousarray(xb.T)
    m["wqT"] = np.ascontiguousarray(wq.T[:, perm])
    m["wkT"] = np.ascontiguousarray(wk.T[:, perm])
    m["wvT"] = np.ascontiguousarray(wv.T)
    m["woT"] = np.ascontiguousarray(wo.T)
    m["bq"] = np.ascontiguousarray(bq_[perm].reshape(DC, 128).T)
    m["bk"] = np.ascontiguousarray(bk_[perm].reshape(DC, 128).T)
    m["bvr"] = np.ascontiguousarray(bv_.reshape(1, D))
    m["bo"] = np.ascontiguousarray(bo_.reshape(DC, 128).T)
    m["w1T"] = np.ascontiguousarray(w1.T)
    m["b1"] = np.ascontiguousarray(b1_.reshape(FC, 128).T)
    m["w2T"] = np.ascontiguousarray(w2.T)
    m["b2"] = np.ascontiguousarray(b2_.reshape(DC, 128).T)
    m["srow"] = np.arange(S, dtype=np.float32).reshape(1, S)
    m["icol"] = np.arange(64, dtype=np.float32).reshape(64, 1)
    m["ones1"] = np.ones((1, 128), np.float32)
    m["onesk"] = np.ones((128, 1), np.float32)
    m["oneskb"] = np.ones((128, 1), ml_dtypes.bfloat16)
    m["ones1b"] = np.ones((1, 128), ml_dtypes.bfloat16)
    m["ones1sc"] = np.full((1, 128), 2.0 ** 20, ml_dtypes.bfloat16)
    i_ = np.arange(128)[:, None]
    j_ = np.arange(512)[None, :]
    masks = np.stack([(i_ + mi * 128 <= j_) for mi in range(4)], axis=1)
    m["masks"] = masks.astype(ml_dtypes.bfloat16)
    return m


def kernel(L0_attn_x, L0_attn_Wq, L0_attn_bq, L0_attn_Wk, L0_attn_bk,
           L0_attn_Wv, L0_attn_bv, L0_attn_Wo, L0_attn_bo,
           L0_ffn_w1, L0_ffn_b1, L0_ffn_w2, L0_ffn_b2):
    cfg = Cfg(D=2048, S=2048, DFF=8192)
    nc = build(cfg)
    args = [np.asarray(a) for a in
            (L0_attn_Wq, L0_attn_bq, L0_attn_Wk, L0_attn_bk, L0_attn_Wv,
             L0_attn_bv, L0_attn_Wo, L0_attn_bo, L0_ffn_w1, L0_ffn_b1,
             L0_ffn_w2, L0_ffn_b2)]
    x = np.asarray(L0_attn_x)
    in_maps = [make_inputs_for_core(cfg, x[b], *args) for b in range(NCORES)]
    res = run_bass_kernel_spmd(nc, in_maps, core_ids=list(range(NCORES)))
    out = np.empty((NCORES, cfg.S, cfg.D), np.float32)
    for b in range(NCORES):
        out[b] = res.results[b]["outT"].T
    return out
